# version 19
# baseline (speedup 1.0000x reference)
import numpy as np
import ml_dtypes
from contextlib import ExitStack

import concourse.bass as bass
import concourse.mybir as mybir
from concourse.bass_utils import run_bass_kernel_spmd

F32 = mybir.dt.float32
BF16 = mybir.dt.bfloat16
FP8 = mybir.dt.float8e4
I32 = mybir.dt.int32
AF = mybir.ActivationFunctionType
ALU = mybir.AluOpType
AX = mybir.AxisListType.X

T_TOT = 16384
D = 1024
DC = 8
NB = 16
BLK = 1024
BT = 8
NCORES = 8
NVIS = 17
HID = 4096
HC = 32
NH = 16
HD = 64
EPS_LN = 1e-5
EPS_LA = 1e-6
INV127 = 1.0 / 127.0
INVD = 1.0 / D


class Sched:
    ENGS = ("T", "V", "S", "G", "Y")

    def __init__(self):
        self.ops = {e: [] for e in self.ENGS}
        self.last_writer = {}
        self.readers = {}
        self.accum_open = {}
        self.clock = {e: {} for e in self.ENGS}
        self.last_dma = {e: -1 for e in self.ENGS}
        self.fence_fns = {}
        self.fence_writes = {}

    def emit(self, eng, fn, reads=(), writes=(), accum=False, is_dma=False,
             is_coll=False, tag=""):
        idx = len(self.ops[eng])
        need = {}

        def add_edge(peng, pidx):
            if peng == eng and pidx == idx:
                return
            pop = self.ops[peng][pidx]
            async_done = pop["is_dma"] or pop["is_coll"]
            if pop["is_dma"] and not pop["is_coll"]:
                pidx = self.last_dma[peng]
                pop = self.ops[peng][pidx]
            if not async_done:
                key = (peng, "c")
            else:
                key = (peng, "Gl" if pop["is_coll"] else "d")
            if self.clock[eng].get(key, -1) >= pidx:
                return
            if need.get(key, -1) < pidx:
                need[key] = pidx

        for k in reads:
            w = self.last_writer.get(k)
            if w is not None:
                add_edge(*w)
        for k in writes:
            w = self.last_writer.get(k)
            if w is not None:
                if not (accum and w[0] == eng and self.accum_open.get(k)):
                    add_edge(*w)
            for r in self.readers.get(k, ()):
                add_edge(*r)
        for (peng, kind), pidx in need.items():
            self.ops[peng][pidx]["inc"] = True
            pc = self.ops[peng][pidx]["clock"]
            for k2, v2 in pc.items():
                if self.clock[eng].get(k2, -1) < v2:
                    self.clock[eng][k2] = v2
            if self.clock[eng].get((peng, kind), -1) < pidx:
                self.clock[eng][(peng, kind)] = pidx
        myclock = dict(self.clock[eng])
        myclock[(eng, "c")] = idx
        self.ops[eng].append(dict(fn=fn, waits=dict(need),
                                  inc=bool(is_dma),
                                  is_dma=is_dma, is_coll=is_coll,
                                  clock=myclock, tag=tag))
        if is_dma and not is_coll:
            self.last_dma[eng] = idx
        for k in reads:
            self.readers.setdefault(k, []).append((eng, idx))
        for k in writes:
            self.readers[k] = []
            self.last_writer[k] = (eng, idx)
            self.accum_open[k] = accum

    def barrier(self):
        key = ("__bar__", sum(len(self.ops[e]) for e in self.ENGS))
        for e in self.ENGS:
            fn = self.fence_fns.get(e, lambda E: E.nop())
            self.emit(e, fn, reads=(("fence", "init"),),
                      writes=((key, e), ("fence", e)) + self.fence_writes.get(e, ()))
            if self.last_dma[e] >= 0:
                op = self.ops[e][-1]
                if op["waits"].get((e, "d"), -1) < self.last_dma[e]:
                    op["waits"][(e, "d")] = self.last_dma[e]
                    self.ops[e][self.last_dma[e]]["inc"] = True
        allk = tuple((key, e2) for e2 in self.ENGS)
        for e in self.ENGS:
            self.emit(e, lambda E: E.nop(), reads=allk, writes=())

    @staticmethod
    def _sem_kind(e, op):
        if op["is_coll"]:
            return "Gl"
        if op["is_dma"]:
            return e + "d"
        return e + "c"

    def replay(self, block, sems):
        for e in self.ENGS:
            run = {}
            for op in self.ops[e]:
                if op["inc"]:
                    kind = self._sem_kind(e, op)
                    run[kind] = run.get(kind, 0) + \
                        (1 if op["is_coll"] else (16 if op["is_dma"] else 1))
                op["cum"] = dict(run)

        def section(eng_obj, e):
            for op in self.ops[e]:
                for (peng, _k), pidx in sorted(op["waits"].items()):
                    pop = self.ops[peng][pidx]
                    kind = self._sem_kind(peng, pop)
                    eng_obj.wait_ge(sems[kind], pop["cum"][kind])
                if op["inc"]:
                    kind = self._sem_kind(e, op)
                    step = 1 if op["is_coll"] else (16 if op["is_dma"] else 1)
                    prev = op["cum"][kind] - step
                    if prev > 0:
                        eng_obj.wait_ge(sems[kind], prev)
                inst = op["fn"](eng_obj)
                if op["inc"]:
                    kind = self._sem_kind(e, op)
                    inst.then_inc(
                        sems[kind],
                        1 if op["is_coll"] else (16 if op["is_dma"] else 1))

        @block.tensor
        def _(tensor):
            section(tensor, "T")

        @block.vector
        def _(vector):
            section(vector, "V")

        @block.scalar
        def _(scalar):
            section(scalar, "S")

        @block.gpsimd
        def _(gpsimd):
            section(gpsimd, "G")

        @block.sync
        def _(sync):
            section(sync, "Y")


def build_program(debug=0):
    nc = bass.Bass("TRN2", target_bir_lowering=False, debug=False)
    sch = Sched()

    def din(name, shape, dt=F32):
        return nc.dram_tensor(name, shape, dt, kind="ExternalInput").ap()

    kvis = din("kvis", [NVIS, BT, 128, D])
    myblk = din("myblk", [2, BT, 128, D])
    wsel = din("wsel", [NVIS * 2])
    halosel = din("halosel", [4 * NCORES, 4])
    stw = din("stw", [NB * 2])
    w1_ = din("w1", [HID, D])
    w2_ = din("w2", [D, HID])
    convw_ = din("conv_w", [HID, 3])
    convb_ = din("conv_b", [HID])
    wq_ = din("wq", [D, D])
    wk_ = din("wk", [D, D])
    wv_ = din("wv", [D, D])
    wo_ = din("wo", [D, D])
    bq_ = din("bq", [D])
    bk_ = din("bk", [D])
    bvb_ = din("bvb", [D])
    bo_ = din("bo", [D])
    bvq_ = din("bv_q", [D])
    bvk_ = din("bv_k", [D])
    bvv_ = din("bv_v", [D])

    out_y = nc.dram_tensor("y", [2 * BLK, D], F32, kind="ExternalOutput").ap()
    dbg = None
    dbg2 = None
    if debug:
        dbg = nc.dram_tensor("dbg", [min(debug, 3), 2 * BLK, D], F32,
                             kind="ExternalOutput").ap()
        if debug >= 4:
            dbg2 = nc.dram_tensor("dbg2", [16, 128, 8192], F32,
                                  kind="ExternalOutput").ap()

    w1q_s = nc.dram_tensor("w1q_s", [HC, 2, 128, 512], BF16).ap()
    w2q_s = nc.dram_tensor("w2q_s", [HID, D], BF16).ap()
    spill = nc.dram_tensor("spill", [2, 2, BT, 128, D], BF16).ap()
    ag_ei = [nc.dram_tensor(f"ag_ei{i}", [4, D], F32).ap() for i in range(2)]
    ag_eo = [nc.dram_tensor(f"ag_eo{i}", [4 * NCORES, D], F32,
                            addr_space="Shared").ap() for i in range(2)]
    ag_si = nc.dram_tensor("ag_si", [2, 128, 8 * 65], F32).ap()
    ag_so = nc.dram_tensor("ag_so", [NB, 128, 8 * 65], F32,
                           addr_space="Shared").ap()

    es = ExitStack()

    def sbt(name, shape, dt=F32):
        return es.enter_context(nc.sbuf_tensor(name, shape, dt))

    resid = sbt("resid", [128, 2 * BT, D])
    st_s = sbt("st_s", [128, NB])
    st_q = sbt("st_q", [128, NB])
    st_m = sbt("st_m", [128, NB])
    st_v = sbt("st_v", [128, NB])
    st_r = sbt("st_r", [128, NB])
    st_b = sbt("st_b", [128, NB])
    stb2 = sbt("stb2", [128, NB])
    ident8 = sbt("ident8", [128, 128], FP8)
    identb = sbt("identb", [128, 128], BF16)
    maskUT = sbt("maskUT", [128, 128], BF16)
    maskW = sbt("maskW", [128, 512], BF16)
    g_fm = sbt("g_fm", [128, DC])
    bvs_bc = sbt("bvs_bc", [128, D], BF16)
    wbuf = sbt("wbuf", [128, NVIS * 2])
    stwb = sbt("stwb", [128, NB * 2])
    mglob = sbt("mglob", [128, 4])
    cw0 = sbt("cw0", [128, HC])
    cw1 = sbt("cw1", [128, HC])
    cw2 = sbt("cw2", [128, HC])
    cbv = sbt("cbv", [128, HC])
    onesb = sbt("onesb", [128, 1], BF16)
    zrow = sbt("zrow", [1, 260], BF16)
    tiny = sbt("tiny", [128, 96])
    trash = sbt("trash", [128, D])
    tmprow = sbt("tmprow", [128, D])
    onesf = sbt("onesf", [1, 128])
    id4f = sbt("id4f", [4, 4])
    hsel = sbt("hsel", [32, 4])
    agbuf = sbt("agbuf", [32, D])
    halox = sbt("halox", [4, D])
    halor = sbt("halor", [128, DC, 4], BF16)
    qa = sbt("qa", [128, NB])
    qs1a = sbt("qs1a", [128, NB])
    qs1b = sbt("qs1b", [128, NB])
    qsi = sbt("qsi", [128, NB])
    s2t = sbt("s2t", [128, BT])
    s2b16 = sbt("s2b16", [128, BT], BF16)
    s2i = sbt("s2i", [128, BT])

    sems = {}
    for e in Sched.ENGS:
        sems[e + "c"] = es.enter_context(nc.semaphore("sm_" + e + "c"))
        sems[e + "d"] = es.enter_context(nc.semaphore("sm_" + e + "d"))
    sems["Gl"] = es.enter_context(nc.semaphore("sm_Gl"))

    pb = [es.enter_context(nc.psum_tensor(f"pb{i}", [128, 512], F32))
          for i in range(8)]
    PSK = [("pb", i) for i in range(8)]
    fence = sbt("fence", [1, 8])
    sch.emit("G", lambda E: E.memset(fence[:], 0.0), (),
             (("fence", "init"),))
    sch.fence_fns = {
        "V": lambda E: E.tensor_copy(fence[0:1, 0:1], fence[0:1, 1:2]),
        "S": lambda E: E.activation(fence[0:1, 2:3], fence[0:1, 3:4],
                                    AF.Copy, bias=0.0, scale=1.0),
        "G": lambda E: E.memset(fence[0:1, 4:5], 0.0),
        "T": lambda E: E.matmul(pb[0][0:1, 0:1], fence[0:1, 5:6],
                                fence[0:1, 6:7], start=True, stop=True),
    }
    sch.fence_writes = {"T": (("pb", 0),)}

    def V(fn, r, w, tag=""):
        sch.emit("V", fn, r, w, tag=tag)

    def S(fn, r, w, tag=""):
        sch.emit("S", fn, r, w, tag=tag)

    def T(fn, r, w, accum=False, tag=""):
        sch.emit("T", fn, r, w, accum=accum, tag=tag)

    def G(fn, r, w, tag=""):
        sch.emit("G", fn, r, w, tag=tag)

    def DMA(eng, out, in_, r, w, tag=""):
        sch.emit(eng, lambda E, o=out, i=in_: E.dma_start(out=o, in_=i),
                 r, w, is_dma=True, tag=tag)

    dump_n = [0]

    def dump(ap, reads):
        if dbg2 is None:
            return
        i = dump_n[0]
        dump_n[0] += 1
        cols = 1
        for sdim in ap.shape[1:]:
            cols *= sdim
        rows = ap.shape[0]
        DMA("Y", dbg2[i, 0:rows, 0:cols], ap, tuple(reads) + (("dbg2", i),),
            (("dbg2", i),), tag="dump")

    def ln_sq_tile(tile_ap, rkeys, j, tag="", base=0):
        S(lambda E, a=tile_ap, o=st_q[:, base + j:base + j + 1]:
          E.activation(trash[:], a, AF.Square, bias=0.0, scale=1.0,
                       accum_out=o),
          tuple(rkeys), (("st_q", base + j), "trash"), tag)

    def ln_sums_tile(tile_ap, rkeys, j, tag="", base=0, sum_on_scalar=False,
                     square=True):
        if sum_on_scalar:
            S(lambda E, a=tile_ap, o=st_s[:, base + j:base + j + 1]:
              E.activation(trash[:], a, AF.Identity, bias=0.0, scale=1.0,
                           accum_out=o),
              tuple(rkeys), (("st_s", base + j), "trash"), tag)
        else:
            V(lambda E, a=tile_ap, o=st_s[:, base + j:base + j + 1]:
              E.tensor_reduce(o, a, AX, ALU.add),
              tuple(rkeys), (("st_s", base + j),), tag)
        if square:
            ln_sq_tile(tile_ap, rkeys, j, tag, base)

    def ln_stats(tiles_ap, rkey, n, tag="", base=0, sum_on_scalar=False):
        for j in range(n):
            ln_sums_tile(tiles_ap(j), (rkey(j),), j, tag, base, sum_on_scalar)
        ln_finish(n, tag, base)

    def ln_finish_mean(n, tag="", base=0):
        cs = slice(base, base + n)
        rk = tuple(("st_s", base + j) for j in range(n))
        km = ("st_m", base); kb = ("st_b", base)
        V(lambda E: E.tensor_scalar(st_m[:, cs], st_s[:, cs], INVD, None,
                                    ALU.mult), rk, (km,), tag)
        V(lambda E: E.tensor_scalar(st_b[:, cs], st_m[:, cs], -1.0, None,
                                    ALU.mult), (km,), (kb,), tag)

    def ln_finish_rstd(n, tag="", base=0):
        cs = slice(base, base + n)
        qk = tuple(("st_q", base + j) for j in range(n))
        km = ("st_m", base); kv = ("st_v", base); kq2 = ("st_q2", base)
        kr = ("st_r", base); kb = ("st_b", base); kb2 = ("stb2", base)
        V(lambda E: E.tensor_tensor(st_v[:, cs], st_m[:, cs], st_m[:, cs],
                                    ALU.mult), (km,), (kv,), tag)
        V(lambda E: E.scalar_tensor_tensor(st_v[:, cs], st_q[:, cs], INVD,
                                           st_v[:, cs], ALU.mult,
                                           ALU.subtract),
          qk + (kv,), (kv,), tag)
        V(lambda E: E.tensor_scalar(st_v[:, cs], st_v[:, cs], EPS_LN, None,
                                    ALU.add), (kv,), (kv,), tag)
        S(lambda E: E.activation(st_q[:, cs], st_v[:, cs], AF.Sqrt,
                                 bias=0.0, scale=1.0),
          (kv,) + qk, (kq2,), tag)
        V(lambda E: E.reciprocal(st_r[:, cs], st_q[:, cs]),
          (kq2,), (kr,), tag)
        V(lambda E: E.tensor_tensor(st_q[:, cs], st_r[:, cs], st_r[:, cs],
                                    ALU.mult), (kr, kq2), (kq2,), tag)
        V(lambda E: E.tensor_tensor(st_q[:, cs], st_q[:, cs], st_v[:, cs],
                                    ALU.mult), (kq2, kv), (kq2,), tag)
        V(lambda E: E.tensor_scalar(st_q[:, cs], st_q[:, cs], -0.5, 1.5,
                                    ALU.mult, ALU.add), (kq2,), (kq2,), tag)
        V(lambda E: E.tensor_tensor(st_r[:, cs], st_r[:, cs], st_q[:, cs],
                                    ALU.mult), (kr, kq2), (kr,), tag)
        V(lambda E: E.tensor_tensor(stb2[:, cs], st_b[:, cs], st_r[:, cs],
                                    ALU.mult), (kb, kr), (kb2,), tag)

    def ln_finish(n, tag="", base=0):
        ln_finish_mean(n, tag, base)
        ln_finish_rstd(n, tag, base)

    def tp8(src_ap_fn, src_keys, dst_fn, dst_key_fn, dst_dt_evac, tag,
            scale_ap_fn=None, scale_key=None, banks=(4, 5),
            wide_dst_fn=None, wide_key_fn=None):
        for half in range(2):
            b = banks[half]
            idt = "id8" if dst_dt_evac == FP8 else "idb"
            idap = ident8 if dst_dt_evac == FP8 else identb
            for k in range(4):
                dc = half * 4 + k
                T(lambda E, d=dc, bb=b, kk=k, ia=idap: E.matmul(
                    pb[bb][:, kk * 128:(kk + 1) * 128], src_ap_fn(d), ia[:],
                    start=True, stop=True),
                  tuple(src_keys) + (idt,), (PSK[b],),
                  accum=True, tag=tag)
            if wide_dst_fn is not None:
                S(lambda E, h=half, bb=b: E.activation(
                    wide_dst_fn(h), pb[bb][:], AF.Copy, bias=0.0, scale=1.0),
                  (PSK[b],), (wide_key_fn(half),), tag=tag)
                continue
            for k in range(4):
                dc = half * 4 + k
                if scale_ap_fn is not None:
                    S(lambda E, d=dc, bb=b, kk=k: E.activation(
                        dst_fn(d), pb[bb][:, kk * 128:(kk + 1) * 128],
                        AF.Copy, bias=0.0, scale=scale_ap_fn(d)),
                      (PSK[b], scale_key), (dst_key_fn(dc),),
                      tag=tag)
                else:
                    S(lambda E, d=dc, bb=b, kk=k: E.activation(
                        dst_fn(d), pb[bb][:, kk * 128:(kk + 1) * 128],
                        AF.Copy, bias=0.0, scale=1.0),
                      (PSK[b],), (dst_key_fn(dc),), tag=tag)

    G(lambda E: E.memset(id4f[:], 0.0), (), ("id4f",))
    G(lambda E: E.affine_select(out=id4f[:], in_=id4f[:],
                                compare_op=ALU.not_equal, fill=1.0, base=0,
                                pattern=[[-1, 4]], channel_multiplier=1),
      ("id4f",), ("id4f",))
    for name, buf in (("id8", ident8), ("idb", identb)):
        G(lambda E, b=buf: E.memset(b[:], 0.0), (), (name,))
        G(lambda E, b=buf: E.affine_select(
            out=b[:], in_=b[:], compare_op=ALU.not_equal, fill=1.0, base=0,
            pattern=[[-1, 128]], channel_multiplier=1), (name,), (name,))
    G(lambda E: E.memset(maskUT[:], 0.0), (), ("mUT",))
    G(lambda E: E.affine_select(out=maskUT[:], in_=maskUT[:],
                                compare_op=ALU.is_gt, fill=1.0, base=0,
                                pattern=[[-1, 128]], channel_multiplier=1),
      ("mUT",), ("mUT",))
    for k in range(4):
        V(lambda E, kk=k: E.tensor_copy(maskW[:, kk * 128:(kk + 1) * 128],
                                        maskUT[:]), ("mUT",), (("mW", k),))
    G(lambda E: E.memset(onesb[:], 1.0), (), ("onesb",))
    G(lambda E: E.memset(zrow[:], 0.0), (), ("zrow",))
    G(lambda E: E.memset(onesf[:], 1.0), (), ("ones1f",))

    DMA("Y", tiny[:, 0:DC], bvq_.rearrange("(c p) -> p c", p=128), (),
        (("tiny", 0),))
    DMA("Y", tiny[:, 8:8 + DC], bvk_.rearrange("(c p) -> p c", p=128), (),
        (("tiny", 1),))
    S(lambda E: E.activation(tiny[:, 16:16 + DC], tiny[:, 0:DC], AF.Sign,
                             bias=0.0, scale=1.0), (("tiny", 0),),
      (("tiny", 2),))
    S(lambda E: E.activation(tiny[:, 24:24 + DC], tiny[:, 8:8 + DC], AF.Sign,
                             bias=0.0, scale=1.0), (("tiny", 1),),
      (("tiny", 3),))
    V(lambda E: E.tensor_tensor(g_fm[:], tiny[:, 16:16 + DC],
                                tiny[:, 24:24 + DC], ALU.mult),
      (("tiny", 2), ("tiny", 3)), ("g_fm",))
    DMA("Y", trash[:], bvv_[None, :].partition_broadcast(128), ("trash",),
        ("trash",))
    S(lambda E: E.activation(bvs_bc[:], trash[:], AF.Sign, bias=0.0,
                             scale=1.0), ("trash",), ("bvs",))
    DMA("Y", wbuf[:], wsel[None, :].partition_broadcast(128), (), ("wbuf",))
    DMA("Y", stwb[:], stw[None, :].partition_broadcast(128), (), ("stwb",))
    DMA("G", hsel[:], halosel, (), ("hsel",))

    V(lambda E: E.memset(tiny[:, 40:42], 0.0), (), (("tiny", 40), ("tiny", 41)))
    with nc.sbuf_tensor("wmA", [128, D], F32) as wmA, \
         nc.sbuf_tensor("wmB", [128, D], F32) as wmB, \
         nc.sbuf_tensor("wmC", [128, D], F32) as wmC, \
         nc.sbuf_tensor("wmD", [128, D], F32) as wmD:
        wmbufs = ((trash, "trash", "Y"), (tmprow, "tmprow", "S"),
                  (wmA, "wmA", "G"), (wmB, "wmB", "Y"),
                  (wmC, "wmC", "S"), (wmD, "wmD", "G"))
        wmk = 0
        for wi, (wap, nchunk, ncols) in enumerate(((w1_, HC, D),
                                                   (w2_, DC, HID))):
            for ch in range(nchunk):
                for piece in range(ncols // D):
                    buf, bkey, q = wmbufs[wmk % 6]
                    wmk += 1
                    DMA(q, buf[:],
                        wap[ch * 128:(ch + 1) * 128,
                            piece * D:(piece + 1) * D],
                        (bkey,), (bkey,), tag="wm")
                    V(lambda E, b=buf: E.tensor_reduce(
                        tiny[:, 44:45], b[:], AX, ALU.add,
                        apply_absolute_value=True),
                      (bkey,), (("tiny", 44),), "wm")
                    V(lambda E, c=40 + wi: E.tensor_tensor(
                        tiny[:, c:c + 1], tiny[:, c:c + 1], tiny[:, 44:45],
                        ALU.add),
                      (("tiny", 40 + wi), ("tiny", 44)),
                      (("tiny", 40 + wi),), "wm")
    sch.barrier()
    G(lambda E: E.memset(tiny[:, 46:47], 1.0), (), (("tiny", 46),))
    T(lambda E: E.matmul(pb[0][0:2, 0:1], tiny[:, 40:42], tiny[:, 46:47],
                         start=True, stop=True),
      (("tiny", 40), ("tiny", 41), ("tiny", 46)), (PSK[0],))
    V(lambda E: E.tensor_scalar(tiny[0:2, 48:49], pb[0][0:2, 0:1],
                                1.0 / (HID * D), 1e-5, ALU.mult, ALU.max),
      (PSK[0],), (("tiny", 48),))
    V(lambda E: E.reciprocal(tiny[0:2, 49:50], tiny[0:2, 48:49]),
      (("tiny", 48),), (("tiny", 49),))
    T(lambda E: E.matmul(pb[2][0:1, 0:2], tiny[0:2, 48:49], id4f[0:2, 0:2],
                         start=True, stop=True),
      (("tiny", 48), "id4f"), (PSK[2],))
    T(lambda E: E.matmul(pb[2][0:1, 4:6], tiny[0:2, 49:50], id4f[0:2, 0:2],
                         start=True, stop=True),
      (("tiny", 49), "id4f"), (PSK[2],), accum=True)
    V(lambda E: E.tensor_copy(tiny[0:1, 52:54], pb[2][0:1, 0:2]),
      (PSK[2],), (("tiny", 52),))
    V(lambda E: E.tensor_copy(tiny[0:1, 54:56], pb[2][0:1, 4:6]),
      (PSK[2],), (("tiny", 54),))
    T(lambda E: E.matmul(pb[3][:, 0:4], onesf[:], tiny[0:1, 52:56],
                         start=True, stop=True),
      (("tiny", 52), ("tiny", 54), "ones1f"), (PSK[3],))
    V(lambda E: E.tensor_copy(mglob[:], pb[3][:, 0:4]),
      (PSK[3],), ("mglob",))

    for k, buf in ((0, cw0), (1, cw1), (2, cw2)):
        DMA("Y", buf[:], convw_.rearrange("(c p) t -> p c t", p=128)[:, :, k],
            (), (("cw", k),))
        V(lambda E, b=buf, kk=k: E.tensor_scalar(b[:], b[:], mglob[:, 0:1],
                                                 None, ALU.mult),
          (("cw", k), "mglob"), (("cw", k),))
    DMA("Y", cbv[:], convb_.rearrange("(c p) -> p c", p=128), (), ("cbv",))

    with nc.sbuf_tensor("wload0", [128, 2, D], F32) as wl0, \
         nc.sbuf_tensor("wload1", [128, 2, D], F32) as wl1, \
         nc.sbuf_tensor("wi32", [128, 2, D], I32) as wi32, \
         nc.sbuf_tensor("wtern0", [128, 2, D], BF16) as wt0, \
         nc.sbuf_tensor("wtern1", [128, 2, D], BF16) as wt1, \
         nc.sbuf_tensor("wtpA", [128, 512], BF16) as wtpA, \
         nc.sbuf_tensor("wtpB", [128, 512], BF16) as wtpB, \
         nc.sbuf_tensor("wtpC", [128, 512], BF16) as wtpC, \
         nc.sbuf_tensor("wtpD", [128, 512], BF16) as wtpD:
        wloads = (wl0, wl1)
        wterns = (wt0, wt1)
        wtps = (wtpA, wtpB, wtpC, wtpD)
        qn = [0]
        stq = ("S", "G")

        def quant_pair(src_ap, swcol, store_fns):
            sl = qn[0] % 2
            qn[0] += 1
            wload = wloads[sl]
            wtern = wterns[sl]
            DMA("Y", wload[:], src_ap, (("wld", sl),),
                (("wld", sl),), tag="wql")
            V(lambda E, c=swcol, b=wload: E.tensor_scalar(
                wi32[:].rearrange("p a c -> p (a c)"),
                b[:].rearrange("p a c -> p (a c)"),
                mglob[:, c:c + 1], None, ALU.mult),
              (("wld", sl), "mglob"), ("wi32",))
            V(lambda E, b=wtern: E.tensor_scalar(
                b[:].rearrange("p a c -> p (a c)"),
                wi32[:].rearrange("p a c -> p (a c)"), -1.0, 1.0,
                ALU.max, ALU.min),
              ("wi32",), (("wtn", sl),))
            for ch in range(2):
                for half in range(2):
                    b = 4 + half
                    wtp = wtps[ch * 2 + half]
                    for k in range(4):
                        dc = half * 4 + k
                        T(lambda E, d=dc, bb=b, kk=k, wt=wtern, cc=ch:
                          E.matmul(
                              pb[bb][:, kk * 128:(kk + 1) * 128],
                              wt[:, cc, d * 128:(d + 1) * 128], identb[:],
                              start=True, stop=True),
                          (("wtn", sl), "idb"), (PSK[b],), accum=True,
                          tag="wqt")
                    S(lambda E, bb=b, wp=wtp: E.activation(
                        wp[:], pb[bb][:], AF.Identity, bias=0.0, scale=1.0),
                      (PSK[b],), (("wtp", ch, half),))
                    store_fns[ch](half, wtp, ("wtp", ch, half),
                                  stq[(ch * 2 + half) % 2])

        def mk_st1(hh):
            def st1(half, wtp, wkey, q):
                DMA(q, w1q_s[hh, half], wtp[:], (wkey,),
                    (("w1qs", hh, half),), tag="w1st")
            return st1

        for hc2 in range(HC // 2):
            quant_pair(w1_[hc2 * 256:(hc2 + 1) * 256, :]
                       .rearrange("(a p) c -> p a c", p=128), 2,
                       (mk_st1(hc2 * 2), mk_st1(hc2 * 2 + 1)))

        def mk_st2(dd, pp):
            def st2(half, wtp, wkey, q):
                r0 = (pp * 8 + half * 4) * 128
                DMA(q, w2q_s[r0:r0 + 512, dd * 128:(dd + 1) * 128]
                    .rearrange("(a p) c -> p a c", p=128),
                    wtp[:].rearrange("p (a c) -> p a c", c=128),
                    (wkey,),
                    tuple(("w2qs", pp, half * 4 + k, dd)
                          for k in range(4)),
                    tag="w2st")
            return st2

        for dmc in range(DC):
            for p2 in range(2):
                quant_pair(w2_[dmc * 128:(dmc + 1) * 128,
                               p2 * 2 * D:(p2 + 1) * 2 * D]
                           .rearrange("p (a c) -> p a c", c=D), 3,
                           (mk_st2(dmc, p2 * 2), mk_st2(dmc, p2 * 2 + 1)))

    sch.barrier()

    ph1 = ExitStack()
    _p = lambda n, s, d=F32: ph1.enter_context(nc.sbuf_tensor(n, s, d))
    qLc = _p("qLc", [128, DC, BLK], FP8)
    qHc = _p("qHc", [128, DC, BLK], FP8)
    qmix = _p("qmix", [128, DC, BLK], FP8)
    kfm = [_p("kfm0", [128, BT, D], FP8), _p("kfm1", [128, BT, D], FP8)]
    xnk2 = [_p("xnk0", [128, BT, D], BF16), _p("xnk1", [128, BT, D], BF16)]
    x0k = _p("x0k", [128, BT, D], F32)
    attnT = _p("attnT", [128, BT, BLK], BF16)
    _sg = _p("sg80", [128, D], FP8)
    sg8 = [_sg, _sg]
    if True:
        for t in range(2 * BT):
            V(lambda E, tt=t: E.memset(resid[:, tt, :], 0.0), (),
              (("resid", t),))

        def emit_dmas(src_ap_fn):
            for t in range(BT):
                DMA("Y" if t % 2 == 0 else "G", x0k[:, t, :], src_ap_fn(t),
                    (("x0k", t),), (("x0k", t),), tag="kld")

        def emit_stats_sums(par, refine=False):
            base = par * BT
            for t in range(BT):
                ln_sums_tile(x0k[:, t, :], (("x0k", t),), t, "st", base,
                             False, square=False)
            ln_finish_mean(BT, "st", base)
            if refine:
                emit_refine(par)

        def emit_squares(par):
            base = par * BT
            for t in range(BT):
                ln_sq_tile(x0k[:, t, :], (("x0k", t),), t, "st", base)

        def emit_refine(par):
            base = par * BT
            for t in range(BT):
                S(lambda E, b=base, tt=t: E.activation(
                    trash[:], x0k[:, tt, :], AF.Identity,
                    bias=st_b[:, b + tt:b + tt + 1], scale=1.0),
                  (("x0k", t), ("st_b", base)), ("trash",), tag="cmp")
                V(lambda E: E.tensor_reduce(
                    tiny[:, 60:61], trash[:], AX, ALU.add),
                  ("trash",), (("tiny", 60),), tag="cmp")
                V(lambda E, tt=t, b=base: E.scalar_tensor_tensor(
                    st_b[:, b + tt:b + tt + 1], tiny[:, 60:61], -INVD,
                    st_b[:, b + tt:b + tt + 1], ALU.mult, ALU.add),
                  (("tiny", 60), ("st_b", base)), (("st_b", base),),
                  tag="cmp")

        def emit_stats_sign(par, refine=False):
            emit_stats_sums(par, refine)
            emit_squares(par)
            ln_finish_rstd(BT, "st", par * BT)

        def emit_sign_tp(dst, dstkey, gscale, par, tiles):
            base = par * BT
            sg = sg8[par]
            sgk = ("sg8",)
            for t in tiles:
                S(lambda E, tt=t, b=base, s2=sg: E.activation(
                    s2[:], x0k[:, tt, :], AF.Sign,
                    bias=st_b[:, b + tt:b + tt + 1], scale=1.0),
                  (("x0k", t), ("st_b", base)), (sgk,), tag="sgn")
                tp8(lambda d, s2=sg: s2[:, d * 128:(d + 1) * 128], (sgk,),
                    (lambda d, tt=t, d2=dst: d2[:, d, tt * 128:(tt + 1) * 128]),
                    (lambda dc, tt=t, dk=dstkey: dk + (dc, tt)), FP8,
                    "sg" + str(dstkey),
                    scale_ap_fn=(lambda d: g_fm[:, d:d + 1]) if gscale
                    else None,
                    scale_key="g_fm" if gscale else None)

        def emit_sign_tpk(par, tiles):
            base = par * BT
            sg = sg8[par]
            sgk = ("sg8",)
            kb2 = kfm[par]
            for t in tiles:
                S(lambda E, tt=t, b=base, s2=sg: E.activation(
                    s2[:], x0k[:, tt, :], AF.Sign,
                    bias=st_b[:, b + tt:b + tt + 1], scale=1.0),
                  (("x0k", t), ("st_b", base)), (sgk,), tag="sgn")
                tp8(lambda d, s2=sg: s2[:, d * 128:(d + 1) * 128], (sgk,),
                    None, None, FP8, "sgk",
                    wide_dst_fn=(lambda h, tt=t, kb=kb2:
                                 kb[:, tt, h * 512:(h + 1) * 512]),
                    wide_key_fn=(lambda h, tt=t, pp=par: ("kf", pp, tt, h)))

        def prep_partA(src_ap_fn, dst, dstkey, gscale, par, refine=False):
            emit_dmas(src_ap_fn)
            emit_stats_sign(par, refine)
            emit_sign_tp(dst, dstkey, gscale, par, range(BT))

        def prep_partK(src_ap_fn, par):
            emit_dmas(src_ap_fn)
            emit_stats_sign(par)
            emit_sign_tpk(par, range(BT))

        def emit_xn_tile(v, t):
            base = (v % 2) * BT
            xb = xnk2[v % 2]
            S(lambda E, tt=t, b=base, x2=xb: E.activation(
                x2[:, tt, :], x0k[:, tt, :], AF.Identity,
                bias=stb2[:, b + tt:b + tt + 1],
                scale=st_r[:, b + tt:b + tt + 1]),
              (("x0k", t), ("st_r", base), ("stb2", base)),
              (("xnk", v % 2, t),), tag="xnk")

        def emit_dma_tile(src_ap_fn, t):
            DMA("Y" if t % 2 == 0 else "G", x0k[:, t, :], src_ap_fn(t),
                (("x0k", t),), (("x0k", t),), tag="kld")

        def emit_sum_tile(par, t):
            ln_sums_tile(x0k[:, t, :], (("x0k", t),), t, "st", par * BT,
                         False, square=False)

        def emit_xn(v):
            for t in range(BT):
                emit_xn_tile(v, t)

        for s, qdst, qkey in ((0, qLc, ("qL",)), (1, qHc, ("qH",))):
            prep_partA(lambda t, s2=s: myblk[s2, t], qdst, qkey, True, s,
                       refine=True)

        qL_reads = tuple(("qL", dc, t) for dc in range(DC) for t in range(BT))
        qH_reads = tuple(("qH", dc, t) for dc in range(DC) for t in range(BT))

        def emit_qmix_piece(v, idx):
            wLap = wbuf[:, 2 * v:2 * v + 1]
            wHap = wbuf[:, 2 * v + 1:2 * v + 2]
            dc = idx // 2
            if idx % 2 == 0:
                V(lambda E, d=dc, w=wLap: E.tensor_scalar(
                    qmix[:, d, :], qLc[:, d, :], w, None, ALU.mult),
                  qL_reads + ("wbuf",), (("qmix", dc),), tag="qmx")
            else:
                V(lambda E, d=dc, w=wHap: E.scalar_tensor_tensor(
                    qmix[:, d, :], qHc[:, d, :], w, qmix[:, d, :],
                    ALU.mult, ALU.add),
                  qH_reads + ("wbuf", ("qmix", dc)), (("qmix", dc),),
                  tag="qmx")

        def emit_qmix(v):
            for idx in range(2 * DC):
                emit_qmix_piece(v, idx)

        def emit_scores(v, inject_fn=None):
            par = v % 2
            diag = v >= 15
            kbuf = kfm[par]
            sb_rot = 0
            nsteps = 12 if diag else 16
            for jc in range(BT):
                for ih in range(2):
                    if diag and jc > ih * 4 + 3:
                        continue
                    if inject_fn is not None:
                        t_inj = sb_rot - (nsteps - BT)
                        if t_inj >= 0:
                            inject_fn(t_inj)
                    b = sb_rot % 4
                    sb_rot += 1
                    for dc in range(DC):
                        T(lambda E, d=dc, j=jc, i=ih, bb=b, kb2=kbuf:
                          E.matmul(
                              pb[bb][:], kb2[:, j, d * 128:(d + 1) * 128],
                              qmix[:, d, i * 512:(i + 1) * 512],
                              start=(d == 0), stop=(d == DC - 1)),
                          (("kf", par, jc, dc // 4), ("qmix", dc)),
                          (PSK[b],), accum=True, tag="sc")
                    S(lambda E, j=jc, i=ih, bb=b: E.activation(
                        attnT[:, j, i * 512:(i + 1) * 512], pb[bb][:],
                        AF.Sigmoid, bias=0.0, scale=0.125),
                      (PSK[b],), (("aT", jc, ih),), tag="sig")
                    if diag:
                        for k in range(4):
                            ic = ih * 4 + k
                            if ic == jc:
                                V(lambda E, j=jc, c=ic: E.tensor_tensor(
                                    attnT[:, j, c * 128:(c + 1) * 128],
                                    attnT[:, j, c * 128:(c + 1) * 128],
                                    maskUT[:], ALU.mult),
                                  (("aT", jc, ih), "mUT"), (("aT", jc, ih),),
                                  tag="msk")
        def emit_av(v, inject_fn=None):
            par = v % 2
            diag = v >= 15
            wLap = wbuf[:, 2 * v:2 * v + 1]
            wHap = wbuf[:, 2 * v + 1:2 * v + 2]
            av_rot = 0
            for ic in range(BT):
                njc = (ic + 1) if diag else BT
                for dmh in range(2):
                    b = 6 + av_rot % 2
                    av_rot += 1
                    for jj in range(njc):
                        T(lambda E, j=jj, c=ic, m=dmh, bb=b, n=njc,
                          x2=xnk2[v % 2]:
                          E.matmul(
                              pb[bb][:], attnT[:, j, c * 128:(c + 1) * 128],
                              x2[:, j, m * 512:(m + 1) * 512],
                              start=(j == 0), stop=(j == n - 1)),
                          (("aT", jj, ic // 4), ("xnk", v % 2, jj)),
                          (PSK[b],), accum=True, tag="av")
                    for slot, wap in ((0, wLap), (1, wHap)):
                        rt = slot * BT + ic
                        V(lambda E, bb=b, r=rt, m=dmh, w=wap:
                          E.scalar_tensor_tensor(
                              resid[:, r, m * 512:(m + 1) * 512], pb[bb][:],
                              w, resid[:, r, m * 512:(m + 1) * 512],
                              ALU.mult, ALU.add),
                          ((PSK[b],) + ("wbuf", ("resid", rt))),
                          (("resid", rt),), tag="avac")
                    if inject_fn is not None:
                        inject_fn(av_rot - 1)

        prep_partK(lambda t: kvis[0, t], 0)
        emit_xn(0)
        emit_qmix(0)
        for t in range(BT):
            emit_dma_tile(lambda tt: kvis[1, tt], t)
            emit_sum_tile(1, t)
        for v in range(NVIS):
            nx = v + 1
            if nx < NVIS:
                ln_finish_mean(BT, "st", (nx % 2) * BT)
                inj_tp = lambda t, p=nx % 2: emit_sign_tpk(p, [t])
                work = []
                for i in range(8):
                    work.append(lambda i=i, n=nx: emit_qmix_piece(n, i))
                work.append(
                    lambda n=nx: ln_finish_rstd(BT, "st", (n % 2) * BT))
                for i in range(8, 16):
                    work.append(lambda i=i, n=nx: emit_qmix_piece(n, i))
                for t in range(BT):
                    def wtile(t=t, n=nx):
                        emit_xn_tile(n, t)
                        if n + 1 < NVIS:
                            emit_dma_tile(
                                lambda tt, v2=n + 1: kvis[v2, tt], t)
                            emit_sum_tile((n + 1) % 2, t)
                    work.append(wtile)
                done = [0]

                def inj_qm(slot, w=work, d=done):
                    want = (len(w) * (slot + 1) + 15) // 16
                    while d[0] < min(want, len(w)):
                        w[d[0]]()
                        d[0] += 1
            else:
                inj_tp = None
                inj_qm = None
            emit_scores(v, inject_fn=inj_tp)
            if nx < NVIS:
                emit_squares(nx % 2)
            emit_av(v, inject_fn=inj_qm)
            if nx < NVIS:
                inj_qm(15)

        def fold_dma(rt):
            sl = rt % 4
            DMA("Y" if rt % 2 == 0 else "G", x0k[:, sl, :],
                myblk[rt // BT, rt % BT],
                (("x0k", sl),), (("x0k", sl),), tag="fold")
        for rt in range(4):
            fold_dma(rt)
        for rt in range(2 * BT):
            sl = rt % 4
            V(lambda E, r=rt: E.tensor_tensor(tmprow[:], resid[:, r, :],
                                              bvs_bc[:], ALU.mult),
              (("resid", rt), "bvs"), ("tmprow",), tag="fold")
            V(lambda E, r=rt, s2=sl: E.tensor_tensor(
                resid[:, r, :], tmprow[:], x0k[:, s2, :], ALU.add),
              ("tmprow", ("x0k", sl)), (("resid", rt),), tag="fold")
            if rt + 4 < 2 * BT:
                fold_dma(rt + 4)

    ph1.close()
    sch.barrier()
    if debug >= 1:
        for t in range(2 * BT):
            DMA("Y", dbg[0, t * 128:(t + 1) * 128, :], resid[:, t, :],
                (("resid", t),), (("dbg", 0, t),), tag="dbg")
        sch.barrier()

    def ffn_sublayer(agi):
        phf = ExitStack()
        _q = lambda n, s, d=F32: phf.enter_context(nc.sbuf_tensor(n, s, d))
        hbuf = _q(f"hbuf{agi}", [128, HC, BLK + 2], BF16)
        r1fm = _q(f"r1fm{agi}", [128, DC, BLK + 2], BF16)
        wsh = _q(f"wsh{agi}", [128, 2048], BF16)
        wshb = _q(f"wshb{agi}", [128, 2048], BF16)
        wshs = (wsh, wshb)
        xq1 = _q(f"xq1{agi}", [128, D], BF16)
        fi32 = _q(f"fi32{agi}", [128, D], I32)
        utmp = _q(f"utmp{agi}", [128, BLK], F32)
        utmps = ((utmp, ("ut",)), (trash, ("trash",)))
        r2s = _q(f"r2s{agi}", [128, 512], BF16)
        r2sb = _q(f"r2sb{agi}", [128, 512], BF16)
        r2ss = (r2s, r2sb)
        amaxp = _q(f"amx{agi}", [128, BLK], BF16)
        aminp = _q(f"amn{agi}", [128, BLK], BF16)
        haloq = _q(f"hq{agi}", [4, D], BF16)
        haloi = _q(f"hi{agi}", [4, D], I32)
        s2bc = _q(f"s2bc{agi}", [128, BLK], BF16)
        if True:

            edges = [(0, 0, 0), (0, BT - 1, 127), (1, 0, 0), (1, BT - 1, 127)]
            for r, (s, t, p) in enumerate(edges):
                DMA("Y", ag_ei[agi][r:r + 1, :],
                    resid[p:p + 1, s * BT + t, :],
                    (("resid", s * BT + t),), (("aei", agi, r),), tag="edg")
            sch.emit("G", lambda E: E.collective_compute(
                "AllGather", ALU.bypass,
                replica_groups=[list(range(NCORES))],
                ins=[ag_ei[agi][:]], outs=[ag_eo[agi][:]]),
                tuple(("aei", agi, r) for r in range(4)), (("aeo", agi),),
                is_dma=True, is_coll=True, tag="AG")
            DMA("Y", agbuf[:], ag_eo[agi], (("aeo", agi), "agbuf"),
                ("agbuf",), tag="agl")
            ln_stats(lambda j: resid[:, j, :], lambda j: ("resid", j), NB,
                     "fst")
            for half in range(2):
                T(lambda E, m=half: E.matmul(
                    pb[6][0:4, 0:512], hsel[:],
                    agbuf[:, m * 512:(m + 1) * 512],
                    start=True, stop=True),
                  ("hsel", "agbuf"), (PSK[6],), tag="hx")
                V(lambda E, m=half: E.tensor_copy(
                    halox[0:4, m * 512:(m + 1) * 512], pb[6][0:4, 0:512]),
                  (PSK[6],), ("halox",))
            hx = halox[0:4, :]
            ti = lambda a, b: tiny[0:4, a:b]
            V(lambda E: E.tensor_reduce(ti(32, 33), hx, AX, ALU.add),
              ("halox",), (("hti", 32),))
            S(lambda E: E.activation(trash[0:4, :], hx, AF.Square, bias=0.0,
                                     scale=1.0, accum_out=ti(33, 34)),
              ("halox",), (("hti", 33), "trash"))
            V(lambda E: E.tensor_scalar(ti(34, 35), ti(32, 33), INVD, None,
                                        ALU.mult), (("hti", 32),),
              (("hti", 34),))
            V(lambda E: E.tensor_tensor(ti(35, 36), ti(34, 35), ti(34, 35),
                                        ALU.mult), (("hti", 34),),
              (("hti", 35),))
            V(lambda E: E.scalar_tensor_tensor(ti(35, 36), ti(33, 34), INVD,
                                               ti(35, 36), ALU.mult,
                                               ALU.subtract),
              (("hti", 33), ("hti", 35)), (("hti", 35),))
            V(lambda E: E.tensor_scalar(ti(35, 36), ti(35, 36), EPS_LN,
                                        None, ALU.add),
              (("hti", 35),), (("hti", 35),))
            S(lambda E: E.activation(ti(35, 36), ti(35, 36), AF.Sqrt,
                                     bias=0.0, scale=1.0),
              (("hti", 35),), (("hti", 35),))
            V(lambda E: E.reciprocal(ti(36, 37), ti(35, 36)),
              (("hti", 35),), (("hti", 36),))
            V(lambda E: E.tensor_reduce(ti(37, 38), hx, AX, ALU.max),
              ("halox",), (("hti", 37),))
            V(lambda E: E.tensor_reduce(ti(38, 39), hx, AX, ALU.min),
              ("halox",), (("hti", 38),))
            V(lambda E: E.tensor_tensor(ti(37, 38), ti(37, 38), ti(34, 35),
                                        ALU.subtract),
              (("hti", 37), ("hti", 34)), (("hti", 37),))
            V(lambda E: E.tensor_tensor(ti(38, 39), ti(34, 35), ti(38, 39),
                                        ALU.subtract),
              (("hti", 38), ("hti", 34)), (("hti", 38),))
            V(lambda E: E.tensor_tensor(ti(37, 38), ti(37, 38), ti(38, 39),
                                        ALU.max),
              (("hti", 37), ("hti", 38)), (("hti", 37),))
            V(lambda E: E.tensor_tensor(ti(37, 38), ti(37, 38), ti(36, 37),
                                        ALU.mult),
              (("hti", 37), ("hti", 36)), (("hti", 37),))
            V(lambda E: E.tensor_scalar(ti(37, 38), ti(37, 38), 1e-5, None,
                                        ALU.max), (("hti", 37),),
              (("hti", 37),))
            V(lambda E: E.reciprocal(ti(39, 40), ti(37, 38)),
              (("hti", 37),), (("hti", 39),))
            V(lambda E: E.tensor_scalar(ti(39, 40), ti(39, 40), 127.0, None,
                                        ALU.mult), (("hti", 39),),
              (("hti", 39),))
            V(lambda E: E.tensor_scalar(ti(41, 42), ti(37, 38), INV127, None,
                                        ALU.mult), (("hti", 37),),
              (("hti", 41),))
            V(lambda E: E.tensor_tensor(ti(42, 43), ti(36, 37), ti(39, 40),
                                        ALU.mult),
              (("hti", 36), ("hti", 39)), (("hti", 42),))
            V(lambda E: E.scalar_tensor_tensor(ti(43, 44), ti(34, 35), -1.0,
                                               ti(42, 43), ALU.mult,
                                               ALU.mult),
              (("hti", 34), ("hti", 42)), (("hti", 43),))
            V(lambda E: E.tensor_scalar(haloi[:], hx, ti(42, 43), ti(43, 44),
                                        ALU.mult, ALU.add),
              ("halox", ("hti", 42), ("hti", 43)), ("haloi",))
            V(lambda E: E.tensor_scalar(haloq[:], haloi[:], ti(41, 42), None,
                                        ALU.mult),
              ("haloi", ("hti", 41)), ("haloq",))
            for dc in range(DC):
                T(lambda E, d=dc: E.matmul(
                    pb[7][:, d * 4:(d + 1) * 4],
                    haloq[0:4, d * 128:(d + 1) * 128], identb[0:4, 0:4],
                    start=(d == 0), stop=(d == DC - 1)),
                  ("haloq", "idb"), (PSK[7],), accum=True, tag="ht")
            V(lambda E: E.tensor_copy(
                halor[:].rearrange("p a b -> p (a b)"), pb[7][:, 0:32]),
              (PSK[7],), ("halor",))

            for s in range(2):
                for t in range(BT):
                    rt = s * BT + t
                    V(lambda E, r=rt: E.tensor_reduce(
                        tiny[:, 64:65], resid[:, r, :], AX, ALU.max),
                      (("resid", rt),), (("fti", 64),))
                    V(lambda E, r=rt: E.tensor_reduce(
                        tiny[:, 65:66], resid[:, r, :], AX, ALU.min),
                      (("resid", rt),), (("fti", 65),))
                    V(lambda E, r=rt: E.tensor_tensor(
                        tiny[:, 64:65], tiny[:, 64:65], st_m[:, r:r + 1],
                        ALU.subtract),
                      (("fti", 64), ("st_m", 0)), (("fti", 64),))
                    V(lambda E, r=rt: E.tensor_tensor(
                        tiny[:, 65:66], st_m[:, r:r + 1], tiny[:, 65:66],
                        ALU.subtract),
                      (("fti", 65), ("st_m", 0)), (("fti", 65),))
                    V(lambda E: E.tensor_tensor(
                        tiny[:, 64:65], tiny[:, 64:65], tiny[:, 65:66],
                        ALU.max),
                      (("fti", 64), ("fti", 65)), (("fti", 64),))
                    V(lambda E, r=rt: E.tensor_tensor(
                        qa[:, r:r + 1], tiny[:, 64:65], st_r[:, r:r + 1],
                        ALU.mult),
                      (("fti", 64), ("st_r", 0)), (("qa", rt),))
                    V(lambda E, r=rt: E.tensor_scalar(
                        qa[:, r:r + 1], qa[:, r:r + 1], 1e-5, None, ALU.max),
                      (("qa", rt),), (("qa", rt),))
                    V(lambda E, r=rt: E.reciprocal(tiny[:, 66:67],
                                                   qa[:, r:r + 1]),
                      (("qa", rt),), (("fti", 66),))
                    V(lambda E, r=rt: E.scalar_tensor_tensor(
                        qs1a[:, r:r + 1], tiny[:, 66:67], 127.0,
                        st_r[:, r:r + 1], ALU.mult, ALU.mult),
                      (("fti", 66), ("st_r", 0)), (("qs1a", rt),))
                    V(lambda E, r=rt: E.scalar_tensor_tensor(
                        qs1b[:, r:r + 1], st_m[:, r:r + 1], -1.0,
                        qs1a[:, r:r + 1], ALU.mult, ALU.mult),
                      (("st_m", 0), ("qs1a", rt)), (("qs1b", rt),))
                    V(lambda E, r=rt: E.tensor_scalar(
                        qsi[:, r:r + 1], qa[:, r:r + 1], INV127, None,
                        ALU.mult), (("qa", rt),), (("qsi", rt),))
                    V(lambda E, r=rt: E.tensor_scalar(
                        fi32[:], resid[:, r, :], qs1a[:, r:r + 1],
                        qs1b[:, r:r + 1], ALU.mult, ALU.add),
                      (("resid", rt), ("qs1a", rt), ("qs1b", rt)), ("fi32",))
                    V(lambda E, r=rt: E.tensor_scalar(
                        xq1[:], fi32[:], qsi[:, r:r + 1], None, ALU.mult),
                      ("fi32", ("qsi", rt)), ("xq1",))
                    tp8(lambda d: xq1[:, d * 128:(d + 1) * 128], ("xq1",),
                        (lambda d, tt=t: r1fm[:, d,
                                              1 + tt * 128:1 + (tt + 1) * 128]),
                        (lambda dc, tt=t: ("r1", dc, tt)), BF16, "r1")
                V(lambda E, ss=s: E.tensor_copy(
                    r1fm[:, :, 0:1].rearrange("p a b -> p (a b)"),
                    halor[:, :, 2 * ss:2 * ss + 1]
                    .rearrange("p a b -> p (a b)")),
                  ("halor",), (("r1", "hl"),), tag="r1h")
                V(lambda E, ss=s: E.tensor_copy(
                    r1fm[:, :, BLK + 1:BLK + 2]
                    .rearrange("p a b -> p (a b)"),
                    halor[:, :, 2 * ss + 1:2 * ss + 2]
                    .rearrange("p a b -> p (a b)")),
                  ("halor",), (("r1", "hr"),), tag="r1h")
                r1_reads = tuple(("r1", dc, t) for dc in range(DC)
                                 for t in range(BT)) + \
                    (("r1", "hl"), ("r1", "hr"))

                rot = 0
                for hcg in range(16):
                    wsl = hcg % 2
                    wsc = wshs[wsl]
                    DMA("G", wsc[:].rearrange("p (g c) -> p g c", c=512),
                        w1q_s[hcg * 2:(hcg + 1) * 2]
                        .rearrange("a h p c -> p (a h) c"),
                        (("wsh", wsl),) + tuple(("w1qs", hcg * 2 + i, hf)
                                                for i in range(2)
                                                for hf in range(2)),
                        (("wsh", wsl),), tag="w1ld")
                    for hcl in range(2):
                        hc = hcg * 2 + hcl
                        banks = [(rot + i) % 6 for i in range(3)]
                        rot += 3
                        for gi in range(3):
                            c0 = gi * 342
                            cw = 342
                            for dc in range(DC):
                                T(lambda E, d=dc, h=hcl, bb=banks[gi], cc=c0,
                                  ww=cw, wc=wsc: E.matmul(
                                      pb[bb][:, 0:ww],
                                      wc[:, (h * 8 + d) * 128:
                                         (h * 8 + d + 1) * 128],
                                      r1fm[:, d, cc:cc + ww],
                                      start=(d == 0), stop=(d == DC - 1)),
                                  (("wsh", wsl),) + r1_reads,
                                  ((PSK[banks[gi]], "f1"),), accum=True,
                                  tag="f1")
                        for gi in range(3):
                            c0 = gi * 342
                            V(lambda E, h=hc, bb=banks[gi], cc=c0:
                              E.tensor_copy(hbuf[:, h, cc:cc + 342],
                                            pb[bb][:, 0:342]),
                              ((PSK[banks[gi]], "f1"),), (("hb", hc),),
                              tag="f1ev")
                        ut, utk = utmps[hc % 2]
                        V(lambda E, h=hc, u=ut: E.tensor_scalar(
                            u[:, 0:BLK], hbuf[:, h, 0:BLK], cw0[:, h:h + 1],
                            None, ALU.mult), (("hb", hc), ("cw", 0)),
                          utk, tag="cv")
                        V(lambda E, h=hc, u=ut: E.scalar_tensor_tensor(
                            u[:, 0:BLK], hbuf[:, h, 1:BLK + 1],
                            cw1[:, h:h + 1], u[:, 0:BLK], ALU.mult, ALU.add),
                          (("hb", hc), ("cw", 1)) + utk, utk, tag="cv")
                        V(lambda E, h=hc, u=ut: E.scalar_tensor_tensor(
                            u[:, 0:BLK], hbuf[:, h, 2:BLK + 2],
                            cw2[:, h:h + 1], u[:, 0:BLK], ALU.mult, ALU.add),
                          (("hb", hc), ("cw", 2)) + utk, utk, tag="cv")
                        S(lambda E, h=hc, u=ut: E.activation(
                            hbuf[:, h, 0:BLK], u[:, 0:BLK], AF.Silu,
                            bias=cbv[:, h:h + 1], scale=1.0),
                          utk + ("cbv",), (("hb", hc),), tag="silu")
                        if hc == 1:
                            V(lambda E: E.tensor_tensor(
                                amaxp[:], hbuf[:, 0, 0:BLK],
                                hbuf[:, 1, 0:BLK], ALU.max),
                              (("hb", 0), ("hb", 1)), ("amx",), tag="am")
                            V(lambda E: E.tensor_tensor(
                                aminp[:], hbuf[:, 0, 0:BLK],
                                hbuf[:, 1, 0:BLK], ALU.min),
                              (("hb", 0), ("hb", 1)), ("amn",), tag="am")
                        elif hc >= 2:
                            V(lambda E, h=hc: E.tensor_tensor(
                                amaxp[:], amaxp[:], hbuf[:, h, 0:BLK],
                                ALU.max),
                              ("amx", ("hb", hc)), ("amx",), tag="am")
                            V(lambda E, h=hc: E.tensor_tensor(
                                aminp[:], aminp[:], hbuf[:, h, 0:BLK],
                                ALU.min),
                              ("amn", ("hb", hc)), ("amn",), tag="am")

                V(lambda E: E.tensor_scalar(aminp[:], aminp[:], -1.0, None,
                                            ALU.mult), ("amn",), ("amn",),
                  tag="am")
                V(lambda E: E.tensor_tensor(amaxp[:], amaxp[:], aminp[:],
                                            ALU.max), ("amx", "amn"),
                  ("amx",), tag="am")
                for t in range(BT):
                    bb = 6 + (t % 2)
                    T(lambda E, tt=t, b2=bb: E.matmul(
                        pb[b2][:, 0:128], amaxp[:, tt * 128:(tt + 1) * 128],
                        identb[:], start=True, stop=True),
                      ("amx", "idb"), (PSK[bb],), tag="amt")
                    V(lambda E, tt=t, b2=bb: E.tensor_reduce(
                        s2t[:, tt:tt + 1], pb[b2][:, 0:128], AX, ALU.max),
                      (PSK[bb],), (("s2", t),), tag="amr")
                s2_reads = tuple(("s2", t) for t in range(BT))
                V(lambda E: E.tensor_scalar(s2t[:], s2t[:], 1e-5, None,
                                            ALU.max), s2_reads, ("s2c",))
                V(lambda E: E.reciprocal(tiny[:, 48:48 + BT], s2t[:]),
                  ("s2c",), (("fti", 48),))
                V(lambda E: E.tensor_scalar(s2t[:], tiny[:, 48:48 + BT],
                                            127.0, None, ALU.mult),
                  (("fti", 48),), ("s2v",))
                V(lambda E: E.tensor_copy(s2b16[:], s2t[:]), ("s2v",),
                  ("s2b",))
                V(lambda E: E.tensor_copy(tiny[:, 56:56 + BT], s2b16[:]),
                  ("s2b",), (("fti", 56),))
                V(lambda E: E.reciprocal(s2i[:], tiny[:, 56:56 + BT]),
                  (("fti", 56),), ("s2iv",))
                V(lambda E: E.tensor_scalar(s2i[:], s2i[:], mglob[:, 1:2],
                                            None, ALU.mult),
                  ("s2iv", "mglob"), ("s2iv",))
                for grp in range(2):
                    for k in range(4):
                        T(lambda E, tt=grp * 4 + k, kk=k: E.matmul(
                            pb[6][0:1, kk * 128:(kk + 1) * 128],
                            s2b16[:, tt:tt + 1], identb[:],
                            start=(kk == 0), stop=(kk == 3)),
                          ("s2b", "idb"), (PSK[6],), accum=True, tag="s2r")
                    V(lambda E, g=grp: E.tensor_copy(
                        tmprow[0:1, g * 512:(g + 1) * 512],
                        pb[6][0:1, 0:512]),
                      (PSK[6],), ("s2row",))
                for half in range(2):
                    T(lambda E, m=half: E.matmul(
                        pb[7][:, 0:512], onesf[:],
                        tmprow[0:1, m * 512:(m + 1) * 512], start=True,
                        stop=True),
                      ("s2row", "ones1f"), (PSK[7],), tag="s2bm")
                    V(lambda E, m=half: E.tensor_copy(
                        s2bc[:, m * 512:(m + 1) * 512], pb[7][:, 0:512]),
                      (PSK[7],), (("s2bc", half),), tag="s2be")

                if dbg2 is not None and agi == 1 and s == 0:
                    dump(s2t[:, 0:BT], ("s2v",))
                    dump(s2i[:, 0:BT], ("s2iv",))
                    i = dump_n[0]
                    dump_n[0] += 1
                    DMA("G", dbg2[i, :, 0:BLK], s2bc[:],
                        (("s2bc", 0), ("s2bc", 1), ("dbg2", i)),
                        (("dbg2", i),), tag="dumps2bc")
                    i = dump_n[0]
                    dump_n[0] += 1
                    DMA("G", dbg2[i, :, 0:BLK], amaxp[:, 0:BLK],
                        ("amx", ("dbg2", i)), (("dbg2", i),), tag="dumpam")
                for th in range(2):
                    for hcg in range(16):
                        wsl = hcg % 2
                        wsc = wshs[wsl]
                        DMA("G", wsc[:].rearrange("p (g c) -> p g c", c=D),
                            w2q_s[hcg * 256:(hcg + 1) * 256, :]
                            .rearrange("(a p) c -> p a c", p=128),
                            (("wsh", wsl),) + tuple(("w2qs", pp, k2, dd)
                                                    for pp in range(4)
                                                    for k2 in range(8)
                                                    for dd in range(DC)
                                                    if (pp * 8 + k2) // 2
                                                    == hcg),
                            (("wsh", wsl),), tag="w2ld")
                        for hcl in range(2):
                            hc = hcg * 2 + hcl
                            rsl = hc % 2
                            rsc = r2ss[rsl]
                            V(lambda E, h=hc, tt=th: E.tensor_tensor(
                                fi32[:, 0:512],
                                hbuf[:, h, tt * 512:(tt + 1) * 512],
                                s2bc[:, tt * 512:(tt + 1) * 512], ALU.mult),
                              (("hb", hc), ("s2bc", th)), ("f32q",),
                              tag="q2")
                            V(lambda E, rc=rsc: E.tensor_copy(
                                rc[:], fi32[:, 0:512]),
                              ("f32q",), (("r2s", rsl),), tag="q2c")
                            for tt in range(4):
                                for dmh in range(2):
                                    b = tt * 2 + dmh
                                    T(lambda E, h=hcl, t4=tt, m=dmh, bb=b,
                                      fr=(hc == 0), la=(hc == HC - 1),
                                      rc=rsc, wc=wsc:
                                      E.matmul(
                                          pb[bb][:],
                                          rc[:, t4 * 128:(t4 + 1) * 128],
                                          wc[:, h * 1024 + m * 512:
                                             h * 1024 + (m + 1) * 512],
                                          start=fr, stop=la),
                                      (("r2s", rsl), ("wsh", wsl)), (PSK[b],),
                                      accum=True, tag="f2")
                    for tt in range(4):
                        rt = s * BT + th * 4 + tt
                        for dmh in range(2):
                            b = tt * 2 + dmh
                            V(lambda E, bb=b, r=rt, m=dmh,
                              sc=th * 4 + tt: E.scalar_tensor_tensor(
                                  resid[:, r, m * 512:(m + 1) * 512],
                                  pb[bb][:], s2i[:, sc:sc + 1],
                                  resid[:, r, m * 512:(m + 1) * 512],
                                  ALU.mult, ALU.add),
                              (PSK[b], "s2iv", ("resid", rt)),
                              (("resid", rt),), tag="f2ev")
        phf.close()
        sch.barrier()

    ffn_sublayer(0)
    if debug >= 2:
        for t in range(2 * BT):
            DMA("Y", dbg[1, t * 128:(t + 1) * 128, :], resid[:, t, :],
                (("resid", t),), (("dbg", 1, t),), tag="dbg")
        sch.barrier()

    ph2 = ExitStack()
    _r = lambda n, s, d=F32: ph2.enter_context(nc.sbuf_tensor(n, s, d))
    phikT = _r("phikT", [128, BT, D], BF16)
    vtm = _r("vtm", [128, BT, D], BF16)
    phiqF = _r("phiqF", [128, DC, BLK], BF16)
    pkF = _r("pkF", [128, DC, 128], BF16)
    xnfm = _r("xnfm", [128, DC, 512], BF16)
    xnt = _r("xnt", [128, D], BF16)
    amk = _r("amk", [128, 2048], BF16)
    otm = _r("otm", [128, 4, D], BF16)
    ofm = _r("ofm", [128, DC, 128], BF16)
    kvst = _r("kvst", [128, 8 * 65], BF16)
    kvpref = _r("kvpref", [128, 2, 8 * 65], F32)
    stbuf = _r("stbuf", [128, 8 * 65], F32)
    wstr = _r("wstr", [128, D], BF16)
    wob = _r("wob", [128, DC, D], BF16)
    bqfm = _r("bqfm", [128, DC], F32)
    bkbc = _r("bkbc", [128, D], F32)
    bvbc = _r("bvbc", [128, D], F32)
    bobc = _r("bobc", [128, D], F32)
    putm = _r("putm", [128, 512], F32)
    phm = _r("phm", [128, 512], BF16)
    phe = _r("phe", [128, 512], BF16)
    r16 = _r("r16", [128, 16], F32)
    if True:

        DMA("Y", bqfm[:], bq_.rearrange("(c p) -> p c", p=128), (), ("bqf",))
        DMA("Y", bkbc[:], bk_[None, :].partition_broadcast(128), (), ("bkb",))
        DMA("Y", bvbc[:], bvb_[None, :].partition_broadcast(128), (), ("bvb",))
        DMA("Y", bobc[:], bo_[None, :].partition_broadcast(128), (), ("bob",))
        for hdc in range(DC):
            DMA("G", wob[:, hdc, :], wo_[hdc * 128:(hdc + 1) * 128, :],
                (("wob", hdc),), (("wob", hdc),), tag="wold")
        ln_stats(lambda j: resid[:, j, :], lambda j: ("resid", j), NB, "bst")


    def state_bank_open(bb, tag):
        T(lambda E, b2=bb: E.matmul(pb[b2][:, 0:260], zrow[0:1, 0:128],
                                    zrow[0:1, 0:260], start=True, stop=False),
          ("zrow",), (PSK[bb],), accum=True, tag=tag + "o")

    def state_bank_close(bb, tag):
        T(lambda E, b2=bb: E.matmul(pb[b2][:, 0:260], zrow[0:1, 0:128],
                                    zrow[0:1, 0:260], start=False, stop=True,
                                    skip_group_check=True),
          ("zrow",), (PSK[bb],), accum=True, tag=tag + "c")

        def xn_chunk(s, ach):
            for k in range(4):
                t = ach * 4 + k
                rt = s * BT + t
                V(lambda E, r=rt: E.tensor_scalar(
                    xnt[:], resid[:, r, :], st_r[:, r:r + 1],
                    stb2[:, r:r + 1], ALU.mult, ALU.add),
                  (("resid", rt), ("st_r", 0), ("stb2", 0)), ("xnt",), tag="bxn")
                tp8(lambda d: xnt[:, d * 128:(d + 1) * 128], ("xnt",),
                    (lambda d, kk=k: xnfm[:, d, kk * 128:(kk + 1) * 128]),
                    (lambda dc, kk=k: ("xnf", dc, kk)), BF16, "bx")

        xn_reads = tuple(("xnf", dc, k) for dc in range(DC) for k in range(4))

        def phi_evac(bb, bkey, dst_ap, dkey, bias_ap, bias_key):
            V(lambda E, a=bias_ap: E.tensor_tensor(putm[:], pb[bb][:], a,
                                                   ALU.add),
              (bkey, bias_key), ("putm",), tag="phb")
            V(lambda E: E.tensor_scalar(phm[:], putm[:], 0.0, None,
                                        ALU.min), ("putm",), ("trm",),
              tag="phm")
            S(lambda E: E.activation(phe[:], phm[:],
                                     AF.Exp, bias=0.0, scale=1.0),
              ("trm",), ("tre",), tag="phe")
            V(lambda E, d=dst_ap: E.scalar_tensor_tensor(
                d, putm[:], 0.0, phe[:], ALU.max, ALU.add),
              ("putm", "tre"), (dkey,), tag="phx")

        for s in range(2):
            for ach in range(2):
                xn_chunk(s, ach)
                for wi, (wap, dstbuf, isphi, bias_ap, bias_key, wkey) in \
                        enumerate(((wk_, phikT, True, bkbc, "bkb", "wK"),
                                   (wv_, vtm, False, bvbc, "bvb", "wV"))):
                    for dmc in range(DC):
                        DMA("G", wstr[:], wap[dmc * 128:(dmc + 1) * 128, :],
                            ((wkey,),) if False else ("wstr",), ("wstr",),
                            tag="bwld")
                        for k in range(4):
                            for hdh in range(2):
                                b = k * 2 + hdh
                                T(lambda E, d=dmc, k2=k, m=hdh, bb=b:
                                  E.matmul(
                                      pb[bb][:],
                                      xnfm[:, d, k2 * 128:(k2 + 1) * 128],
                                      wstr[:, m * 512:(m + 1) * 512],
                                      start=(d == 0), stop=(d == DC - 1)),
                                  (("xnf", dmc, k), "wstr"),
                                  (PSK[b],), accum=True,
                                  tag="bpj")
                    for k in range(4):
                        t = ach * 4 + k
                        for hdh in range(2):
                            b = k * 2 + hdh
                            if isphi:
                                phi_evac(
                                    b, PSK[b],
                                    dstbuf[:, t, hdh * 512:(hdh + 1) * 512],
                                    ("pk", t, hdh),
                                    bias_ap[:, hdh * 512:(hdh + 1) * 512],
                                    bias_key)
                            else:
                                V(lambda E, bb=b, t2=t, m=hdh:
                                  E.tensor_tensor(
                                      vtm[:, t2, m * 512:(m + 1) * 512],
                                      pb[bb][:],
                                      bvbc[:, m * 512:(m + 1) * 512],
                                      ALU.add),
                                  (PSK[b], "bvb"),
                                  (("vt", t, hdh),), tag="bpv")
            state_bank_open(6, "s1")
            state_bank_open(7, "s1")
            for n in range(BT):
                for h in range(NH):
                    bb = 6 + h // 8
                    p0 = (h % 2) * 64
                    c0 = ((h % 8) // 2) * 65
                    hk = (h * 64) // 512
                    T(lambda E, hh=h, nn=n, b2=bb, pp=p0, cc=c0: E.matmul(
                        pb[b2][pp:pp + 64, cc:cc + 64],
                        phikT[:, nn, hh * 64:(hh + 1) * 64],
                        vtm[:, nn, hh * 64:(hh + 1) * 64],
                        start=False, stop=False, skip_group_check=True),
                      (("pk", n, hk), ("vt", n, hk)),
                      (PSK[bb],), accum=True, tag="s1")
                    T(lambda E, hh=h, nn=n, b2=bb, pp=p0, cc=c0: E.matmul(
                        pb[b2][pp:pp + 64, cc + 64:cc + 65],
                        phikT[:, nn, hh * 64:(hh + 1) * 64], onesb[:],
                        start=False, stop=False, skip_group_check=True),
                      (("pk", n, hk), "onesb"),
                      (PSK[bb],), accum=True, tag="s1k")
            for half in range(2):
                bb = 6 + half
                V(lambda E, b2=bb, hh=half: E.tensor_copy(
                    stbuf[:, hh * 260:(hh + 1) * 260], pb[b2][:, 0:260]),
                  (PSK[bb],), (("stb", s, half),), tag="s1ev")
            state_bank_close(6, "s1")
            state_bank_close(7, "s1")
            DMA("Y", ag_si[s], stbuf[:], (("stb", s, 0), ("stb", s, 1)),
                (("agsi", s),), tag="stpub")
            for t in range(BT):
                DMA("Y", spill[s, 0, t], phikT[:, t, :],
                    (("pk", t, 0), ("pk", t, 1)), (("spl", s, 0, t),),
                    tag="spl")
                DMA("Y", spill[s, 1, t], vtm[:, t, :],
                    (("vt", t, 0), ("vt", t, 1)), (("spl", s, 1, t),),
                    tag="spl")

        sch.emit("G", lambda E: E.collective_compute(
            "AllGather", ALU.bypass, replica_groups=[list(range(NCORES))],
            ins=[ag_si[:]], outs=[ag_so[:]]),
            (("agsi", 0), ("agsi", 1)), ("agso",), is_dma=True, is_coll=True,
            tag="AGs")

        for s in range(2):
            V(lambda E, ss=s: E.memset(kvpref[:, ss, :], 0.0), (),
              (("kvp", s),))
        for b in range(NB):
            DMA("Y", stbuf[:], ag_so[b], ("agso", ("stb", "p")),
                (("stb", "p"),), tag="pfl")
            for s in range(2):
                V(lambda E, bb=b, ss=s: E.scalar_tensor_tensor(
                    kvpref[:, ss, :], stbuf[:],
                    stwb[:, 2 * bb + ss:2 * bb + ss + 1], kvpref[:, ss, :],
                    ALU.mult, ALU.add),
                  (("stb", "p"), "stwb", ("kvp", s)), (("kvp", s),),
                  tag="pfw")

        for s in range(2):
            for t in range(BT):
                DMA("Y", phikT[:, t, :], spill[s, 0, t],
                    (("spl", s, 0, t), ("pk", t, 0), ("pk", t, 1)),
                    (("pk", t, 0), ("pk", t, 1)), tag="rld")
                DMA("Y", vtm[:, t, :], spill[s, 1, t],
                    (("spl", s, 1, t), ("vt", t, 0), ("vt", t, 1)),
                    (("vt", t, 0), ("vt", t, 1)), tag="rld")
            for ach in range(2):
                xn_chunk(s, ach)
                for dmc in range(DC):
                    DMA("G", wstr[:], wq_[dmc * 128:(dmc + 1) * 128, :],
                        ("wstr",), ("wstr",), tag="bwq")
                    for hdc in range(DC):
                        T(lambda E, d=dmc, hh=hdc: E.matmul(
                            pb[hh][:], wstr[:, hh * 128:(hh + 1) * 128],
                            xnfm[:, d, :], start=(d == 0),
                            stop=(d == DC - 1)),
                          (("wstr"), ) + xn_reads, (PSK[hdc],),
                          accum=True, tag="bqj")
                for hdc in range(DC):
                    V(lambda E, hh=hdc: E.tensor_scalar(
                        putm[:], pb[hh][:], bqfm[:, hh:hh + 1], None,
                        ALU.add),
                      (PSK[hdc], "bqf"), ("putm",), tag="bqb")
                    V(lambda E: E.tensor_scalar(phm[:], putm[:],
                                                0.0, None, ALU.min),
                      ("putm",), ("trm",), tag="bqm")
                    S(lambda E: E.activation(phe[:],
                                             phm[:], AF.Exp,
                                             bias=0.0, scale=1.0),
                      ("trm",), ("tre",), tag="bqe")
                    V(lambda E, hh=hdc, aa=ach: E.scalar_tensor_tensor(
                        phiqF[:, hh, aa * 512:(aa + 1) * 512], putm[:], 0.0,
                        phe[:], ALU.max, ALU.add),
                      ("putm", "tre"), (("pq", hdc, ach),), tag="bqx")
            V(lambda E, ss=s: E.tensor_copy(kvst[:], kvpref[:, ss, :]),
              (("kvp", s),), ("kvst",))
            state_bank_open(6, "s2")
            state_bank_open(7, "s2")
            for n in range(BT):
                tp8(lambda d: phikT[:, n, d * 128:(d + 1) * 128],
                    (("pk", n, 0), ("pk", n, 1)),
                    (lambda d: pkF[:, d, :]),
                    (lambda dc: ("pkf", dc)), BF16, "pk2")
                for h in range(NH):
                    bb = h // 4
                    cc = (h % 4) * 128
                    p0 = (h % 2) * 64
                    dq = h // 2
                    T(lambda E, b2=bb, c2=cc, pp=p0, d=dq, nn=n: E.matmul(
                        pb[b2][:, c2:c2 + 128], pkF[pp:pp + 64, d, :],
                        phiqF[pp:pp + 64, d, nn * 128:(nn + 1) * 128],
                        start=True, stop=True),
                      (("pkf", dq), ("pq", dq, n // 4)),
                      (PSK[bb],), accum=True, tag="A")
                for bb in range(4):
                    V(lambda E, b2=bb: E.tensor_tensor(
                        amk[:, b2 * 512:(b2 + 1) * 512], pb[b2][:],
                        maskW[:], ALU.mult),
                      (PSK[bb], ("mW", 0), ("mW", 1), ("mW", 2),
                       ("mW", 3)),
                      (("amk", bb),), tag="Am")
                NB_OF = [(0, 0, 7), (1, 7, 7), (2, 14, 2)]
                for bb, h0, cnt in NB_OF:
                    for i in range(cnt):
                        h = h0 + i
                        cc = i * 65
                        p0 = (h % 2) * 64
                        dq = h // 2
                        hk = (h * 64) // 512
                        T(lambda E, b2=bb, c2=cc, hh=h, nn=n, i2z=i: E.matmul(
                            pb[b2][:, c2:c2 + 64],
                            amk[:, ((hh // 4) * 512 + (hh % 4) * 128):
                                ((hh // 4) * 512 + (hh % 4) * 128 + 128)],
                            vtm[:, nn, hh * 64:(hh + 1) * 64],
                            start=(i2z == 0), stop=False),
                          (("amk", h // 4), ("vt", n, hk)),
                          (PSK[bb],), accum=True, tag="nmv")
                        T(lambda E, b2=bb, c2=cc, hh=h, nn=n: E.matmul(
                            pb[b2][:, c2 + 64:c2 + 65],
                            amk[:, ((hh // 4) * 512 + (hh % 4) * 128):
                                ((hh // 4) * 512 + (hh % 4) * 128 + 128)],
                            onesb[:], start=False, stop=False),
                          (("amk", h // 4), "onesb"),
                          (PSK[bb],), accum=True, tag="nmd")
                        T(lambda E, b2=bb, c2=cc, hh=h, pp=p0, d=dq, nn=n:
                          E.matmul(
                              pb[b2][:, c2:c2 + 65],
                              phiqF[pp:pp + 64, d,
                                    nn * 128:(nn + 1) * 128],
                              kvst[pp:pp + 64, (hh // 2) * 65:
                                   (hh // 2) * 65 + 65],
                              start=False, stop=(hh == h0 + cnt - 1)),
                          (("pq", dq, n // 4), "kvst"),
                          (PSK[bb],), accum=True, tag="nmp")
                for bb, h0, cnt in NB_OF:
                    V(lambda E, b2=bb, hh=h0, c2=cnt: E.tensor_scalar(
                        r16[:, hh:hh + c2],
                        pb[b2][:, 0:c2 * 65].rearrange(
                            "p (a b) -> p a b", b=65)[:, :, 64], EPS_LA,
                        None, ALU.add),
                      (PSK[bb],), (("r16", bb),), tag="den")
                V(lambda E: E.reciprocal(r16[:], r16[:]),
                  (("r16", 0), ("r16", 1), ("r16", 2)), ("r16r",), tag="rcp")
                for bb, h0, cnt in NB_OF:
                    for i in range(cnt):
                        h = h0 + i
                        V(lambda E, b2=bb, i2=i, hh=h, nn=n: E.tensor_scalar(
                            otm[:, nn % 4, hh * 64:(hh + 1) * 64],
                            pb[b2][:, i2 * 65:i2 * 65 + 64],
                            r16[:, hh:hh + 1], None, ALU.mult),
                          (PSK[bb], "r16r"),
                          (("otm", n % 4, h),), tag="oev")
                for h in range(NH):
                    bb = 6 + h // 8
                    p0 = (h % 2) * 64
                    c0 = ((h % 8) // 2) * 65
                    hk = (h * 64) // 512
                    T(lambda E, hh=h, nn=n, b2=bb, pp=p0, cc=c0: E.matmul(
                        pb[b2][pp:pp + 64, cc:cc + 64],
                        phikT[:, nn, hh * 64:(hh + 1) * 64],
                        vtm[:, nn, hh * 64:(hh + 1) * 64],
                        start=False, stop=False, skip_group_check=True),
                      (("pk", n, hk), ("vt", n, hk)),
                      (PSK[bb],), accum=True, tag="s2")
                    T(lambda E, hh=h, nn=n, b2=bb, pp=p0, cc=c0: E.matmul(
                        pb[b2][pp:pp + 64, cc + 64:cc + 65],
                        phikT[:, nn, hh * 64:(hh + 1) * 64], onesb[:],
                        start=False, stop=False, skip_group_check=True),
                      (("pk", n, hk), "onesb"),
                      (PSK[bb],), accum=True, tag="s2k")
                for half in range(2):
                    bb = 6 + half
                    V(lambda E, b2=bb, hh=half, ss=s: E.tensor_tensor(
                        kvst[:, hh * 260:(hh + 1) * 260], pb[b2][:, 0:260],
                        kvpref[:, ss, hh * 260:(hh + 1) * 260], ALU.add),
                      (PSK[bb], ("kvp", s)), ("kvst",), tag="kvu")
                if n == BT - 1:
                    state_bank_close(6, "s2")
                    state_bank_close(7, "s2")
                if n % 4 == 3:
                    ach = n // 4
                    for k in range(4):
                        tp8(lambda d, kk=k: otm[:, kk, d * 128:(d + 1) * 128],
                            tuple(("otm", k, h) for h in range(NH)),
                            (lambda d: ofm[:, d, :]),
                            (lambda dc: ("ofm", dc)), BF16, "otp")
                        for dmh in range(2):
                            b = 4 + dmh
                            for hdc in range(DC):
                                T(lambda E, d=hdc, m=dmh, bb=b: E.matmul(
                                    pb[bb][:], ofm[:, d, :],
                                    wob[:, d, m * 512:(m + 1) * 512],
                                    start=(d == 0), stop=(d == DC - 1)),
                                  (("ofm", hdc), ("wob", hdc)),
                                  (PSK[b],), accum=True, tag="woj")
                            rt = s * BT + ach * 4 + k
                            V(lambda E, bb=b, m=dmh: E.tensor_tensor(
                                putm[:], pb[bb][:],
                                bobc[:, m * 512:(m + 1) * 512], ALU.add),
                              (PSK[b], "bob"), ("putm",), tag="wob")
                            V(lambda E, r=rt, m=dmh: E.tensor_tensor(
                                resid[:, r, m * 512:(m + 1) * 512],
                                resid[:, r, m * 512:(m + 1) * 512],
                                putm[:], ALU.add),
                              ("putm", ("resid", rt)), (("resid", rt),),
                              tag="wor")
    ph2.close()
    sch.barrier()
    if debug >= 3:
        for t in range(2 * BT):
            DMA("Y", dbg[2, t * 128:(t + 1) * 128, :], resid[:, t, :],
                (("resid", t),), (("dbg", 2, t),), tag="dbg")
        sch.barrier()

    ffn_sublayer(1)

    for t in range(2 * BT):
        DMA("Y", out_y[t * 128:(t + 1) * 128, :], resid[:, t, :],
            (("resid", t),), (("out", t),), tag="out")
    sch.emit("Y", lambda E: E.nop(),
             tuple(("out", t) for t in range(2 * BT)), (), tag="fin")

    with es, nc.allow_non_contiguous_dma(
            reason="tiny one-time vector loads (bias/conv rearranges)"), \
            nc.Block() as block:
        sch.replay(block, sems)
    return nc


def _stage_inputs(inputs):
    x = np.ascontiguousarray(
        np.asarray(inputs["L0_attn_a_x"], dtype=np.float32).reshape(T_TOT, D))
    xt = x.reshape(NB, BT, 128, D)

    common = dict(
        w1=np.ascontiguousarray(np.asarray(inputs["L0_ffn_w1"], np.float32)),
        w2=np.ascontiguousarray(np.asarray(inputs["L0_ffn_w2"], np.float32)),
        conv_w=np.ascontiguousarray(
            np.asarray(inputs["L0_ffn_conv_w"], np.float32).reshape(HID, 3)),
        conv_b=np.ascontiguousarray(
            np.asarray(inputs["L0_ffn_conv_b"], np.float32)),
        wq=np.ascontiguousarray(np.asarray(inputs["L0_attn_b_Wq"], np.float32)),
        wk=np.ascontiguousarray(np.asarray(inputs["L0_attn_b_Wk"], np.float32)),
        wv=np.ascontiguousarray(np.asarray(inputs["L0_attn_b_Wv"], np.float32)),
        wo=np.ascontiguousarray(np.asarray(inputs["L0_attn_b_Wo"], np.float32)),
        bq=np.ascontiguousarray(np.asarray(inputs["L0_attn_b_bq"], np.float32)),
        bk=np.ascontiguousarray(np.asarray(inputs["L0_attn_b_bk"], np.float32)),
        bvb=np.ascontiguousarray(np.asarray(inputs["L0_attn_b_bv"], np.float32)),
        bo=np.ascontiguousarray(np.asarray(inputs["L0_attn_b_bo"], np.float32)),
        bv_q=np.ascontiguousarray(np.asarray(inputs["L0_attn_a_bv_q"], np.float32)),
        bv_k=np.ascontiguousarray(np.asarray(inputs["L0_attn_a_bv_k"], np.float32)),
        bv_v=np.ascontiguousarray(np.asarray(inputs["L0_attn_a_bv_v"], np.float32)),
    )

    in_maps = []
    for c in range(NCORES):
        bl, bh = c, 15 - c
        visits = [(0, kb) for kb in range(bl)] + \
                 [(1, kb) for kb in range(bh)] + [(0, bl), (1, bh)]
        assert len(visits) == NVIS
        kv = np.stack([xt[kb] for (_, kb) in visits])
        ws = np.zeros(NVIS * 2, np.float32)
        for v, (slot, _) in enumerate(visits):
            ws[2 * v + slot] = 1.0
        mb = np.stack([xt[bl], xt[bh]])
        hs = np.zeros((4 * NCORES, 4), np.float32)
        halo_toks = [bl * BLK - 1, (bl + 1) * BLK, bh * BLK - 1,
                     (bh + 1) * BLK]
        for j, tk in enumerate(halo_toks):
            if tk < 0 or tk >= T_TOT:
                continue
            b = tk // BLK
            r = b if b <= 7 else 15 - b
            so = 0 if b == r else 1
            first = (tk % BLK == 0)
            row = 4 * r + 2 * so + (0 if first else 1)
            hs[row, j] = 1.0
        sw = np.zeros(NB * 2, np.float32)
        for bidx in range(NB):
            blkid = (bidx // 2) if bidx % 2 == 0 else 15 - (bidx // 2)
            if blkid < bl:
                sw[2 * bidx + 0] = 1.0
            if blkid < bh:
                sw[2 * bidx + 1] = 1.0
        in_maps.append(dict(common, kvis=np.ascontiguousarray(kv),
                            myblk=np.ascontiguousarray(mb), wsel=ws,
                            halosel=hs, stw=sw))
    return in_maps


_NC_CACHE = {}


def kernel(**inputs):
    debug = int(inputs.pop("_debug", 0))
    if debug not in _NC_CACHE:
        _NC_CACHE[debug] = build_program(debug)
    nc = _NC_CACHE[debug]
    in_maps = _stage_inputs(inputs)
    res = run_bass_kernel_spmd(nc, in_maps, core_ids=list(range(NCORES)))
    out = np.zeros((T_TOT, D), np.float32)
    for c in range(NCORES):
        y = np.asarray(res.results[c]["y"]).reshape(2 * BLK, D)
        out[c * BLK:(c + 1) * BLK] = y[0:BLK]
        out[(15 - c) * BLK:(16 - c) * BLK] = y[BLK:2 * BLK]
    if debug:
        nd = min(debug, 3)
        dbgs = [np.asarray(res.results[c]["dbg"]).reshape(nd, 2 * BLK, D)
                for c in range(NCORES)]
        full = np.zeros((nd, T_TOT, D), np.float32)
        for c in range(NCORES):
            full[:, c * BLK:(c + 1) * BLK] = dbgs[c][:, 0:BLK]
            full[:, (15 - c) * BLK:(16 - c) * BLK] = dbgs[c][:, BLK:2 * BLK]
        kernel._dbg = full
        if debug >= 4:
            kernel._dbg2 = [np.asarray(res.results[c]["dbg2"])
                            .reshape(16, 128, 8192) for c in range(NCORES)]
    return out.reshape(1, T_TOT, D)


# version 20
# speedup vs baseline: 1.0061x; 1.0061x over previous
import numpy as np
import ml_dtypes
from contextlib import ExitStack

import concourse.bass as bass
import concourse.mybir as mybir
from concourse.bass_utils import run_bass_kernel_spmd

F32 = mybir.dt.float32
BF16 = mybir.dt.bfloat16
FP8 = mybir.dt.float8e4
I32 = mybir.dt.int32
AF = mybir.ActivationFunctionType
ALU = mybir.AluOpType
AX = mybir.AxisListType.X

T_TOT = 16384
D = 1024
DC = 8
NB = 16
BLK = 1024
BT = 8
NCORES = 8
NVIS = 17
HID = 4096
HC = 32
NH = 16
HD = 64
EPS_LN = 1e-5
EPS_LA = 1e-6
INV127 = 1.0 / 127.0
INVD = 1.0 / D


class Sched:
    ENGS = ("T", "V", "S", "G", "Y")

    def __init__(self):
        self.ops = {e: [] for e in self.ENGS}
        self.last_writer = {}
        self.readers = {}
        self.accum_open = {}
        self.clock = {e: {} for e in self.ENGS}
        self.last_dma = {e: -1 for e in self.ENGS}
        self.fence_fns = {}
        self.fence_writes = {}

    def emit(self, eng, fn, reads=(), writes=(), accum=False, is_dma=False,
             is_coll=False, tag=""):
        idx = len(self.ops[eng])
        need = {}

        def add_edge(peng, pidx):
            if peng == eng and pidx == idx:
                return
            pop = self.ops[peng][pidx]
            async_done = pop["is_dma"] or pop["is_coll"]
            if pop["is_dma"] and not pop["is_coll"]:
                pidx = self.last_dma[peng]
                pop = self.ops[peng][pidx]
            if not async_done:
                key = (peng, "c")
            else:
                key = (peng, "Gl" if pop["is_coll"] else "d")
            if self.clock[eng].get(key, -1) >= pidx:
                return
            if need.get(key, -1) < pidx:
                need[key] = pidx

        for k in reads:
            w = self.last_writer.get(k)
            if w is not None:
                add_edge(*w)
        for k in writes:
            w = self.last_writer.get(k)
            if w is not None:
                if not (accum and w[0] == eng and self.accum_open.get(k)):
                    add_edge(*w)
            for r in self.readers.get(k, ()):
                add_edge(*r)
        for (peng, kind), pidx in need.items():
            self.ops[peng][pidx]["inc"] = True
            pc = self.ops[peng][pidx]["clock"]
            for k2, v2 in pc.items():
                if self.clock[eng].get(k2, -1) < v2:
                    self.clock[eng][k2] = v2
            if self.clock[eng].get((peng, kind), -1) < pidx:
                self.clock[eng][(peng, kind)] = pidx
        myclock = dict(self.clock[eng])
        myclock[(eng, "c")] = idx
        self.ops[eng].append(dict(fn=fn, waits=dict(need),
                                  inc=bool(is_dma),
                                  is_dma=is_dma, is_coll=is_coll,
                                  clock=myclock, tag=tag))
        if is_dma and not is_coll:
            self.last_dma[eng] = idx
        for k in reads:
            self.readers.setdefault(k, []).append((eng, idx))
        for k in writes:
            self.readers[k] = []
            self.last_writer[k] = (eng, idx)
            self.accum_open[k] = accum

    def barrier(self):
        key = ("__bar__", sum(len(self.ops[e]) for e in self.ENGS))
        for e in self.ENGS:
            fn = self.fence_fns.get(e, lambda E: E.nop())
            self.emit(e, fn, reads=(("fence", "init"),),
                      writes=((key, e), ("fence", e)) + self.fence_writes.get(e, ()))
            if self.last_dma[e] >= 0:
                op = self.ops[e][-1]
                if op["waits"].get((e, "d"), -1) < self.last_dma[e]:
                    op["waits"][(e, "d")] = self.last_dma[e]
                    self.ops[e][self.last_dma[e]]["inc"] = True
        allk = tuple((key, e2) for e2 in self.ENGS)
        for e in self.ENGS:
            self.emit(e, lambda E: E.nop(), reads=allk, writes=())

    @staticmethod
    def _sem_kind(e, op):
        if op["is_coll"]:
            return "Gl"
        if op["is_dma"]:
            return e + "d"
        return e + "c"

    def replay(self, block, sems):
        for e in self.ENGS:
            run = {}
            for op in self.ops[e]:
                if op["inc"]:
                    kind = self._sem_kind(e, op)
                    run[kind] = run.get(kind, 0) + \
                        (1 if op["is_coll"] else (16 if op["is_dma"] else 1))
                op["cum"] = dict(run)

        def section(eng_obj, e):
            for op in self.ops[e]:
                for (peng, _k), pidx in sorted(op["waits"].items()):
                    pop = self.ops[peng][pidx]
                    kind = self._sem_kind(peng, pop)
                    eng_obj.wait_ge(sems[kind], pop["cum"][kind])
                if op["inc"]:
                    kind = self._sem_kind(e, op)
                    step = 1 if op["is_coll"] else (16 if op["is_dma"] else 1)
                    prev = op["cum"][kind] - step
                    if prev > 0:
                        eng_obj.wait_ge(sems[kind], prev)
                inst = op["fn"](eng_obj)
                if op["inc"]:
                    kind = self._sem_kind(e, op)
                    inst.then_inc(
                        sems[kind],
                        1 if op["is_coll"] else (16 if op["is_dma"] else 1))

        @block.tensor
        def _(tensor):
            section(tensor, "T")

        @block.vector
        def _(vector):
            section(vector, "V")

        @block.scalar
        def _(scalar):
            section(scalar, "S")

        @block.gpsimd
        def _(gpsimd):
            section(gpsimd, "G")

        @block.sync
        def _(sync):
            section(sync, "Y")


def build_program(debug=0):
    nc = bass.Bass("TRN2", target_bir_lowering=False, debug=False)
    sch = Sched()

    def din(name, shape, dt=F32):
        return nc.dram_tensor(name, shape, dt, kind="ExternalInput").ap()

    kvis = din("kvis", [NVIS, BT, 128, D])
    myblk = din("myblk", [2, BT, 128, D])
    wsel = din("wsel", [NVIS * 2])
    halosel = din("halosel", [4 * NCORES, 4])
    stw = din("stw", [NB * 2])
    w1_ = din("w1", [HID, D])
    w2_ = din("w2", [D, HID])
    convw_ = din("conv_w", [HID, 3])
    convb_ = din("conv_b", [HID])
    wq_ = din("wq", [D, D])
    wk_ = din("wk", [D, D])
    wv_ = din("wv", [D, D])
    wo_ = din("wo", [D, D])
    bq_ = din("bq", [D])
    bk_ = din("bk", [D])
    bvb_ = din("bvb", [D])
    bo_ = din("bo", [D])
    bvq_ = din("bv_q", [D])
    bvk_ = din("bv_k", [D])
    bvv_ = din("bv_v", [D])

    out_y = nc.dram_tensor("y", [2 * BLK, D], F32, kind="ExternalOutput").ap()
    dbg = None
    dbg2 = None
    if debug:
        dbg = nc.dram_tensor("dbg", [min(debug, 3), 2 * BLK, D], F32,
                             kind="ExternalOutput").ap()
        if debug >= 4:
            dbg2 = nc.dram_tensor("dbg2", [16, 128, 8192], F32,
                                  kind="ExternalOutput").ap()

    w1q_s = nc.dram_tensor("w1q_s", [HC, 2, 128, 512], BF16).ap()
    w2q_s = nc.dram_tensor("w2q_s", [HID, D], BF16).ap()
    spill = nc.dram_tensor("spill", [2, 2, BT, 128, D], BF16).ap()
    ag_ei = [nc.dram_tensor(f"ag_ei{i}", [4, D], F32).ap() for i in range(2)]
    ag_eo = [nc.dram_tensor(f"ag_eo{i}", [4 * NCORES, D], F32,
                            addr_space="Shared").ap() for i in range(2)]
    ag_si = nc.dram_tensor("ag_si", [2, 128, 8 * 65], F32).ap()
    ag_so = nc.dram_tensor("ag_so", [NB, 128, 8 * 65], F32,
                           addr_space="Shared").ap()

    es = ExitStack()

    def sbt(name, shape, dt=F32):
        return es.enter_context(nc.sbuf_tensor(name, shape, dt))

    resid = sbt("resid", [128, 2 * BT, D])
    st_s = sbt("st_s", [128, NB])
    st_q = sbt("st_q", [128, NB])
    st_m = sbt("st_m", [128, NB])
    st_v = sbt("st_v", [128, NB])
    st_r = sbt("st_r", [128, NB])
    st_b = sbt("st_b", [128, NB])
    stb2 = sbt("stb2", [128, NB])
    ident8 = sbt("ident8", [128, 128], FP8)
    identb = sbt("identb", [128, 128], BF16)
    maskUT = sbt("maskUT", [128, 128], BF16)
    maskW = sbt("maskW", [128, 512], BF16)
    g_fm = sbt("g_fm", [128, DC])
    bvs_bc = sbt("bvs_bc", [128, D], BF16)
    wbuf = sbt("wbuf", [128, NVIS * 2])
    stwb = sbt("stwb", [128, NB * 2])
    mglob = sbt("mglob", [128, 4])
    cw0 = sbt("cw0", [128, HC])
    cw1 = sbt("cw1", [128, HC])
    cw2 = sbt("cw2", [128, HC])
    cbv = sbt("cbv", [128, HC])
    onesb = sbt("onesb", [128, 1], BF16)
    zrow = sbt("zrow", [1, 260], BF16)
    tiny = sbt("tiny", [128, 96])
    trash = sbt("trash", [128, D])
    tmprow = sbt("tmprow", [128, D])
    onesf = sbt("onesf", [1, 128])
    id4f = sbt("id4f", [4, 4])
    hsel = sbt("hsel", [32, 4])
    agbuf = sbt("agbuf", [32, D])
    halox = sbt("halox", [4, D])
    halor = sbt("halor", [128, DC, 4], BF16)
    qa = sbt("qa", [128, NB])
    qs1a = sbt("qs1a", [128, NB])
    qs1b = sbt("qs1b", [128, NB])
    qsi = sbt("qsi", [128, NB])
    s2t = sbt("s2t", [128, BT])
    s2b16 = sbt("s2b16", [128, BT], BF16)
    s2i = sbt("s2i", [128, BT])

    sems = {}
    for e in Sched.ENGS:
        sems[e + "c"] = es.enter_context(nc.semaphore("sm_" + e + "c"))
        sems[e + "d"] = es.enter_context(nc.semaphore("sm_" + e + "d"))
    sems["Gl"] = es.enter_context(nc.semaphore("sm_Gl"))

    pb = [es.enter_context(nc.psum_tensor(f"pb{i}", [128, 512], F32))
          for i in range(8)]
    PSK = [("pb", i) for i in range(8)]
    fence = sbt("fence", [1, 8])
    sch.emit("G", lambda E: E.memset(fence[:], 0.0), (),
             (("fence", "init"),))
    sch.fence_fns = {
        "V": lambda E: E.tensor_copy(fence[0:1, 0:1], fence[0:1, 1:2]),
        "S": lambda E: E.activation(fence[0:1, 2:3], fence[0:1, 3:4],
                                    AF.Copy, bias=0.0, scale=1.0),
        "G": lambda E: E.memset(fence[0:1, 4:5], 0.0),
        "T": lambda E: E.matmul(pb[0][0:1, 0:1], fence[0:1, 5:6],
                                fence[0:1, 6:7], start=True, stop=True),
    }
    sch.fence_writes = {"T": (("pb", 0),)}

    def V(fn, r, w, tag=""):
        sch.emit("V", fn, r, w, tag=tag)

    def S(fn, r, w, tag=""):
        sch.emit("S", fn, r, w, tag=tag)

    def T(fn, r, w, accum=False, tag=""):
        sch.emit("T", fn, r, w, accum=accum, tag=tag)

    def G(fn, r, w, tag=""):
        sch.emit("G", fn, r, w, tag=tag)

    def DMA(eng, out, in_, r, w, tag=""):
        sch.emit(eng, lambda E, o=out, i=in_: E.dma_start(out=o, in_=i),
                 r, w, is_dma=True, tag=tag)

    dump_n = [0]

    def dump(ap, reads):
        if dbg2 is None:
            return
        i = dump_n[0]
        dump_n[0] += 1
        cols = 1
        for sdim in ap.shape[1:]:
            cols *= sdim
        rows = ap.shape[0]
        DMA("Y", dbg2[i, 0:rows, 0:cols], ap, tuple(reads) + (("dbg2", i),),
            (("dbg2", i),), tag="dump")

    def ln_sq_tile(tile_ap, rkeys, j, tag="", base=0):
        S(lambda E, a=tile_ap, o=st_q[:, base + j:base + j + 1]:
          E.activation(trash[:], a, AF.Square, bias=0.0, scale=1.0,
                       accum_out=o),
          tuple(rkeys), (("st_q", base + j), "trash"), tag)

    def ln_sums_tile(tile_ap, rkeys, j, tag="", base=0, sum_on_scalar=False,
                     square=True):
        if sum_on_scalar:
            S(lambda E, a=tile_ap, o=st_s[:, base + j:base + j + 1]:
              E.activation(trash[:], a, AF.Identity, bias=0.0, scale=1.0,
                           accum_out=o),
              tuple(rkeys), (("st_s", base + j), "trash"), tag)
        else:
            V(lambda E, a=tile_ap, o=st_s[:, base + j:base + j + 1]:
              E.tensor_reduce(o, a, AX, ALU.add),
              tuple(rkeys), (("st_s", base + j),), tag)
        if square:
            ln_sq_tile(tile_ap, rkeys, j, tag, base)

    def ln_stats(tiles_ap, rkey, n, tag="", base=0, sum_on_scalar=False):
        for j in range(n):
            ln_sums_tile(tiles_ap(j), (rkey(j),), j, tag, base, sum_on_scalar)
        ln_finish(n, tag, base)

    def ln_finish_mean(n, tag="", base=0):
        cs = slice(base, base + n)
        rk = tuple(("st_s", base + j) for j in range(n))
        km = ("st_m", base); kb = ("st_b", base)
        V(lambda E: E.tensor_scalar(st_m[:, cs], st_s[:, cs], INVD, None,
                                    ALU.mult), rk, (km,), tag)
        V(lambda E: E.tensor_scalar(st_b[:, cs], st_m[:, cs], -1.0, None,
                                    ALU.mult), (km,), (kb,), tag)

    def ln_finish_rstd(n, tag="", base=0):
        cs = slice(base, base + n)
        qk = tuple(("st_q", base + j) for j in range(n))
        km = ("st_m", base); kv = ("st_v", base); kq2 = ("st_q2", base)
        kr = ("st_r", base); kb = ("st_b", base); kb2 = ("stb2", base)
        V(lambda E: E.tensor_tensor(st_v[:, cs], st_m[:, cs], st_m[:, cs],
                                    ALU.mult), (km,), (kv,), tag)
        V(lambda E: E.scalar_tensor_tensor(st_v[:, cs], st_q[:, cs], INVD,
                                           st_v[:, cs], ALU.mult,
                                           ALU.subtract),
          qk + (kv,), (kv,), tag)
        V(lambda E: E.tensor_scalar(st_v[:, cs], st_v[:, cs], EPS_LN, None,
                                    ALU.add), (kv,), (kv,), tag)
        S(lambda E: E.activation(st_q[:, cs], st_v[:, cs], AF.Sqrt,
                                 bias=0.0, scale=1.0),
          (kv,) + qk, (kq2,), tag)
        V(lambda E: E.reciprocal(st_r[:, cs], st_q[:, cs]),
          (kq2,), (kr,), tag)
        V(lambda E: E.tensor_tensor(st_q[:, cs], st_r[:, cs], st_r[:, cs],
                                    ALU.mult), (kr, kq2), (kq2,), tag)
        V(lambda E: E.tensor_tensor(st_q[:, cs], st_q[:, cs], st_v[:, cs],
                                    ALU.mult), (kq2, kv), (kq2,), tag)
        V(lambda E: E.tensor_scalar(st_q[:, cs], st_q[:, cs], -0.5, 1.5,
                                    ALU.mult, ALU.add), (kq2,), (kq2,), tag)
        V(lambda E: E.tensor_tensor(st_r[:, cs], st_r[:, cs], st_q[:, cs],
                                    ALU.mult), (kr, kq2), (kr,), tag)
        V(lambda E: E.tensor_tensor(stb2[:, cs], st_b[:, cs], st_r[:, cs],
                                    ALU.mult), (kb, kr), (kb2,), tag)

    def ln_finish(n, tag="", base=0):
        ln_finish_mean(n, tag, base)
        ln_finish_rstd(n, tag, base)

    def tp8(src_ap_fn, src_keys, dst_fn, dst_key_fn, dst_dt_evac, tag,
            scale_ap_fn=None, scale_key=None, banks=(4, 5),
            wide_dst_fn=None, wide_key_fn=None):
        for half in range(2):
            b = banks[half]
            idt = "id8" if dst_dt_evac == FP8 else "idb"
            idap = ident8 if dst_dt_evac == FP8 else identb
            for k in range(4):
                dc = half * 4 + k
                T(lambda E, d=dc, bb=b, kk=k, ia=idap: E.matmul(
                    pb[bb][:, kk * 128:(kk + 1) * 128], src_ap_fn(d), ia[:],
                    start=True, stop=True),
                  tuple(src_keys) + (idt,), (PSK[b],),
                  accum=True, tag=tag)
            if wide_dst_fn is not None:
                S(lambda E, h=half, bb=b: E.activation(
                    wide_dst_fn(h), pb[bb][:], AF.Copy, bias=0.0, scale=1.0),
                  (PSK[b],), (wide_key_fn(half),), tag=tag)
                continue
            for k in range(4):
                dc = half * 4 + k
                if scale_ap_fn is not None:
                    S(lambda E, d=dc, bb=b, kk=k: E.activation(
                        dst_fn(d), pb[bb][:, kk * 128:(kk + 1) * 128],
                        AF.Copy, bias=0.0, scale=scale_ap_fn(d)),
                      (PSK[b], scale_key), (dst_key_fn(dc),),
                      tag=tag)
                else:
                    S(lambda E, d=dc, bb=b, kk=k: E.activation(
                        dst_fn(d), pb[bb][:, kk * 128:(kk + 1) * 128],
                        AF.Copy, bias=0.0, scale=1.0),
                      (PSK[b],), (dst_key_fn(dc),), tag=tag)

    G(lambda E: E.memset(id4f[:], 0.0), (), ("id4f",))
    G(lambda E: E.affine_select(out=id4f[:], in_=id4f[:],
                                compare_op=ALU.not_equal, fill=1.0, base=0,
                                pattern=[[-1, 4]], channel_multiplier=1),
      ("id4f",), ("id4f",))
    for name, buf in (("id8", ident8), ("idb", identb)):
        G(lambda E, b=buf: E.memset(b[:], 0.0), (), (name,))
        G(lambda E, b=buf: E.affine_select(
            out=b[:], in_=b[:], compare_op=ALU.not_equal, fill=1.0, base=0,
            pattern=[[-1, 128]], channel_multiplier=1), (name,), (name,))
    G(lambda E: E.memset(maskUT[:], 0.0), (), ("mUT",))
    G(lambda E: E.affine_select(out=maskUT[:], in_=maskUT[:],
                                compare_op=ALU.is_gt, fill=1.0, base=0,
                                pattern=[[-1, 128]], channel_multiplier=1),
      ("mUT",), ("mUT",))
    for k in range(4):
        V(lambda E, kk=k: E.tensor_copy(maskW[:, kk * 128:(kk + 1) * 128],
                                        maskUT[:]), ("mUT",), (("mW", k),))
    G(lambda E: E.memset(onesb[:], 1.0), (), ("onesb",))
    G(lambda E: E.memset(zrow[:], 0.0), (), ("zrow",))
    G(lambda E: E.memset(onesf[:], 1.0), (), ("ones1f",))

    DMA("Y", tiny[:, 0:DC], bvq_.rearrange("(c p) -> p c", p=128), (),
        (("tiny", 0),))
    DMA("Y", tiny[:, 8:8 + DC], bvk_.rearrange("(c p) -> p c", p=128), (),
        (("tiny", 1),))
    S(lambda E: E.activation(tiny[:, 16:16 + DC], tiny[:, 0:DC], AF.Sign,
                             bias=0.0, scale=1.0), (("tiny", 0),),
      (("tiny", 2),))
    S(lambda E: E.activation(tiny[:, 24:24 + DC], tiny[:, 8:8 + DC], AF.Sign,
                             bias=0.0, scale=1.0), (("tiny", 1),),
      (("tiny", 3),))
    V(lambda E: E.tensor_tensor(g_fm[:], tiny[:, 16:16 + DC],
                                tiny[:, 24:24 + DC], ALU.mult),
      (("tiny", 2), ("tiny", 3)), ("g_fm",))
    DMA("Y", trash[:], bvv_[None, :].partition_broadcast(128), ("trash",),
        ("trash",))
    S(lambda E: E.activation(bvs_bc[:], trash[:], AF.Sign, bias=0.0,
                             scale=1.0), ("trash",), ("bvs",))
    DMA("Y", wbuf[:], wsel[None, :].partition_broadcast(128), (), ("wbuf",))
    DMA("Y", stwb[:], stw[None, :].partition_broadcast(128), (), ("stwb",))
    DMA("G", hsel[:], halosel, (), ("hsel",))

    V(lambda E: E.memset(tiny[:, 40:42], 0.0), (), (("tiny", 40), ("tiny", 41)))
    with nc.sbuf_tensor("wmA", [128, D], F32) as wmA, \
         nc.sbuf_tensor("wmB", [128, D], F32) as wmB, \
         nc.sbuf_tensor("wmC", [128, D], F32) as wmC, \
         nc.sbuf_tensor("wmD", [128, D], F32) as wmD:
        wmbufs = ((trash, "trash", "Y"), (tmprow, "tmprow", "S"),
                  (wmA, "wmA", "G"), (wmB, "wmB", "Y"),
                  (wmC, "wmC", "S"), (wmD, "wmD", "G"))
        wmk = 0
        for wi, (wap, nchunk, ncols) in enumerate(((w1_, HC, D),
                                                   (w2_, DC, HID))):
            for ch in range(nchunk):
                for piece in range(ncols // D):
                    buf, bkey, q = wmbufs[wmk % 6]
                    wmk += 1
                    DMA(q, buf[:],
                        wap[ch * 128:(ch + 1) * 128,
                            piece * D:(piece + 1) * D],
                        (bkey,), (bkey,), tag="wm")
                    V(lambda E, b=buf: E.tensor_reduce(
                        tiny[:, 44:45], b[:], AX, ALU.add,
                        apply_absolute_value=True),
                      (bkey,), (("tiny", 44),), "wm")
                    V(lambda E, c=40 + wi: E.tensor_tensor(
                        tiny[:, c:c + 1], tiny[:, c:c + 1], tiny[:, 44:45],
                        ALU.add),
                      (("tiny", 40 + wi), ("tiny", 44)),
                      (("tiny", 40 + wi),), "wm")
    sch.barrier()
    G(lambda E: E.memset(tiny[:, 46:47], 1.0), (), (("tiny", 46),))
    T(lambda E: E.matmul(pb[0][0:2, 0:1], tiny[:, 40:42], tiny[:, 46:47],
                         start=True, stop=True),
      (("tiny", 40), ("tiny", 41), ("tiny", 46)), (PSK[0],))
    V(lambda E: E.tensor_scalar(tiny[0:2, 48:49], pb[0][0:2, 0:1],
                                1.0 / (HID * D), 1e-5, ALU.mult, ALU.max),
      (PSK[0],), (("tiny", 48),))
    V(lambda E: E.reciprocal(tiny[0:2, 49:50], tiny[0:2, 48:49]),
      (("tiny", 48),), (("tiny", 49),))
    T(lambda E: E.matmul(pb[2][0:1, 0:2], tiny[0:2, 48:49], id4f[0:2, 0:2],
                         start=True, stop=True),
      (("tiny", 48), "id4f"), (PSK[2],))
    T(lambda E: E.matmul(pb[2][0:1, 4:6], tiny[0:2, 49:50], id4f[0:2, 0:2],
                         start=True, stop=True),
      (("tiny", 49), "id4f"), (PSK[2],), accum=True)
    V(lambda E: E.tensor_copy(tiny[0:1, 52:54], pb[2][0:1, 0:2]),
      (PSK[2],), (("tiny", 52),))
    V(lambda E: E.tensor_copy(tiny[0:1, 54:56], pb[2][0:1, 4:6]),
      (PSK[2],), (("tiny", 54),))
    T(lambda E: E.matmul(pb[3][:, 0:4], onesf[:], tiny[0:1, 52:56],
                         start=True, stop=True),
      (("tiny", 52), ("tiny", 54), "ones1f"), (PSK[3],))
    V(lambda E: E.tensor_copy(mglob[:], pb[3][:, 0:4]),
      (PSK[3],), ("mglob",))

    for k, buf in ((0, cw0), (1, cw1), (2, cw2)):
        DMA("Y", buf[:], convw_.rearrange("(c p) t -> p c t", p=128)[:, :, k],
            (), (("cw", k),))
        V(lambda E, b=buf, kk=k: E.tensor_scalar(b[:], b[:], mglob[:, 0:1],
                                                 None, ALU.mult),
          (("cw", k), "mglob"), (("cw", k),))
    DMA("Y", cbv[:], convb_.rearrange("(c p) -> p c", p=128), (), ("cbv",))

    with nc.sbuf_tensor("wload0", [128, 2, D], F32) as wl0, \
         nc.sbuf_tensor("wload1", [128, 2, D], F32) as wl1, \
         nc.sbuf_tensor("wi32", [128, 2, D], I32) as wi32, \
         nc.sbuf_tensor("wtern0", [128, 2, D], BF16) as wt0, \
         nc.sbuf_tensor("wtern1", [128, 2, D], BF16) as wt1, \
         nc.sbuf_tensor("wtpA", [128, 512], BF16) as wtpA, \
         nc.sbuf_tensor("wtpB", [128, 512], BF16) as wtpB, \
         nc.sbuf_tensor("wtpC", [128, 512], BF16) as wtpC, \
         nc.sbuf_tensor("wtpD", [128, 512], BF16) as wtpD:
        wloads = (wl0, wl1)
        wterns = (wt0, wt1)
        wtps = (wtpA, wtpB, wtpC, wtpD)
        qn = [0]
        stq = ("S", "G")

        def quant_pair(src_ap, swcol, store_fns):
            sl = qn[0] % 2
            qn[0] += 1
            wload = wloads[sl]
            wtern = wterns[sl]
            DMA("Y", wload[:], src_ap, (("wld", sl),),
                (("wld", sl),), tag="wql")
            V(lambda E, c=swcol, b=wload: E.tensor_scalar(
                wi32[:].rearrange("p a c -> p (a c)"),
                b[:].rearrange("p a c -> p (a c)"),
                mglob[:, c:c + 1], None, ALU.mult),
              (("wld", sl), "mglob"), ("wi32",))
            V(lambda E, b=wtern: E.tensor_scalar(
                b[:].rearrange("p a c -> p (a c)"),
                wi32[:].rearrange("p a c -> p (a c)"), -1.0, 1.0,
                ALU.max, ALU.min),
              ("wi32",), (("wtn", sl),))
            for ch in range(2):
                for half in range(2):
                    b = 4 + half
                    wtp = wtps[ch * 2 + half]
                    for k in range(4):
                        dc = half * 4 + k
                        T(lambda E, d=dc, bb=b, kk=k, wt=wtern, cc=ch:
                          E.matmul(
                              pb[bb][:, kk * 128:(kk + 1) * 128],
                              wt[:, cc, d * 128:(d + 1) * 128], identb[:],
                              start=True, stop=True),
                          (("wtn", sl), "idb"), (PSK[b],), accum=True,
                          tag="wqt")
                    S(lambda E, bb=b, wp=wtp: E.activation(
                        wp[:], pb[bb][:], AF.Identity, bias=0.0, scale=1.0),
                      (PSK[b],), (("wtp", ch, half),))
                    store_fns[ch](half, wtp, ("wtp", ch, half),
                                  stq[(ch * 2 + half) % 2])

        def mk_st1(hh):
            def st1(half, wtp, wkey, q):
                DMA(q, w1q_s[hh, half], wtp[:], (wkey,),
                    (("w1qs", hh, half),), tag="w1st")
            return st1

        for hc2 in range(HC // 2):
            quant_pair(w1_[hc2 * 256:(hc2 + 1) * 256, :]
                       .rearrange("(a p) c -> p a c", p=128), 2,
                       (mk_st1(hc2 * 2), mk_st1(hc2 * 2 + 1)))

        def mk_st2(dd, pp):
            def st2(half, wtp, wkey, q):
                r0 = (pp * 8 + half * 4) * 128
                DMA(q, w2q_s[r0:r0 + 512, dd * 128:(dd + 1) * 128]
                    .rearrange("(a p) c -> p a c", p=128),
                    wtp[:].rearrange("p (a c) -> p a c", c=128),
                    (wkey,),
                    tuple(("w2qs", pp, half * 4 + k, dd)
                          for k in range(4)),
                    tag="w2st")
            return st2

        for dmc in range(DC):
            for p2 in range(2):
                quant_pair(w2_[dmc * 128:(dmc + 1) * 128,
                               p2 * 2 * D:(p2 + 1) * 2 * D]
                           .rearrange("p (a c) -> p a c", c=D), 3,
                           (mk_st2(dmc, p2 * 2), mk_st2(dmc, p2 * 2 + 1)))

    sch.barrier()

    ph1 = ExitStack()
    _p = lambda n, s, d=F32: ph1.enter_context(nc.sbuf_tensor(n, s, d))
    qLc = _p("qLc", [128, DC, BLK], FP8)
    qHc = _p("qHc", [128, DC, BLK], FP8)
    qmix = _p("qmix", [128, DC, BLK], FP8)
    kfm = [_p("kfm0", [128, BT, D], FP8), _p("kfm1", [128, BT, D], FP8)]
    xnk2 = [_p("xnk0", [128, BT, D], BF16), _p("xnk1", [128, BT, D], BF16)]
    x0k = _p("x0k", [128, BT, D], F32)
    attnT = _p("attnT", [128, BT, BLK], BF16)
    _sg = _p("sg80", [128, D], FP8)
    sg8 = [_sg, _sg]
    if True:
        for t in range(2 * BT):
            V(lambda E, tt=t: E.memset(resid[:, tt, :], 0.0), (),
              (("resid", t),))

        def emit_dmas(src_ap_fn):
            for t in range(BT):
                DMA("Y" if t % 2 == 0 else "G", x0k[:, t, :], src_ap_fn(t),
                    (("x0k", t),), (("x0k", t),), tag="kld")

        def emit_stats_sums(par, refine=False):
            base = par * BT
            for t in range(BT):
                ln_sums_tile(x0k[:, t, :], (("x0k", t),), t, "st", base,
                             False, square=False)
            ln_finish_mean(BT, "st", base)
            if refine:
                emit_refine(par)

        def emit_squares(par):
            base = par * BT
            for t in range(BT):
                ln_sq_tile(x0k[:, t, :], (("x0k", t),), t, "st", base)

        def emit_refine(par):
            base = par * BT
            for t in range(BT):
                S(lambda E, b=base, tt=t: E.activation(
                    trash[:], x0k[:, tt, :], AF.Identity,
                    bias=st_b[:, b + tt:b + tt + 1], scale=1.0),
                  (("x0k", t), ("st_b", base)), ("trash",), tag="cmp")
                V(lambda E: E.tensor_reduce(
                    tiny[:, 60:61], trash[:], AX, ALU.add),
                  ("trash",), (("tiny", 60),), tag="cmp")
                V(lambda E, tt=t, b=base: E.scalar_tensor_tensor(
                    st_b[:, b + tt:b + tt + 1], tiny[:, 60:61], -INVD,
                    st_b[:, b + tt:b + tt + 1], ALU.mult, ALU.add),
                  (("tiny", 60), ("st_b", base)), (("st_b", base),),
                  tag="cmp")

        def emit_stats_sign(par, refine=False):
            emit_stats_sums(par, refine)
            emit_squares(par)
            ln_finish_rstd(BT, "st", par * BT)

        def emit_sign_tp(dst, dstkey, gscale, par, tiles):
            base = par * BT
            sg = sg8[par]
            sgk = ("sg8",)
            for t in tiles:
                S(lambda E, tt=t, b=base, s2=sg: E.activation(
                    s2[:], x0k[:, tt, :], AF.Sign,
                    bias=st_b[:, b + tt:b + tt + 1], scale=1.0),
                  (("x0k", t), ("st_b", base)), (sgk,), tag="sgn")
                tp8(lambda d, s2=sg: s2[:, d * 128:(d + 1) * 128], (sgk,),
                    (lambda d, tt=t, d2=dst: d2[:, d, tt * 128:(tt + 1) * 128]),
                    (lambda dc, tt=t, dk=dstkey: dk + (dc, tt)), FP8,
                    "sg" + str(dstkey),
                    scale_ap_fn=(lambda d: g_fm[:, d:d + 1]) if gscale
                    else None,
                    scale_key="g_fm" if gscale else None)

        def emit_sign_tpk(par, tiles):
            base = par * BT
            sg = sg8[par]
            sgk = ("sg8",)
            kb2 = kfm[par]
            for t in tiles:
                S(lambda E, tt=t, b=base, s2=sg: E.activation(
                    s2[:], x0k[:, tt, :], AF.Sign,
                    bias=st_b[:, b + tt:b + tt + 1], scale=1.0),
                  (("x0k", t), ("st_b", base)), (sgk,), tag="sgn")
                tp8(lambda d, s2=sg: s2[:, d * 128:(d + 1) * 128], (sgk,),
                    None, None, FP8, "sgk",
                    wide_dst_fn=(lambda h, tt=t, kb=kb2:
                                 kb[:, tt, h * 512:(h + 1) * 512]),
                    wide_key_fn=(lambda h, tt=t, pp=par: ("kf", pp, tt, h)))

        def prep_partA(src_ap_fn, dst, dstkey, gscale, par, refine=False):
            emit_dmas(src_ap_fn)
            emit_stats_sign(par, refine)
            emit_sign_tp(dst, dstkey, gscale, par, range(BT))

        def prep_partK(src_ap_fn, par):
            emit_dmas(src_ap_fn)
            emit_stats_sign(par)
            emit_sign_tpk(par, range(BT))

        def emit_xn_tile(v, t):
            base = (v % 2) * BT
            xb = xnk2[v % 2]
            S(lambda E, tt=t, b=base, x2=xb: E.activation(
                x2[:, tt, :], x0k[:, tt, :], AF.Identity,
                bias=stb2[:, b + tt:b + tt + 1],
                scale=st_r[:, b + tt:b + tt + 1]),
              (("x0k", t), ("st_r", base), ("stb2", base)),
              (("xnk", v % 2, t),), tag="xnk")

        def emit_dma_tile(src_ap_fn, t):
            DMA("Y" if t % 2 == 0 else "G", x0k[:, t, :], src_ap_fn(t),
                (("x0k", t),), (("x0k", t),), tag="kld")

        def emit_sum_tile(par, t):
            ln_sums_tile(x0k[:, t, :], (("x0k", t),), t, "st", par * BT,
                         False, square=False)

        def emit_xn(v):
            for t in range(BT):
                emit_xn_tile(v, t)

        for s, qdst, qkey in ((0, qLc, ("qL",)), (1, qHc, ("qH",))):
            prep_partA(lambda t, s2=s: myblk[s2, t], qdst, qkey, True, s,
                       refine=True)

        qL_reads = tuple(("qL", dc, t) for dc in range(DC) for t in range(BT))
        qH_reads = tuple(("qH", dc, t) for dc in range(DC) for t in range(BT))

        def emit_qmix_piece(v, idx):
            wLap = wbuf[:, 2 * v:2 * v + 1]
            wHap = wbuf[:, 2 * v + 1:2 * v + 2]
            dc = idx // 2
            if idx % 2 == 0:
                V(lambda E, d=dc, w=wLap: E.tensor_scalar(
                    qmix[:, d, :], qLc[:, d, :], w, None, ALU.mult),
                  qL_reads + ("wbuf",), (("qmix", dc),), tag="qmx")
            else:
                V(lambda E, d=dc, w=wHap: E.scalar_tensor_tensor(
                    qmix[:, d, :], qHc[:, d, :], w, qmix[:, d, :],
                    ALU.mult, ALU.add),
                  qH_reads + ("wbuf", ("qmix", dc)), (("qmix", dc),),
                  tag="qmx")

        def emit_qmix(v):
            for idx in range(2 * DC):
                emit_qmix_piece(v, idx)

        def emit_scores(v, inject_fn=None):
            par = v % 2
            diag = v >= 15
            kbuf = kfm[par]
            sb_rot = 0
            nsteps = 12 if diag else 16
            for jc in range(BT):
                for ih in range(2):
                    if diag and jc > ih * 4 + 3:
                        continue
                    if inject_fn is not None:
                        t_inj = sb_rot - (nsteps - BT)
                        if t_inj >= 0:
                            inject_fn(t_inj)
                    b = sb_rot % 4
                    sb_rot += 1
                    for dc in range(DC):
                        T(lambda E, d=dc, j=jc, i=ih, bb=b, kb2=kbuf:
                          E.matmul(
                              pb[bb][:], kb2[:, j, d * 128:(d + 1) * 128],
                              qmix[:, d, i * 512:(i + 1) * 512],
                              start=(d == 0), stop=(d == DC - 1)),
                          (("kf", par, jc, dc // 4), ("qmix", dc)),
                          (PSK[b],), accum=True, tag="sc")
                    S(lambda E, j=jc, i=ih, bb=b: E.activation(
                        attnT[:, j, i * 512:(i + 1) * 512], pb[bb][:],
                        AF.Sigmoid, bias=0.0, scale=0.125),
                      (PSK[b],), (("aT", jc, ih),), tag="sig")
                    if diag:
                        for k in range(4):
                            ic = ih * 4 + k
                            if ic == jc:
                                V(lambda E, j=jc, c=ic: E.tensor_tensor(
                                    attnT[:, j, c * 128:(c + 1) * 128],
                                    attnT[:, j, c * 128:(c + 1) * 128],
                                    maskUT[:], ALU.mult),
                                  (("aT", jc, ih), "mUT"), (("aT", jc, ih),),
                                  tag="msk")
        def emit_av(v, inject_fn=None):
            par = v % 2
            diag = v >= 15
            wLap = wbuf[:, 2 * v:2 * v + 1]
            wHap = wbuf[:, 2 * v + 1:2 * v + 2]
            av_rot = 0
            for ic in range(BT):
                njc = (ic + 1) if diag else BT
                for dmh in range(2):
                    b = 6 + av_rot % 2
                    av_rot += 1
                    for jj in range(njc):
                        T(lambda E, j=jj, c=ic, m=dmh, bb=b, n=njc,
                          x2=xnk2[v % 2]:
                          E.matmul(
                              pb[bb][:], attnT[:, j, c * 128:(c + 1) * 128],
                              x2[:, j, m * 512:(m + 1) * 512],
                              start=(j == 0), stop=(j == n - 1)),
                          (("aT", jj, ic // 4), ("xnk", v % 2, jj)),
                          (PSK[b],), accum=True, tag="av")
                    for slot, wap in ((0, wLap), (1, wHap)):
                        rt = slot * BT + ic
                        V(lambda E, bb=b, r=rt, m=dmh, w=wap:
                          E.scalar_tensor_tensor(
                              resid[:, r, m * 512:(m + 1) * 512], pb[bb][:],
                              w, resid[:, r, m * 512:(m + 1) * 512],
                              ALU.mult, ALU.add),
                          ((PSK[b],) + ("wbuf", ("resid", rt))),
                          (("resid", rt),), tag="avac")
                    if inject_fn is not None:
                        inject_fn(av_rot - 1)

        prep_partK(lambda t: kvis[0, t], 0)
        emit_xn(0)
        emit_qmix(0)
        for t in range(BT):
            emit_dma_tile(lambda tt: kvis[1, tt], t)
            emit_sum_tile(1, t)
        for v in range(NVIS):
            nx = v + 1
            if nx < NVIS:
                ln_finish_mean(BT, "st", (nx % 2) * BT)
                inj_tp = lambda t, p=nx % 2: emit_sign_tpk(p, [t])
                work = []
                for i in range(8):
                    work.append(lambda i=i, n=nx: emit_qmix_piece(n, i))
                work.append(
                    lambda n=nx: ln_finish_rstd(BT, "st", (n % 2) * BT))
                for i in range(8, 16):
                    work.append(lambda i=i, n=nx: emit_qmix_piece(n, i))
                for t in range(BT):
                    def wtile(t=t, n=nx):
                        emit_xn_tile(n, t)
                        if n + 1 < NVIS:
                            emit_dma_tile(
                                lambda tt, v2=n + 1: kvis[v2, tt], t)
                            emit_sum_tile((n + 1) % 2, t)
                    work.append(wtile)
                done = [0]

                def inj_qm(slot, w=work, d=done):
                    want = (len(w) * (slot + 1) + 15) // 16
                    while d[0] < min(want, len(w)):
                        w[d[0]]()
                        d[0] += 1
            else:
                inj_tp = None
                inj_qm = None
            emit_scores(v, inject_fn=inj_tp)
            if nx < NVIS:
                emit_squares(nx % 2)
            emit_av(v, inject_fn=inj_qm)
            if nx < NVIS:
                inj_qm(15)

        def fold_dma(rt):
            sl = rt % 4
            DMA("Y" if rt % 2 == 0 else "G", x0k[:, sl, :],
                myblk[rt // BT, rt % BT],
                (("x0k", sl),), (("x0k", sl),), tag="fold")
        for rt in range(4):
            fold_dma(rt)
        for rt in range(2 * BT):
            sl = rt % 4
            V(lambda E, r=rt: E.tensor_tensor(tmprow[:], resid[:, r, :],
                                              bvs_bc[:], ALU.mult),
              (("resid", rt), "bvs"), ("tmprow",), tag="fold")
            V(lambda E, r=rt, s2=sl: E.tensor_tensor(
                resid[:, r, :], tmprow[:], x0k[:, s2, :], ALU.add),
              ("tmprow", ("x0k", sl)), (("resid", rt),), tag="fold")
            if rt + 4 < 2 * BT:
                fold_dma(rt + 4)

    ph1.close()
    sch.barrier()
    if debug >= 1:
        for t in range(2 * BT):
            DMA("Y", dbg[0, t * 128:(t + 1) * 128, :], resid[:, t, :],
                (("resid", t),), (("dbg", 0, t),), tag="dbg")
        sch.barrier()

    def ffn_sublayer(agi):
        phf = ExitStack()
        _q = lambda n, s, d=F32: phf.enter_context(nc.sbuf_tensor(n, s, d))
        hbuf = _q(f"hbuf{agi}", [128, HC, BLK + 2], BF16)
        r1fm = _q(f"r1fm{agi}", [128, DC, BLK + 2], BF16)
        wsh = _q(f"wsh{agi}", [128, 2048], BF16)
        wshb = _q(f"wshb{agi}", [128, 2048], BF16)
        wshs = (wsh, wshb)
        xq1 = _q(f"xq1{agi}", [128, D], BF16)
        fi32 = _q(f"fi32{agi}", [128, D], I32)
        utmp = _q(f"utmp{agi}", [128, BLK], F32)
        utmps = ((utmp, ("ut",)), (trash, ("trash",)))
        r2s = _q(f"r2s{agi}", [128, 512], BF16)
        r2sb = _q(f"r2sb{agi}", [128, 512], BF16)
        r2ss = (r2s, r2sb)
        amaxp = _q(f"amx{agi}", [128, BLK], BF16)
        aminp = _q(f"amn{agi}", [128, BLK], BF16)
        haloq = _q(f"hq{agi}", [4, D], BF16)
        haloi = _q(f"hi{agi}", [4, D], I32)
        s2bc = _q(f"s2bc{agi}", [128, BLK], BF16)
        if True:

            edges = [(0, 0, 0), (0, BT - 1, 127), (1, 0, 0), (1, BT - 1, 127)]
            for r, (s, t, p) in enumerate(edges):
                DMA("Y", ag_ei[agi][r:r + 1, :],
                    resid[p:p + 1, s * BT + t, :],
                    (("resid", s * BT + t),), (("aei", agi, r),), tag="edg")
            sch.emit("G", lambda E: E.collective_compute(
                "AllGather", ALU.bypass,
                replica_groups=[list(range(NCORES))],
                ins=[ag_ei[agi][:]], outs=[ag_eo[agi][:]]),
                tuple(("aei", agi, r) for r in range(4)), (("aeo", agi),),
                is_dma=True, is_coll=True, tag="AG")
            DMA("Y", agbuf[:], ag_eo[agi], (("aeo", agi), "agbuf"),
                ("agbuf",), tag="agl")
            ln_stats(lambda j: resid[:, j, :], lambda j: ("resid", j), NB,
                     "fst")
            for half in range(2):
                T(lambda E, m=half: E.matmul(
                    pb[6][0:4, 0:512], hsel[:],
                    agbuf[:, m * 512:(m + 1) * 512],
                    start=True, stop=True),
                  ("hsel", "agbuf"), (PSK[6],), tag="hx")
                V(lambda E, m=half: E.tensor_copy(
                    halox[0:4, m * 512:(m + 1) * 512], pb[6][0:4, 0:512]),
                  (PSK[6],), ("halox",))
            hx = halox[0:4, :]
            ti = lambda a, b: tiny[0:4, a:b]
            V(lambda E: E.tensor_reduce(ti(32, 33), hx, AX, ALU.add),
              ("halox",), (("hti", 32),))
            S(lambda E: E.activation(trash[0:4, :], hx, AF.Square, bias=0.0,
                                     scale=1.0, accum_out=ti(33, 34)),
              ("halox",), (("hti", 33), "trash"))
            V(lambda E: E.tensor_scalar(ti(34, 35), ti(32, 33), INVD, None,
                                        ALU.mult), (("hti", 32),),
              (("hti", 34),))
            V(lambda E: E.tensor_tensor(ti(35, 36), ti(34, 35), ti(34, 35),
                                        ALU.mult), (("hti", 34),),
              (("hti", 35),))
            V(lambda E: E.scalar_tensor_tensor(ti(35, 36), ti(33, 34), INVD,
                                               ti(35, 36), ALU.mult,
                                               ALU.subtract),
              (("hti", 33), ("hti", 35)), (("hti", 35),))
            V(lambda E: E.tensor_scalar(ti(35, 36), ti(35, 36), EPS_LN,
                                        None, ALU.add),
              (("hti", 35),), (("hti", 35),))
            S(lambda E: E.activation(ti(35, 36), ti(35, 36), AF.Sqrt,
                                     bias=0.0, scale=1.0),
              (("hti", 35),), (("hti", 35),))
            V(lambda E: E.reciprocal(ti(36, 37), ti(35, 36)),
              (("hti", 35),), (("hti", 36),))
            V(lambda E: E.tensor_reduce(ti(37, 38), hx, AX, ALU.max),
              ("halox",), (("hti", 37),))
            V(lambda E: E.tensor_reduce(ti(38, 39), hx, AX, ALU.min),
              ("halox",), (("hti", 38),))
            V(lambda E: E.tensor_tensor(ti(37, 38), ti(37, 38), ti(34, 35),
                                        ALU.subtract),
              (("hti", 37), ("hti", 34)), (("hti", 37),))
            V(lambda E: E.tensor_tensor(ti(38, 39), ti(34, 35), ti(38, 39),
                                        ALU.subtract),
              (("hti", 38), ("hti", 34)), (("hti", 38),))
            V(lambda E: E.tensor_tensor(ti(37, 38), ti(37, 38), ti(38, 39),
                                        ALU.max),
              (("hti", 37), ("hti", 38)), (("hti", 37),))
            V(lambda E: E.tensor_tensor(ti(37, 38), ti(37, 38), ti(36, 37),
                                        ALU.mult),
              (("hti", 37), ("hti", 36)), (("hti", 37),))
            V(lambda E: E.tensor_scalar(ti(37, 38), ti(37, 38), 1e-5, None,
                                        ALU.max), (("hti", 37),),
              (("hti", 37),))
            V(lambda E: E.reciprocal(ti(39, 40), ti(37, 38)),
              (("hti", 37),), (("hti", 39),))
            V(lambda E: E.tensor_scalar(ti(39, 40), ti(39, 40), 127.0, None,
                                        ALU.mult), (("hti", 39),),
              (("hti", 39),))
            V(lambda E: E.tensor_scalar(ti(41, 42), ti(37, 38), INV127, None,
                                        ALU.mult), (("hti", 37),),
              (("hti", 41),))
            V(lambda E: E.tensor_tensor(ti(42, 43), ti(36, 37), ti(39, 40),
                                        ALU.mult),
              (("hti", 36), ("hti", 39)), (("hti", 42),))
            V(lambda E: E.scalar_tensor_tensor(ti(43, 44), ti(34, 35), -1.0,
                                               ti(42, 43), ALU.mult,
                                               ALU.mult),
              (("hti", 34), ("hti", 42)), (("hti", 43),))
            V(lambda E: E.tensor_scalar(haloi[:], hx, ti(42, 43), ti(43, 44),
                                        ALU.mult, ALU.add),
              ("halox", ("hti", 42), ("hti", 43)), ("haloi",))
            V(lambda E: E.tensor_scalar(haloq[:], haloi[:], ti(41, 42), None,
                                        ALU.mult),
              ("haloi", ("hti", 41)), ("haloq",))
            for dc in range(DC):
                T(lambda E, d=dc: E.matmul(
                    pb[7][:, d * 4:(d + 1) * 4],
                    haloq[0:4, d * 128:(d + 1) * 128], identb[0:4, 0:4],
                    start=(d == 0), stop=(d == DC - 1)),
                  ("haloq", "idb"), (PSK[7],), accum=True, tag="ht")
            V(lambda E: E.tensor_copy(
                halor[:].rearrange("p a b -> p (a b)"), pb[7][:, 0:32]),
              (PSK[7],), ("halor",))

            for s in range(2):
                for t in range(BT):
                    rt = s * BT + t
                    V(lambda E, r=rt: E.tensor_reduce(
                        tiny[:, 64:65], resid[:, r, :], AX, ALU.max),
                      (("resid", rt),), (("fti", 64),))
                    V(lambda E, r=rt: E.tensor_reduce(
                        tiny[:, 65:66], resid[:, r, :], AX, ALU.min),
                      (("resid", rt),), (("fti", 65),))
                    V(lambda E, r=rt: E.tensor_tensor(
                        tiny[:, 64:65], tiny[:, 64:65], st_m[:, r:r + 1],
                        ALU.subtract),
                      (("fti", 64), ("st_m", 0)), (("fti", 64),))
                    V(lambda E, r=rt: E.tensor_tensor(
                        tiny[:, 65:66], st_m[:, r:r + 1], tiny[:, 65:66],
                        ALU.subtract),
                      (("fti", 65), ("st_m", 0)), (("fti", 65),))
                    V(lambda E: E.tensor_tensor(
                        tiny[:, 64:65], tiny[:, 64:65], tiny[:, 65:66],
                        ALU.max),
                      (("fti", 64), ("fti", 65)), (("fti", 64),))
                    V(lambda E, r=rt: E.tensor_tensor(
                        qa[:, r:r + 1], tiny[:, 64:65], st_r[:, r:r + 1],
                        ALU.mult),
                      (("fti", 64), ("st_r", 0)), (("qa", rt),))
                    V(lambda E, r=rt: E.tensor_scalar(
                        qa[:, r:r + 1], qa[:, r:r + 1], 1e-5, None, ALU.max),
                      (("qa", rt),), (("qa", rt),))
                    V(lambda E, r=rt: E.reciprocal(tiny[:, 66:67],
                                                   qa[:, r:r + 1]),
                      (("qa", rt),), (("fti", 66),))
                    V(lambda E, r=rt: E.scalar_tensor_tensor(
                        qs1a[:, r:r + 1], tiny[:, 66:67], 127.0,
                        st_r[:, r:r + 1], ALU.mult, ALU.mult),
                      (("fti", 66), ("st_r", 0)), (("qs1a", rt),))
                    V(lambda E, r=rt: E.scalar_tensor_tensor(
                        qs1b[:, r:r + 1], st_m[:, r:r + 1], -1.0,
                        qs1a[:, r:r + 1], ALU.mult, ALU.mult),
                      (("st_m", 0), ("qs1a", rt)), (("qs1b", rt),))
                    V(lambda E, r=rt: E.tensor_scalar(
                        qsi[:, r:r + 1], qa[:, r:r + 1], INV127, None,
                        ALU.mult), (("qa", rt),), (("qsi", rt),))
                    V(lambda E, r=rt: E.tensor_scalar(
                        fi32[:], resid[:, r, :], qs1a[:, r:r + 1],
                        qs1b[:, r:r + 1], ALU.mult, ALU.add),
                      (("resid", rt), ("qs1a", rt), ("qs1b", rt)), ("fi32",))
                    V(lambda E, r=rt: E.tensor_scalar(
                        xq1[:], fi32[:], qsi[:, r:r + 1], None, ALU.mult),
                      ("fi32", ("qsi", rt)), ("xq1",))
                    tp8(lambda d: xq1[:, d * 128:(d + 1) * 128], ("xq1",),
                        (lambda d, tt=t: r1fm[:, d,
                                              1 + tt * 128:1 + (tt + 1) * 128]),
                        (lambda dc, tt=t: ("r1", dc, tt)), BF16, "r1")
                V(lambda E, ss=s: E.tensor_copy(
                    r1fm[:, :, 0:1].rearrange("p a b -> p (a b)"),
                    halor[:, :, 2 * ss:2 * ss + 1]
                    .rearrange("p a b -> p (a b)")),
                  ("halor",), (("r1", "hl"),), tag="r1h")
                V(lambda E, ss=s: E.tensor_copy(
                    r1fm[:, :, BLK + 1:BLK + 2]
                    .rearrange("p a b -> p (a b)"),
                    halor[:, :, 2 * ss + 1:2 * ss + 2]
                    .rearrange("p a b -> p (a b)")),
                  ("halor",), (("r1", "hr"),), tag="r1h")
                r1_reads = tuple(("r1", dc, t) for dc in range(DC)
                                 for t in range(BT)) + \
                    (("r1", "hl"), ("r1", "hr"))

                rot = 0
                for hcg in range(16):
                    wsl = hcg % 2
                    wsc = wshs[wsl]
                    DMA("G", wsc[:].rearrange("p (g c) -> p g c", c=512),
                        w1q_s[hcg * 2:(hcg + 1) * 2]
                        .rearrange("a h p c -> p (a h) c"),
                        (("wsh", wsl),) + tuple(("w1qs", hcg * 2 + i, hf)
                                                for i in range(2)
                                                for hf in range(2)),
                        (("wsh", wsl),), tag="w1ld")
                    for hcl in range(2):
                        hc = hcg * 2 + hcl
                        banks = [(rot + i) % 6 for i in range(3)]
                        rot += 3
                        for gi in range(3):
                            c0 = gi * 342
                            cw = 342
                            for dc in range(DC):
                                T(lambda E, d=dc, h=hcl, bb=banks[gi], cc=c0,
                                  ww=cw, wc=wsc: E.matmul(
                                      pb[bb][:, 0:ww],
                                      wc[:, (h * 8 + d) * 128:
                                         (h * 8 + d + 1) * 128],
                                      r1fm[:, d, cc:cc + ww],
                                      start=(d == 0), stop=(d == DC - 1)),
                                  (("wsh", wsl),) + r1_reads,
                                  ((PSK[banks[gi]], "f1"),), accum=True,
                                  tag="f1")
                        for gi in range(3):
                            c0 = gi * 342
                            V(lambda E, h=hc, bb=banks[gi], cc=c0:
                              E.tensor_copy(hbuf[:, h, cc:cc + 342],
                                            pb[bb][:, 0:342]),
                              ((PSK[banks[gi]], "f1"),), (("hb", hc),),
                              tag="f1ev")
                        ut, utk = utmps[hc % 2]
                        V(lambda E, h=hc, u=ut: E.tensor_scalar(
                            u[:, 0:BLK], hbuf[:, h, 0:BLK], cw0[:, h:h + 1],
                            None, ALU.mult), (("hb", hc), ("cw", 0)),
                          utk, tag="cv")
                        V(lambda E, h=hc, u=ut: E.scalar_tensor_tensor(
                            u[:, 0:BLK], hbuf[:, h, 1:BLK + 1],
                            cw1[:, h:h + 1], u[:, 0:BLK], ALU.mult, ALU.add),
                          (("hb", hc), ("cw", 1)) + utk, utk, tag="cv")
                        V(lambda E, h=hc, u=ut: E.scalar_tensor_tensor(
                            u[:, 0:BLK], hbuf[:, h, 2:BLK + 2],
                            cw2[:, h:h + 1], u[:, 0:BLK], ALU.mult, ALU.add),
                          (("hb", hc), ("cw", 2)) + utk, utk, tag="cv")
                        S(lambda E, h=hc, u=ut: E.activation(
                            hbuf[:, h, 0:BLK], u[:, 0:BLK], AF.Silu,
                            bias=cbv[:, h:h + 1], scale=1.0),
                          utk + ("cbv",), (("hb", hc),), tag="silu")
                        if hc == 1:
                            V(lambda E: E.tensor_tensor(
                                amaxp[:], hbuf[:, 0, 0:BLK],
                                hbuf[:, 1, 0:BLK], ALU.max),
                              (("hb", 0), ("hb", 1)), ("amx",), tag="am")
                            V(lambda E: E.tensor_tensor(
                                aminp[:], hbuf[:, 0, 0:BLK],
                                hbuf[:, 1, 0:BLK], ALU.min),
                              (("hb", 0), ("hb", 1)), ("amn",), tag="am")
                        elif hc >= 2:
                            V(lambda E, h=hc: E.tensor_tensor(
                                amaxp[:], amaxp[:], hbuf[:, h, 0:BLK],
                                ALU.max),
                              ("amx", ("hb", hc)), ("amx",), tag="am")
                            V(lambda E, h=hc: E.tensor_tensor(
                                aminp[:], aminp[:], hbuf[:, h, 0:BLK],
                                ALU.min),
                              ("amn", ("hb", hc)), ("amn",), tag="am")

                V(lambda E: E.tensor_scalar(aminp[:], aminp[:], -1.0, None,
                                            ALU.mult), ("amn",), ("amn",),
                  tag="am")
                V(lambda E: E.tensor_tensor(amaxp[:], amaxp[:], aminp[:],
                                            ALU.max), ("amx", "amn"),
                  ("amx",), tag="am")
                for t in range(BT):
                    bb = 6 + (t % 2)
                    T(lambda E, tt=t, b2=bb: E.matmul(
                        pb[b2][:, 0:128], amaxp[:, tt * 128:(tt + 1) * 128],
                        identb[:], start=True, stop=True),
                      ("amx", "idb"), (PSK[bb],), tag="amt")
                    V(lambda E, tt=t, b2=bb: E.tensor_reduce(
                        s2t[:, tt:tt + 1], pb[b2][:, 0:128], AX, ALU.max),
                      (PSK[bb],), (("s2", t),), tag="amr")
                s2_reads = tuple(("s2", t) for t in range(BT))
                V(lambda E: E.tensor_scalar(s2t[:], s2t[:], 1e-5, None,
                                            ALU.max), s2_reads, ("s2c",))
                V(lambda E: E.reciprocal(tiny[:, 48:48 + BT], s2t[:]),
                  ("s2c",), (("fti", 48),))
                V(lambda E: E.tensor_scalar(s2t[:], tiny[:, 48:48 + BT],
                                            127.0, None, ALU.mult),
                  (("fti", 48),), ("s2v",))
                V(lambda E: E.tensor_copy(s2b16[:], s2t[:]), ("s2v",),
                  ("s2b",))
                V(lambda E: E.tensor_copy(tiny[:, 56:56 + BT], s2b16[:]),
                  ("s2b",), (("fti", 56),))
                V(lambda E: E.reciprocal(s2i[:], tiny[:, 56:56 + BT]),
                  (("fti", 56),), ("s2iv",))
                V(lambda E: E.tensor_scalar(s2i[:], s2i[:], mglob[:, 1:2],
                                            None, ALU.mult),
                  ("s2iv", "mglob"), ("s2iv",))
                for grp in range(2):
                    for k in range(4):
                        T(lambda E, tt=grp * 4 + k, kk=k: E.matmul(
                            pb[6][0:1, kk * 128:(kk + 1) * 128],
                            s2b16[:, tt:tt + 1], identb[:],
                            start=(kk == 0), stop=(kk == 3)),
                          ("s2b", "idb"), (PSK[6],), accum=True, tag="s2r")
                    V(lambda E, g=grp: E.tensor_copy(
                        tmprow[0:1, g * 512:(g + 1) * 512],
                        pb[6][0:1, 0:512]),
                      (PSK[6],), ("s2row",))
                for half in range(2):
                    T(lambda E, m=half: E.matmul(
                        pb[7][:, 0:512], onesf[:],
                        tmprow[0:1, m * 512:(m + 1) * 512], start=True,
                        stop=True),
                      ("s2row", "ones1f"), (PSK[7],), tag="s2bm")
                    V(lambda E, m=half: E.tensor_copy(
                        s2bc[:, m * 512:(m + 1) * 512], pb[7][:, 0:512]),
                      (PSK[7],), (("s2bc", half),), tag="s2be")

                if dbg2 is not None and agi == 1 and s == 0:
                    dump(s2t[:, 0:BT], ("s2v",))
                    dump(s2i[:, 0:BT], ("s2iv",))
                    i = dump_n[0]
                    dump_n[0] += 1
                    DMA("G", dbg2[i, :, 0:BLK], s2bc[:],
                        (("s2bc", 0), ("s2bc", 1), ("dbg2", i)),
                        (("dbg2", i),), tag="dumps2bc")
                    i = dump_n[0]
                    dump_n[0] += 1
                    DMA("G", dbg2[i, :, 0:BLK], amaxp[:, 0:BLK],
                        ("amx", ("dbg2", i)), (("dbg2", i),), tag="dumpam")
                for th in range(2):
                    for hcg in range(16):
                        wsl = hcg % 2
                        wsc = wshs[wsl]
                        DMA("G", wsc[:].rearrange("p (g c) -> p g c", c=D),
                            w2q_s[hcg * 256:(hcg + 1) * 256, :]
                            .rearrange("(a p) c -> p a c", p=128),
                            (("wsh", wsl),) + tuple(("w2qs", pp, k2, dd)
                                                    for pp in range(4)
                                                    for k2 in range(8)
                                                    for dd in range(DC)
                                                    if (pp * 8 + k2) // 2
                                                    == hcg),
                            (("wsh", wsl),), tag="w2ld")
                        for hcl in range(2):
                            hc = hcg * 2 + hcl
                            rsl = hc % 2
                            rsc = r2ss[rsl]
                            V(lambda E, h=hc, tt=th: E.tensor_tensor(
                                fi32[:, 0:512],
                                hbuf[:, h, tt * 512:(tt + 1) * 512],
                                s2bc[:, tt * 512:(tt + 1) * 512], ALU.mult),
                              (("hb", hc), ("s2bc", th)), ("f32q",),
                              tag="q2")
                            V(lambda E, rc=rsc: E.tensor_copy(
                                rc[:], fi32[:, 0:512]),
                              ("f32q",), (("r2s", rsl),), tag="q2c")
                            for tt in range(4):
                                for dmh in range(2):
                                    b = tt * 2 + dmh
                                    T(lambda E, h=hcl, t4=tt, m=dmh, bb=b,
                                      fr=(hc == 0), la=(hc == HC - 1),
                                      rc=rsc, wc=wsc:
                                      E.matmul(
                                          pb[bb][:],
                                          rc[:, t4 * 128:(t4 + 1) * 128],
                                          wc[:, h * 1024 + m * 512:
                                             h * 1024 + (m + 1) * 512],
                                          start=fr, stop=la),
                                      (("r2s", rsl), ("wsh", wsl)), (PSK[b],),
                                      accum=True, tag="f2")
                    for tt in range(4):
                        rt = s * BT + th * 4 + tt
                        for dmh in range(2):
                            b = tt * 2 + dmh
                            V(lambda E, bb=b, r=rt, m=dmh,
                              sc=th * 4 + tt: E.scalar_tensor_tensor(
                                  resid[:, r, m * 512:(m + 1) * 512],
                                  pb[bb][:], s2i[:, sc:sc + 1],
                                  resid[:, r, m * 512:(m + 1) * 512],
                                  ALU.mult, ALU.add),
                              (PSK[b], "s2iv", ("resid", rt)),
                              (("resid", rt),), tag="f2ev")
                        if agi == 1:
                            DMA(("Y", "G", "S")[rt % 3],
                                out_y[rt * 128:(rt + 1) * 128, :],
                                resid[:, rt, :],
                                (("resid", rt),), (("out", rt),), tag="out")
        phf.close()
        sch.barrier()

    ffn_sublayer(0)
    if debug >= 2:
        for t in range(2 * BT):
            DMA("Y", dbg[1, t * 128:(t + 1) * 128, :], resid[:, t, :],
                (("resid", t),), (("dbg", 1, t),), tag="dbg")
        sch.barrier()

    ph2 = ExitStack()
    _r = lambda n, s, d=F32: ph2.enter_context(nc.sbuf_tensor(n, s, d))
    phikT = _r("phikT", [128, BT, D], BF16)
    vtm = _r("vtm", [128, BT, D], BF16)
    phiqF = _r("phiqF", [128, DC, BLK], BF16)
    pkF = _r("pkF", [128, DC, 128], BF16)
    xnfm = _r("xnfm", [128, DC, 512], BF16)
    xnt = _r("xnt", [128, D], BF16)
    amk = _r("amk", [128, 2048], BF16)
    otm = _r("otm", [128, 4, D], BF16)
    ofm = _r("ofm", [128, DC, 128], BF16)
    kvst = _r("kvst", [128, 8 * 65], BF16)
    kvpref = _r("kvpref", [128, 2, 8 * 65], F32)
    stbuf = _r("stbuf", [128, 8 * 65], F32)
    wstr = _r("wstr", [128, D], BF16)
    wob = _r("wob", [128, DC, D], BF16)
    bqfm = _r("bqfm", [128, DC], F32)
    bkbc = _r("bkbc", [128, D], F32)
    bvbc = _r("bvbc", [128, D], F32)
    bobc = _r("bobc", [128, D], F32)
    putm = _r("putm", [128, 512], F32)
    phm = _r("phm", [128, 512], BF16)
    phe = _r("phe", [128, 512], BF16)
    r16 = _r("r16", [128, 16], F32)
    if True:

        DMA("Y", bqfm[:], bq_.rearrange("(c p) -> p c", p=128), (), ("bqf",))
        DMA("Y", bkbc[:], bk_[None, :].partition_broadcast(128), (), ("bkb",))
        DMA("Y", bvbc[:], bvb_[None, :].partition_broadcast(128), (), ("bvb",))
        DMA("Y", bobc[:], bo_[None, :].partition_broadcast(128), (), ("bob",))
        for hdc in range(DC):
            DMA("G", wob[:, hdc, :], wo_[hdc * 128:(hdc + 1) * 128, :],
                (("wob", hdc),), (("wob", hdc),), tag="wold")
        ln_stats(lambda j: resid[:, j, :], lambda j: ("resid", j), NB, "bst")


    def state_bank_open(bb, tag):
        T(lambda E, b2=bb: E.matmul(pb[b2][:, 0:260], zrow[0:1, 0:128],
                                    zrow[0:1, 0:260], start=True, stop=False),
          ("zrow",), (PSK[bb],), accum=True, tag=tag + "o")

    def state_bank_close(bb, tag):
        T(lambda E, b2=bb: E.matmul(pb[b2][:, 0:260], zrow[0:1, 0:128],
                                    zrow[0:1, 0:260], start=False, stop=True,
                                    skip_group_check=True),
          ("zrow",), (PSK[bb],), accum=True, tag=tag + "c")

        def xn_chunk(s, ach):
            for k in range(4):
                t = ach * 4 + k
                rt = s * BT + t
                V(lambda E, r=rt: E.tensor_scalar(
                    xnt[:], resid[:, r, :], st_r[:, r:r + 1],
                    stb2[:, r:r + 1], ALU.mult, ALU.add),
                  (("resid", rt), ("st_r", 0), ("stb2", 0)), ("xnt",), tag="bxn")
                tp8(lambda d: xnt[:, d * 128:(d + 1) * 128], ("xnt",),
                    (lambda d, kk=k: xnfm[:, d, kk * 128:(kk + 1) * 128]),
                    (lambda dc, kk=k: ("xnf", dc, kk)), BF16, "bx")

        xn_reads = tuple(("xnf", dc, k) for dc in range(DC) for k in range(4))

        def phi_evac(bb, bkey, dst_ap, dkey, bias_ap, bias_key):
            V(lambda E, a=bias_ap: E.tensor_tensor(putm[:], pb[bb][:], a,
                                                   ALU.add),
              (bkey, bias_key), ("putm",), tag="phb")
            V(lambda E: E.tensor_scalar(phm[:], putm[:], 0.0, None,
                                        ALU.min), ("putm",), ("trm",),
              tag="phm")
            S(lambda E: E.activation(phe[:], phm[:],
                                     AF.Exp, bias=0.0, scale=1.0),
              ("trm",), ("tre",), tag="phe")
            V(lambda E, d=dst_ap: E.scalar_tensor_tensor(
                d, putm[:], 0.0, phe[:], ALU.max, ALU.add),
              ("putm", "tre"), (dkey,), tag="phx")

        for s in range(2):
            for ach in range(2):
                xn_chunk(s, ach)
                for wi, (wap, dstbuf, isphi, bias_ap, bias_key, wkey) in \
                        enumerate(((wk_, phikT, True, bkbc, "bkb", "wK"),
                                   (wv_, vtm, False, bvbc, "bvb", "wV"))):
                    for dmc in range(DC):
                        DMA("G", wstr[:], wap[dmc * 128:(dmc + 1) * 128, :],
                            ((wkey,),) if False else ("wstr",), ("wstr",),
                            tag="bwld")
                        for k in range(4):
                            for hdh in range(2):
                                b = k * 2 + hdh
                                T(lambda E, d=dmc, k2=k, m=hdh, bb=b:
                                  E.matmul(
                                      pb[bb][:],
                                      xnfm[:, d, k2 * 128:(k2 + 1) * 128],
                                      wstr[:, m * 512:(m + 1) * 512],
                                      start=(d == 0), stop=(d == DC - 1)),
                                  (("xnf", dmc, k), "wstr"),
                                  (PSK[b],), accum=True,
                                  tag="bpj")
                    for k in range(4):
                        t = ach * 4 + k
                        for hdh in range(2):
                            b = k * 2 + hdh
                            if isphi:
                                phi_evac(
                                    b, PSK[b],
                                    dstbuf[:, t, hdh * 512:(hdh + 1) * 512],
                                    ("pk", t, hdh),
                                    bias_ap[:, hdh * 512:(hdh + 1) * 512],
                                    bias_key)
                            else:
                                V(lambda E, bb=b, t2=t, m=hdh:
                                  E.tensor_tensor(
                                      vtm[:, t2, m * 512:(m + 1) * 512],
                                      pb[bb][:],
                                      bvbc[:, m * 512:(m + 1) * 512],
                                      ALU.add),
                                  (PSK[b], "bvb"),
                                  (("vt", t, hdh),), tag="bpv")
            state_bank_open(6, "s1")
            state_bank_open(7, "s1")
            for n in range(BT):
                for h in range(NH):
                    bb = 6 + h // 8
                    p0 = (h % 2) * 64
                    c0 = ((h % 8) // 2) * 65
                    hk = (h * 64) // 512
                    T(lambda E, hh=h, nn=n, b2=bb, pp=p0, cc=c0: E.matmul(
                        pb[b2][pp:pp + 64, cc:cc + 64],
                        phikT[:, nn, hh * 64:(hh + 1) * 64],
                        vtm[:, nn, hh * 64:(hh + 1) * 64],
                        start=False, stop=False, skip_group_check=True),
                      (("pk", n, hk), ("vt", n, hk)),
                      (PSK[bb],), accum=True, tag="s1")
                    T(lambda E, hh=h, nn=n, b2=bb, pp=p0, cc=c0: E.matmul(
                        pb[b2][pp:pp + 64, cc + 64:cc + 65],
                        phikT[:, nn, hh * 64:(hh + 1) * 64], onesb[:],
                        start=False, stop=False, skip_group_check=True),
                      (("pk", n, hk), "onesb"),
                      (PSK[bb],), accum=True, tag="s1k")
            for half in range(2):
                bb = 6 + half
                V(lambda E, b2=bb, hh=half: E.tensor_copy(
                    stbuf[:, hh * 260:(hh + 1) * 260], pb[b2][:, 0:260]),
                  (PSK[bb],), (("stb", s, half),), tag="s1ev")
            state_bank_close(6, "s1")
            state_bank_close(7, "s1")
            DMA("Y", ag_si[s], stbuf[:], (("stb", s, 0), ("stb", s, 1)),
                (("agsi", s),), tag="stpub")
            for t in range(BT):
                DMA("Y", spill[s, 0, t], phikT[:, t, :],
                    (("pk", t, 0), ("pk", t, 1)), (("spl", s, 0, t),),
                    tag="spl")
                DMA("Y", spill[s, 1, t], vtm[:, t, :],
                    (("vt", t, 0), ("vt", t, 1)), (("spl", s, 1, t),),
                    tag="spl")

        sch.emit("G", lambda E: E.collective_compute(
            "AllGather", ALU.bypass, replica_groups=[list(range(NCORES))],
            ins=[ag_si[:]], outs=[ag_so[:]]),
            (("agsi", 0), ("agsi", 1)), ("agso",), is_dma=True, is_coll=True,
            tag="AGs")

        for s in range(2):
            V(lambda E, ss=s: E.memset(kvpref[:, ss, :], 0.0), (),
              (("kvp", s),))
        for b in range(NB):
            DMA("Y", stbuf[:], ag_so[b], ("agso", ("stb", "p")),
                (("stb", "p"),), tag="pfl")
            for s in range(2):
                V(lambda E, bb=b, ss=s: E.scalar_tensor_tensor(
                    kvpref[:, ss, :], stbuf[:],
                    stwb[:, 2 * bb + ss:2 * bb + ss + 1], kvpref[:, ss, :],
                    ALU.mult, ALU.add),
                  (("stb", "p"), "stwb", ("kvp", s)), (("kvp", s),),
                  tag="pfw")

        for s in range(2):
            for t in range(BT):
                DMA("Y", phikT[:, t, :], spill[s, 0, t],
                    (("spl", s, 0, t), ("pk", t, 0), ("pk", t, 1)),
                    (("pk", t, 0), ("pk", t, 1)), tag="rld")
                DMA("Y", vtm[:, t, :], spill[s, 1, t],
                    (("spl", s, 1, t), ("vt", t, 0), ("vt", t, 1)),
                    (("vt", t, 0), ("vt", t, 1)), tag="rld")
            for ach in range(2):
                xn_chunk(s, ach)
                for dmc in range(DC):
                    DMA("G", wstr[:], wq_[dmc * 128:(dmc + 1) * 128, :],
                        ("wstr",), ("wstr",), tag="bwq")
                    for hdc in range(DC):
                        T(lambda E, d=dmc, hh=hdc: E.matmul(
                            pb[hh][:], wstr[:, hh * 128:(hh + 1) * 128],
                            xnfm[:, d, :], start=(d == 0),
                            stop=(d == DC - 1)),
                          (("wstr"), ) + xn_reads, (PSK[hdc],),
                          accum=True, tag="bqj")
                for hdc in range(DC):
                    V(lambda E, hh=hdc: E.tensor_scalar(
                        putm[:], pb[hh][:], bqfm[:, hh:hh + 1], None,
                        ALU.add),
                      (PSK[hdc], "bqf"), ("putm",), tag="bqb")
                    V(lambda E: E.tensor_scalar(phm[:], putm[:],
                                                0.0, None, ALU.min),
                      ("putm",), ("trm",), tag="bqm")
                    S(lambda E: E.activation(phe[:],
                                             phm[:], AF.Exp,
                                             bias=0.0, scale=1.0),
                      ("trm",), ("tre",), tag="bqe")
                    V(lambda E, hh=hdc, aa=ach: E.scalar_tensor_tensor(
                        phiqF[:, hh, aa * 512:(aa + 1) * 512], putm[:], 0.0,
                        phe[:], ALU.max, ALU.add),
                      ("putm", "tre"), (("pq", hdc, ach),), tag="bqx")
            V(lambda E, ss=s: E.tensor_copy(kvst[:], kvpref[:, ss, :]),
              (("kvp", s),), ("kvst",))
            state_bank_open(6, "s2")
            state_bank_open(7, "s2")
            for n in range(BT):
                tp8(lambda d: phikT[:, n, d * 128:(d + 1) * 128],
                    (("pk", n, 0), ("pk", n, 1)),
                    (lambda d: pkF[:, d, :]),
                    (lambda dc: ("pkf", dc)), BF16, "pk2")
                for h in range(NH):
                    bb = h // 4
                    cc = (h % 4) * 128
                    p0 = (h % 2) * 64
                    dq = h // 2
                    T(lambda E, b2=bb, c2=cc, pp=p0, d=dq, nn=n: E.matmul(
                        pb[b2][:, c2:c2 + 128], pkF[pp:pp + 64, d, :],
                        phiqF[pp:pp + 64, d, nn * 128:(nn + 1) * 128],
                        start=True, stop=True),
                      (("pkf", dq), ("pq", dq, n // 4)),
                      (PSK[bb],), accum=True, tag="A")
                for bb in range(4):
                    V(lambda E, b2=bb: E.tensor_tensor(
                        amk[:, b2 * 512:(b2 + 1) * 512], pb[b2][:],
                        maskW[:], ALU.mult),
                      (PSK[bb], ("mW", 0), ("mW", 1), ("mW", 2),
                       ("mW", 3)),
                      (("amk", bb),), tag="Am")
                NB_OF = [(0, 0, 7), (1, 7, 7), (2, 14, 2)]
                for bb, h0, cnt in NB_OF:
                    for i in range(cnt):
                        h = h0 + i
                        cc = i * 65
                        p0 = (h % 2) * 64
                        dq = h // 2
                        hk = (h * 64) // 512
                        T(lambda E, b2=bb, c2=cc, hh=h, nn=n, i2z=i: E.matmul(
                            pb[b2][:, c2:c2 + 64],
                            amk[:, ((hh // 4) * 512 + (hh % 4) * 128):
                                ((hh // 4) * 512 + (hh % 4) * 128 + 128)],
                            vtm[:, nn, hh * 64:(hh + 1) * 64],
                            start=(i2z == 0), stop=False),
                          (("amk", h // 4), ("vt", n, hk)),
                          (PSK[bb],), accum=True, tag="nmv")
                        T(lambda E, b2=bb, c2=cc, hh=h, nn=n: E.matmul(
                            pb[b2][:, c2 + 64:c2 + 65],
                            amk[:, ((hh // 4) * 512 + (hh % 4) * 128):
                                ((hh // 4) * 512 + (hh % 4) * 128 + 128)],
                            onesb[:], start=False, stop=False),
                          (("amk", h // 4), "onesb"),
                          (PSK[bb],), accum=True, tag="nmd")
                        T(lambda E, b2=bb, c2=cc, hh=h, pp=p0, d=dq, nn=n:
                          E.matmul(
                              pb[b2][:, c2:c2 + 65],
                              phiqF[pp:pp + 64, d,
                                    nn * 128:(nn + 1) * 128],
                              kvst[pp:pp + 64, (hh // 2) * 65:
                                   (hh // 2) * 65 + 65],
                              start=False, stop=(hh == h0 + cnt - 1)),
                          (("pq", dq, n // 4), "kvst"),
                          (PSK[bb],), accum=True, tag="nmp")
                for bb, h0, cnt in NB_OF:
                    V(lambda E, b2=bb, hh=h0, c2=cnt: E.tensor_scalar(
                        r16[:, hh:hh + c2],
                        pb[b2][:, 0:c2 * 65].rearrange(
                            "p (a b) -> p a b", b=65)[:, :, 64], EPS_LA,
                        None, ALU.add),
                      (PSK[bb],), (("r16", bb),), tag="den")
                V(lambda E: E.reciprocal(r16[:], r16[:]),
                  (("r16", 0), ("r16", 1), ("r16", 2)), ("r16r",), tag="rcp")
                for bb, h0, cnt in NB_OF:
                    for i in range(cnt):
                        h = h0 + i
                        V(lambda E, b2=bb, i2=i, hh=h, nn=n: E.tensor_scalar(
                            otm[:, nn % 4, hh * 64:(hh + 1) * 64],
                            pb[b2][:, i2 * 65:i2 * 65 + 64],
                            r16[:, hh:hh + 1], None, ALU.mult),
                          (PSK[bb], "r16r"),
                          (("otm", n % 4, h),), tag="oev")
                for h in range(NH):
                    bb = 6 + h // 8
                    p0 = (h % 2) * 64
                    c0 = ((h % 8) // 2) * 65
                    hk = (h * 64) // 512
                    T(lambda E, hh=h, nn=n, b2=bb, pp=p0, cc=c0: E.matmul(
                        pb[b2][pp:pp + 64, cc:cc + 64],
                        phikT[:, nn, hh * 64:(hh + 1) * 64],
                        vtm[:, nn, hh * 64:(hh + 1) * 64],
                        start=False, stop=False, skip_group_check=True),
                      (("pk", n, hk), ("vt", n, hk)),
                      (PSK[bb],), accum=True, tag="s2")
                    T(lambda E, hh=h, nn=n, b2=bb, pp=p0, cc=c0: E.matmul(
                        pb[b2][pp:pp + 64, cc + 64:cc + 65],
                        phikT[:, nn, hh * 64:(hh + 1) * 64], onesb[:],
                        start=False, stop=False, skip_group_check=True),
                      (("pk", n, hk), "onesb"),
                      (PSK[bb],), accum=True, tag="s2k")
                for half in range(2):
                    bb = 6 + half
                    V(lambda E, b2=bb, hh=half, ss=s: E.tensor_tensor(
                        kvst[:, hh * 260:(hh + 1) * 260], pb[b2][:, 0:260],
                        kvpref[:, ss, hh * 260:(hh + 1) * 260], ALU.add),
                      (PSK[bb], ("kvp", s)), ("kvst",), tag="kvu")
                if n == BT - 1:
                    state_bank_close(6, "s2")
                    state_bank_close(7, "s2")
                if n % 4 == 3:
                    ach = n // 4
                    for k in range(4):
                        tp8(lambda d, kk=k: otm[:, kk, d * 128:(d + 1) * 128],
                            tuple(("otm", k, h) for h in range(NH)),
                            (lambda d: ofm[:, d, :]),
                            (lambda dc: ("ofm", dc)), BF16, "otp")
                        for dmh in range(2):
                            b = 4 + dmh
                            for hdc in range(DC):
                                T(lambda E, d=hdc, m=dmh, bb=b: E.matmul(
                                    pb[bb][:], ofm[:, d, :],
                                    wob[:, d, m * 512:(m + 1) * 512],
                                    start=(d == 0), stop=(d == DC - 1)),
                                  (("ofm", hdc), ("wob", hdc)),
                                  (PSK[b],), accum=True, tag="woj")
                            rt = s * BT + ach * 4 + k
                            V(lambda E, bb=b, m=dmh: E.tensor_tensor(
                                putm[:], pb[bb][:],
                                bobc[:, m * 512:(m + 1) * 512], ALU.add),
                              (PSK[b], "bob"), ("putm",), tag="wob")
                            V(lambda E, r=rt, m=dmh: E.tensor_tensor(
                                resid[:, r, m * 512:(m + 1) * 512],
                                resid[:, r, m * 512:(m + 1) * 512],
                                putm[:], ALU.add),
                              ("putm", ("resid", rt)), (("resid", rt),),
                              tag="wor")
    ph2.close()
    sch.barrier()
    if debug >= 3:
        for t in range(2 * BT):
            DMA("Y", dbg[2, t * 128:(t + 1) * 128, :], resid[:, t, :],
                (("resid", t),), (("dbg", 2, t),), tag="dbg")
        sch.barrier()

    ffn_sublayer(1)

    sch.emit("Y", lambda E: E.nop(),
             tuple(("out", t) for t in range(2 * BT)), (), tag="fin")

    with es, nc.allow_non_contiguous_dma(
            reason="tiny one-time vector loads (bias/conv rearranges)"), \
            nc.Block() as block:
        sch.replay(block, sems)
    return nc


def _stage_inputs(inputs):
    x = np.ascontiguousarray(
        np.asarray(inputs["L0_attn_a_x"], dtype=np.float32).reshape(T_TOT, D))
    xt = x.reshape(NB, BT, 128, D)

    common = dict(
        w1=np.ascontiguousarray(np.asarray(inputs["L0_ffn_w1"], np.float32)),
        w2=np.ascontiguousarray(np.asarray(inputs["L0_ffn_w2"], np.float32)),
        conv_w=np.ascontiguousarray(
            np.asarray(inputs["L0_ffn_conv_w"], np.float32).reshape(HID, 3)),
        conv_b=np.ascontiguousarray(
            np.asarray(inputs["L0_ffn_conv_b"], np.float32)),
        wq=np.ascontiguousarray(np.asarray(inputs["L0_attn_b_Wq"], np.float32)),
        wk=np.ascontiguousarray(np.asarray(inputs["L0_attn_b_Wk"], np.float32)),
        wv=np.ascontiguousarray(np.asarray(inputs["L0_attn_b_Wv"], np.float32)),
        wo=np.ascontiguousarray(np.asarray(inputs["L0_attn_b_Wo"], np.float32)),
        bq=np.ascontiguousarray(np.asarray(inputs["L0_attn_b_bq"], np.float32)),
        bk=np.ascontiguousarray(np.asarray(inputs["L0_attn_b_bk"], np.float32)),
        bvb=np.ascontiguousarray(np.asarray(inputs["L0_attn_b_bv"], np.float32)),
        bo=np.ascontiguousarray(np.asarray(inputs["L0_attn_b_bo"], np.float32)),
        bv_q=np.ascontiguousarray(np.asarray(inputs["L0_attn_a_bv_q"], np.float32)),
        bv_k=np.ascontiguousarray(np.asarray(inputs["L0_attn_a_bv_k"], np.float32)),
        bv_v=np.ascontiguousarray(np.asarray(inputs["L0_attn_a_bv_v"], np.float32)),
    )

    in_maps = []
    for c in range(NCORES):
        bl, bh = c, 15 - c
        visits = [(0, kb) for kb in range(bl)] + \
                 [(1, kb) for kb in range(bh)] + [(0, bl), (1, bh)]
        assert len(visits) == NVIS
        kv = np.stack([xt[kb] for (_, kb) in visits])
        ws = np.zeros(NVIS * 2, np.float32)
        for v, (slot, _) in enumerate(visits):
            ws[2 * v + slot] = 1.0
        mb = np.stack([xt[bl], xt[bh]])
        hs = np.zeros((4 * NCORES, 4), np.float32)
        halo_toks = [bl * BLK - 1, (bl + 1) * BLK, bh * BLK - 1,
                     (bh + 1) * BLK]
        for j, tk in enumerate(halo_toks):
            if tk < 0 or tk >= T_TOT:
                continue
            b = tk // BLK
            r = b if b <= 7 else 15 - b
            so = 0 if b == r else 1
            first = (tk % BLK == 0)
            row = 4 * r + 2 * so + (0 if first else 1)
            hs[row, j] = 1.0
        sw = np.zeros(NB * 2, np.float32)
        for bidx in range(NB):
            blkid = (bidx // 2) if bidx % 2 == 0 else 15 - (bidx // 2)
            if blkid < bl:
                sw[2 * bidx + 0] = 1.0
            if blkid < bh:
                sw[2 * bidx + 1] = 1.0
        in_maps.append(dict(common, kvis=np.ascontiguousarray(kv),
                            myblk=np.ascontiguousarray(mb), wsel=ws,
                            halosel=hs, stw=sw))
    return in_maps


_NC_CACHE = {}


def kernel(**inputs):
    debug = int(inputs.pop("_debug", 0))
    if debug not in _NC_CACHE:
        _NC_CACHE[debug] = build_program(debug)
    nc = _NC_CACHE[debug]
    in_maps = _stage_inputs(inputs)
    res = run_bass_kernel_spmd(nc, in_maps, core_ids=list(range(NCORES)))
    out = np.zeros((T_TOT, D), np.float32)
    for c in range(NCORES):
        y = np.asarray(res.results[c]["y"]).reshape(2 * BLK, D)
        out[c * BLK:(c + 1) * BLK] = y[0:BLK]
        out[(15 - c) * BLK:(16 - c) * BLK] = y[BLK:2 * BLK]
    if debug:
        nd = min(debug, 3)
        dbgs = [np.asarray(res.results[c]["dbg"]).reshape(nd, 2 * BLK, D)
                for c in range(NCORES)]
        full = np.zeros((nd, T_TOT, D), np.float32)
        for c in range(NCORES):
            full[:, c * BLK:(c + 1) * BLK] = dbgs[c][:, 0:BLK]
            full[:, (15 - c) * BLK:(16 - c) * BLK] = dbgs[c][:, BLK:2 * BLK]
        kernel._dbg = full
        if debug >= 4:
            kernel._dbg2 = [np.asarray(res.results[c]["dbg2"])
                            .reshape(16, 128, 8192) for c in range(NCORES)]
    return out.reshape(1, T_TOT, D)


# version 21 (final)
# speedup vs baseline: 1.0103x; 1.0042x over previous
import numpy as np
import ml_dtypes
from contextlib import ExitStack

import concourse.bass as bass
import concourse.mybir as mybir
from concourse.bass_utils import run_bass_kernel_spmd

F32 = mybir.dt.float32
BF16 = mybir.dt.bfloat16
FP8 = mybir.dt.float8e4
I32 = mybir.dt.int32
AF = mybir.ActivationFunctionType
ALU = mybir.AluOpType
AX = mybir.AxisListType.X

T_TOT = 16384
D = 1024
DC = 8
NB = 16
BLK = 1024
BT = 8
NCORES = 8
NVIS = 17
HID = 4096
HC = 32
NH = 16
HD = 64
EPS_LN = 1e-5
EPS_LA = 1e-6
INV127 = 1.0 / 127.0
INVD = 1.0 / D


class Sched:
    ENGS = ("T", "V", "S", "G", "Y")

    def __init__(self):
        self.ops = {e: [] for e in self.ENGS}
        self.last_writer = {}
        self.readers = {}
        self.accum_open = {}
        self.clock = {e: {} for e in self.ENGS}
        self.last_dma = {e: -1 for e in self.ENGS}
        self.fence_fns = {}
        self.fence_writes = {}

    def emit(self, eng, fn, reads=(), writes=(), accum=False, is_dma=False,
             is_coll=False, tag=""):
        idx = len(self.ops[eng])
        need = {}

        def add_edge(peng, pidx):
            if peng == eng and pidx == idx:
                return
            pop = self.ops[peng][pidx]
            async_done = pop["is_dma"] or pop["is_coll"]
            if pop["is_dma"] and not pop["is_coll"]:
                pidx = self.last_dma[peng]
                pop = self.ops[peng][pidx]
            if not async_done:
                key = (peng, "c")
            else:
                key = (peng, "Gl" if pop["is_coll"] else "d")
            if self.clock[eng].get(key, -1) >= pidx:
                return
            if need.get(key, -1) < pidx:
                need[key] = pidx

        for k in reads:
            w = self.last_writer.get(k)
            if w is not None:
                add_edge(*w)
        for k in writes:
            w = self.last_writer.get(k)
            if w is not None:
                if not (accum and w[0] == eng and self.accum_open.get(k)):
                    add_edge(*w)
            for r in self.readers.get(k, ()):
                add_edge(*r)
        for (peng, kind), pidx in need.items():
            self.ops[peng][pidx]["inc"] = True
            pc = self.ops[peng][pidx]["clock"]
            for k2, v2 in pc.items():
                if self.clock[eng].get(k2, -1) < v2:
                    self.clock[eng][k2] = v2
            if self.clock[eng].get((peng, kind), -1) < pidx:
                self.clock[eng][(peng, kind)] = pidx
        myclock = dict(self.clock[eng])
        myclock[(eng, "c")] = idx
        self.ops[eng].append(dict(fn=fn, waits=dict(need),
                                  inc=bool(is_dma),
                                  is_dma=is_dma, is_coll=is_coll,
                                  clock=myclock, tag=tag))
        if is_dma and not is_coll:
            self.last_dma[eng] = idx
        for k in reads:
            self.readers.setdefault(k, []).append((eng, idx))
        for k in writes:
            self.readers[k] = []
            self.last_writer[k] = (eng, idx)
            self.accum_open[k] = accum

    def barrier(self):
        key = ("__bar__", sum(len(self.ops[e]) for e in self.ENGS))
        for e in self.ENGS:
            fn = self.fence_fns.get(e, lambda E: E.nop())
            self.emit(e, fn, reads=(("fence", "init"),),
                      writes=((key, e), ("fence", e)) + self.fence_writes.get(e, ()))
            if self.last_dma[e] >= 0:
                op = self.ops[e][-1]
                if op["waits"].get((e, "d"), -1) < self.last_dma[e]:
                    op["waits"][(e, "d")] = self.last_dma[e]
                    self.ops[e][self.last_dma[e]]["inc"] = True
        allk = tuple((key, e2) for e2 in self.ENGS)
        for e in self.ENGS:
            self.emit(e, lambda E: E.nop(), reads=allk, writes=())

    @staticmethod
    def _sem_kind(e, op):
        if op["is_coll"]:
            return "Gl"
        if op["is_dma"]:
            return e + "d"
        return e + "c"

    def replay(self, block, sems):
        for e in self.ENGS:
            run = {}
            for op in self.ops[e]:
                if op["inc"]:
                    kind = self._sem_kind(e, op)
                    run[kind] = run.get(kind, 0) + \
                        (1 if op["is_coll"] else (16 if op["is_dma"] else 1))
                op["cum"] = dict(run)

        def section(eng_obj, e):
            for op in self.ops[e]:
                for (peng, _k), pidx in sorted(op["waits"].items()):
                    pop = self.ops[peng][pidx]
                    kind = self._sem_kind(peng, pop)
                    eng_obj.wait_ge(sems[kind], pop["cum"][kind])
                if op["inc"]:
                    kind = self._sem_kind(e, op)
                    step = 1 if op["is_coll"] else (16 if op["is_dma"] else 1)
                    prev = op["cum"][kind] - step
                    if prev > 0:
                        eng_obj.wait_ge(sems[kind], prev)
                inst = op["fn"](eng_obj)
                if op["inc"]:
                    kind = self._sem_kind(e, op)
                    inst.then_inc(
                        sems[kind],
                        1 if op["is_coll"] else (16 if op["is_dma"] else 1))

        @block.tensor
        def _(tensor):
            section(tensor, "T")

        @block.vector
        def _(vector):
            section(vector, "V")

        @block.scalar
        def _(scalar):
            section(scalar, "S")

        @block.gpsimd
        def _(gpsimd):
            section(gpsimd, "G")

        @block.sync
        def _(sync):
            section(sync, "Y")


def build_program(debug=0):
    nc = bass.Bass("TRN2", target_bir_lowering=False, debug=False)
    sch = Sched()

    def din(name, shape, dt=F32):
        return nc.dram_tensor(name, shape, dt, kind="ExternalInput").ap()

    kvis = din("kvis", [NVIS, BT, 128, D])
    myblk = din("myblk", [2, BT, 128, D])
    wsel = din("wsel", [NVIS * 2])
    halosel = din("halosel", [4 * NCORES, 4])
    stw = din("stw", [NB * 2])
    w1_ = din("w1", [HID, D])
    w2_ = din("w2", [D, HID])
    convw_ = din("conv_w", [HID, 3])
    convb_ = din("conv_b", [HID])
    wq_ = din("wq", [D, D])
    wk_ = din("wk", [D, D])
    wv_ = din("wv", [D, D])
    wo_ = din("wo", [D, D])
    bq_ = din("bq", [D])
    bk_ = din("bk", [D])
    bvb_ = din("bvb", [D])
    bo_ = din("bo", [D])
    bvq_ = din("bv_q", [D])
    bvk_ = din("bv_k", [D])
    bvv_ = din("bv_v", [D])

    out_y = nc.dram_tensor("y", [2 * BLK, D], F32, kind="ExternalOutput").ap()
    dbg = None
    dbg2 = None
    if debug:
        dbg = nc.dram_tensor("dbg", [min(debug, 3), 2 * BLK, D], F32,
                             kind="ExternalOutput").ap()
        if debug >= 4:
            dbg2 = nc.dram_tensor("dbg2", [16, 128, 8192], F32,
                                  kind="ExternalOutput").ap()

    w1q_s = nc.dram_tensor("w1q_s", [HC, 2, 128, 512], BF16).ap()
    w2q_s = nc.dram_tensor("w2q_s", [HID, D], BF16).ap()
    spill = nc.dram_tensor("spill", [2, 2, BT, 128, D], BF16).ap()
    ag_ei = [nc.dram_tensor(f"ag_ei{i}", [4, D], F32).ap() for i in range(2)]
    ag_eo = [nc.dram_tensor(f"ag_eo{i}", [4 * NCORES, D], F32,
                            addr_space="Shared").ap() for i in range(2)]
    ag_si = nc.dram_tensor("ag_si", [2, 128, 8 * 65], F32).ap()
    ag_so = nc.dram_tensor("ag_so", [NB, 128, 8 * 65], F32,
                           addr_space="Shared").ap()

    es = ExitStack()

    def sbt(name, shape, dt=F32):
        return es.enter_context(nc.sbuf_tensor(name, shape, dt))

    resid = sbt("resid", [128, 2 * BT, D])
    st_s = sbt("st_s", [128, NB])
    st_q = sbt("st_q", [128, NB])
    st_m = sbt("st_m", [128, NB])
    st_v = sbt("st_v", [128, NB])
    st_r = sbt("st_r", [128, NB])
    st_b = sbt("st_b", [128, NB])
    stb2 = sbt("stb2", [128, NB])
    ident8 = sbt("ident8", [128, 128], FP8)
    identb = sbt("identb", [128, 128], BF16)
    maskUT = sbt("maskUT", [128, 128], BF16)
    maskW = sbt("maskW", [128, 512], BF16)
    g_fm = sbt("g_fm", [128, DC])
    bvs_bc = sbt("bvs_bc", [128, D], BF16)
    wbuf = sbt("wbuf", [128, NVIS * 2])
    stwb = sbt("stwb", [128, NB * 2])
    mglob = sbt("mglob", [128, 4])
    cw0 = sbt("cw0", [128, HC])
    cw1 = sbt("cw1", [128, HC])
    cw2 = sbt("cw2", [128, HC])
    cbv = sbt("cbv", [128, HC])
    onesb = sbt("onesb", [128, 1], BF16)
    zrow = sbt("zrow", [1, 260], BF16)
    tiny = sbt("tiny", [128, 96])
    trash = sbt("trash", [128, D])
    tmprow = sbt("tmprow", [128, D])
    onesf = sbt("onesf", [1, 128])
    id4f = sbt("id4f", [4, 4])
    hsel = sbt("hsel", [32, 4])
    agbuf = sbt("agbuf", [32, D])
    halox = sbt("halox", [4, D])
    halor = sbt("halor", [128, DC, 4], BF16)
    qa = sbt("qa", [128, NB])
    qs1a = sbt("qs1a", [128, NB])
    qs1b = sbt("qs1b", [128, NB])
    qsi = sbt("qsi", [128, NB])
    s2t = sbt("s2t", [128, BT])
    s2b16 = sbt("s2b16", [128, BT], BF16)
    s2i = sbt("s2i", [128, BT])

    sems = {}
    for e in Sched.ENGS:
        sems[e + "c"] = es.enter_context(nc.semaphore("sm_" + e + "c"))
        sems[e + "d"] = es.enter_context(nc.semaphore("sm_" + e + "d"))
    sems["Gl"] = es.enter_context(nc.semaphore("sm_Gl"))

    pb = [es.enter_context(nc.psum_tensor(f"pb{i}", [128, 512], F32))
          for i in range(8)]
    PSK = [("pb", i) for i in range(8)]
    fence = sbt("fence", [1, 8])
    sch.emit("G", lambda E: E.memset(fence[:], 0.0), (),
             (("fence", "init"),))
    sch.fence_fns = {
        "V": lambda E: E.tensor_copy(fence[0:1, 0:1], fence[0:1, 1:2]),
        "S": lambda E: E.activation(fence[0:1, 2:3], fence[0:1, 3:4],
                                    AF.Copy, bias=0.0, scale=1.0),
        "G": lambda E: E.memset(fence[0:1, 4:5], 0.0),
        "T": lambda E: E.matmul(pb[0][0:1, 0:1], fence[0:1, 5:6],
                                fence[0:1, 6:7], start=True, stop=True),
    }
    sch.fence_writes = {"T": (("pb", 0),)}

    def V(fn, r, w, tag=""):
        sch.emit("V", fn, r, w, tag=tag)

    def S(fn, r, w, tag=""):
        sch.emit("S", fn, r, w, tag=tag)

    def T(fn, r, w, accum=False, tag=""):
        sch.emit("T", fn, r, w, accum=accum, tag=tag)

    def G(fn, r, w, tag=""):
        sch.emit("G", fn, r, w, tag=tag)

    def DMA(eng, out, in_, r, w, tag=""):
        sch.emit(eng, lambda E, o=out, i=in_: E.dma_start(out=o, in_=i),
                 r, w, is_dma=True, tag=tag)

    dump_n = [0]

    def dump(ap, reads):
        if dbg2 is None:
            return
        i = dump_n[0]
        dump_n[0] += 1
        cols = 1
        for sdim in ap.shape[1:]:
            cols *= sdim
        rows = ap.shape[0]
        DMA("Y", dbg2[i, 0:rows, 0:cols], ap, tuple(reads) + (("dbg2", i),),
            (("dbg2", i),), tag="dump")

    def ln_sq_tile(tile_ap, rkeys, j, tag="", base=0):
        S(lambda E, a=tile_ap, o=st_q[:, base + j:base + j + 1]:
          E.activation(trash[:], a, AF.Square, bias=0.0, scale=1.0,
                       accum_out=o),
          tuple(rkeys), (("st_q", base + j), "trash"), tag)

    def ln_sums_tile(tile_ap, rkeys, j, tag="", base=0, sum_on_scalar=False,
                     square=True):
        if sum_on_scalar:
            S(lambda E, a=tile_ap, o=st_s[:, base + j:base + j + 1]:
              E.activation(trash[:], a, AF.Identity, bias=0.0, scale=1.0,
                           accum_out=o),
              tuple(rkeys), (("st_s", base + j), "trash"), tag)
        else:
            V(lambda E, a=tile_ap, o=st_s[:, base + j:base + j + 1]:
              E.tensor_reduce(o, a, AX, ALU.add),
              tuple(rkeys), (("st_s", base + j),), tag)
        if square:
            ln_sq_tile(tile_ap, rkeys, j, tag, base)

    def ln_stats(tiles_ap, rkey, n, tag="", base=0, sum_on_scalar=False):
        for j in range(n):
            ln_sums_tile(tiles_ap(j), (rkey(j),), j, tag, base, sum_on_scalar)
        ln_finish(n, tag, base)

    def ln_finish_mean(n, tag="", base=0):
        cs = slice(base, base + n)
        rk = tuple(("st_s", base + j) for j in range(n))
        km = ("st_m", base); kb = ("st_b", base)
        V(lambda E: E.tensor_scalar(st_m[:, cs], st_s[:, cs], INVD, None,
                                    ALU.mult), rk, (km,), tag)
        V(lambda E: E.tensor_scalar(st_b[:, cs], st_m[:, cs], -1.0, None,
                                    ALU.mult), (km,), (kb,), tag)

    def ln_finish_rstd(n, tag="", base=0):
        cs = slice(base, base + n)
        qk = tuple(("st_q", base + j) for j in range(n))
        km = ("st_m", base); kv = ("st_v", base); kq2 = ("st_q2", base)
        kr = ("st_r", base); kb = ("st_b", base); kb2 = ("stb2", base)
        V(lambda E: E.tensor_tensor(st_v[:, cs], st_m[:, cs], st_m[:, cs],
                                    ALU.mult), (km,), (kv,), tag)
        V(lambda E: E.scalar_tensor_tensor(st_v[:, cs], st_q[:, cs], INVD,
                                           st_v[:, cs], ALU.mult,
                                           ALU.subtract),
          qk + (kv,), (kv,), tag)
        V(lambda E: E.tensor_scalar(st_v[:, cs], st_v[:, cs], EPS_LN, None,
                                    ALU.add), (kv,), (kv,), tag)
        S(lambda E: E.activation(st_q[:, cs], st_v[:, cs], AF.Sqrt,
                                 bias=0.0, scale=1.0),
          (kv,) + qk, (kq2,), tag)
        V(lambda E: E.reciprocal(st_r[:, cs], st_q[:, cs]),
          (kq2,), (kr,), tag)
        V(lambda E: E.tensor_tensor(st_q[:, cs], st_r[:, cs], st_r[:, cs],
                                    ALU.mult), (kr, kq2), (kq2,), tag)
        V(lambda E: E.tensor_tensor(st_q[:, cs], st_q[:, cs], st_v[:, cs],
                                    ALU.mult), (kq2, kv), (kq2,), tag)
        V(lambda E: E.tensor_scalar(st_q[:, cs], st_q[:, cs], -0.5, 1.5,
                                    ALU.mult, ALU.add), (kq2,), (kq2,), tag)
        V(lambda E: E.tensor_tensor(st_r[:, cs], st_r[:, cs], st_q[:, cs],
                                    ALU.mult), (kr, kq2), (kr,), tag)
        V(lambda E: E.tensor_tensor(stb2[:, cs], st_b[:, cs], st_r[:, cs],
                                    ALU.mult), (kb, kr), (kb2,), tag)

    def ln_finish(n, tag="", base=0):
        ln_finish_mean(n, tag, base)
        ln_finish_rstd(n, tag, base)

    def tp8(src_ap_fn, src_keys, dst_fn, dst_key_fn, dst_dt_evac, tag,
            scale_ap_fn=None, scale_key=None, banks=(4, 5),
            wide_dst_fn=None, wide_key_fn=None):
        for half in range(2):
            b = banks[half]
            idt = "id8" if dst_dt_evac == FP8 else "idb"
            idap = ident8 if dst_dt_evac == FP8 else identb
            for k in range(4):
                dc = half * 4 + k
                T(lambda E, d=dc, bb=b, kk=k, ia=idap: E.matmul(
                    pb[bb][:, kk * 128:(kk + 1) * 128], src_ap_fn(d), ia[:],
                    start=True, stop=True),
                  tuple(src_keys) + (idt,), (PSK[b],),
                  accum=True, tag=tag)
            if wide_dst_fn is not None:
                S(lambda E, h=half, bb=b: E.activation(
                    wide_dst_fn(h), pb[bb][:], AF.Copy, bias=0.0, scale=1.0),
                  (PSK[b],), (wide_key_fn(half),), tag=tag)
                continue
            for k in range(4):
                dc = half * 4 + k
                if scale_ap_fn is not None:
                    S(lambda E, d=dc, bb=b, kk=k: E.activation(
                        dst_fn(d), pb[bb][:, kk * 128:(kk + 1) * 128],
                        AF.Copy, bias=0.0, scale=scale_ap_fn(d)),
                      (PSK[b], scale_key), (dst_key_fn(dc),),
                      tag=tag)
                else:
                    S(lambda E, d=dc, bb=b, kk=k: E.activation(
                        dst_fn(d), pb[bb][:, kk * 128:(kk + 1) * 128],
                        AF.Copy, bias=0.0, scale=1.0),
                      (PSK[b],), (dst_key_fn(dc),), tag=tag)

    G(lambda E: E.memset(id4f[:], 0.0), (), ("id4f",))
    G(lambda E: E.affine_select(out=id4f[:], in_=id4f[:],
                                compare_op=ALU.not_equal, fill=1.0, base=0,
                                pattern=[[-1, 4]], channel_multiplier=1),
      ("id4f",), ("id4f",))
    for name, buf in (("id8", ident8), ("idb", identb)):
        G(lambda E, b=buf: E.memset(b[:], 0.0), (), (name,))
        G(lambda E, b=buf: E.affine_select(
            out=b[:], in_=b[:], compare_op=ALU.not_equal, fill=1.0, base=0,
            pattern=[[-1, 128]], channel_multiplier=1), (name,), (name,))
    G(lambda E: E.memset(maskUT[:], 0.0), (), ("mUT",))
    G(lambda E: E.affine_select(out=maskUT[:], in_=maskUT[:],
                                compare_op=ALU.is_gt, fill=1.0, base=0,
                                pattern=[[-1, 128]], channel_multiplier=1),
      ("mUT",), ("mUT",))
    for k in range(4):
        V(lambda E, kk=k: E.tensor_copy(maskW[:, kk * 128:(kk + 1) * 128],
                                        maskUT[:]), ("mUT",), (("mW", k),))
    G(lambda E: E.memset(onesb[:], 1.0), (), ("onesb",))
    G(lambda E: E.memset(zrow[:], 0.0), (), ("zrow",))
    G(lambda E: E.memset(onesf[:], 1.0), (), ("ones1f",))

    DMA("Y", tiny[:, 0:DC], bvq_.rearrange("(c p) -> p c", p=128), (),
        (("tiny", 0),))
    DMA("Y", tiny[:, 8:8 + DC], bvk_.rearrange("(c p) -> p c", p=128), (),
        (("tiny", 1),))
    S(lambda E: E.activation(tiny[:, 16:16 + DC], tiny[:, 0:DC], AF.Sign,
                             bias=0.0, scale=1.0), (("tiny", 0),),
      (("tiny", 2),))
    S(lambda E: E.activation(tiny[:, 24:24 + DC], tiny[:, 8:8 + DC], AF.Sign,
                             bias=0.0, scale=1.0), (("tiny", 1),),
      (("tiny", 3),))
    V(lambda E: E.tensor_tensor(g_fm[:], tiny[:, 16:16 + DC],
                                tiny[:, 24:24 + DC], ALU.mult),
      (("tiny", 2), ("tiny", 3)), ("g_fm",))
    DMA("Y", trash[:], bvv_[None, :].partition_broadcast(128), ("trash",),
        ("trash",))
    S(lambda E: E.activation(bvs_bc[:], trash[:], AF.Sign, bias=0.0,
                             scale=1.0), ("trash",), ("bvs",))
    DMA("Y", wbuf[:], wsel[None, :].partition_broadcast(128), (), ("wbuf",))
    DMA("Y", stwb[:], stw[None, :].partition_broadcast(128), (), ("stwb",))
    DMA("G", hsel[:], halosel, (), ("hsel",))

    V(lambda E: E.memset(tiny[:, 40:42], 0.0), (), (("tiny", 40), ("tiny", 41)))
    with nc.sbuf_tensor("wmA", [128, 2, D], F32) as wmA, \
         nc.sbuf_tensor("wmB", [128, 2, D], F32) as wmB, \
         nc.sbuf_tensor("wmC", [128, 2, D], F32) as wmC, \
         nc.sbuf_tensor("wmD", [128, 2, D], F32) as wmD, \
         nc.sbuf_tensor("wmE", [128, 2, D], F32) as wmE, \
         nc.sbuf_tensor("wmF", [128, 2, D], F32) as wmF:
        wmbufs = ((wmA, "wmA", "Y"), (wmB, "wmB", "S"),
                  (wmC, "wmC", "G"), (wmD, "wmD", "Y"),
                  (wmE, "wmE", "S"), (wmF, "wmF", "G"))
        srcs = [w1_[hc2 * 256:(hc2 + 1) * 256, :]
                .rearrange("(a p) c -> p a c", p=128)
                for hc2 in range(HC // 2)]
        srcs += [w2_[dmc * 128:(dmc + 1) * 128, p2 * 2 * D:(p2 + 1) * 2 * D]
                 .rearrange("p (a c) -> p a c", c=D)
                 for dmc in range(DC) for p2 in range(2)]
        for wmk, sap in enumerate(srcs):
            wi = 0 if wmk < HC // 2 else 1
            buf, bkey, q = wmbufs[wmk % 6]
            DMA(q, buf[:], sap, (bkey,), (bkey,), tag="wm")
            V(lambda E, b=buf: E.tensor_reduce(
                tiny[:, 44:45], b[:].rearrange("p a c -> p (a c)"), AX,
                ALU.add, apply_absolute_value=True),
              (bkey,), (("tiny", 44),), "wm")
            V(lambda E, c=40 + wi: E.tensor_tensor(
                tiny[:, c:c + 1], tiny[:, c:c + 1], tiny[:, 44:45],
                ALU.add),
              (("tiny", 40 + wi), ("tiny", 44)),
              (("tiny", 40 + wi),), "wm")
    sch.barrier()
    G(lambda E: E.memset(tiny[:, 46:47], 1.0), (), (("tiny", 46),))
    T(lambda E: E.matmul(pb[0][0:2, 0:1], tiny[:, 40:42], tiny[:, 46:47],
                         start=True, stop=True),
      (("tiny", 40), ("tiny", 41), ("tiny", 46)), (PSK[0],))
    V(lambda E: E.tensor_scalar(tiny[0:2, 48:49], pb[0][0:2, 0:1],
                                1.0 / (HID * D), 1e-5, ALU.mult, ALU.max),
      (PSK[0],), (("tiny", 48),))
    V(lambda E: E.reciprocal(tiny[0:2, 49:50], tiny[0:2, 48:49]),
      (("tiny", 48),), (("tiny", 49),))
    T(lambda E: E.matmul(pb[2][0:1, 0:2], tiny[0:2, 48:49], id4f[0:2, 0:2],
                         start=True, stop=True),
      (("tiny", 48), "id4f"), (PSK[2],))
    T(lambda E: E.matmul(pb[2][0:1, 4:6], tiny[0:2, 49:50], id4f[0:2, 0:2],
                         start=True, stop=True),
      (("tiny", 49), "id4f"), (PSK[2],), accum=True)
    V(lambda E: E.tensor_copy(tiny[0:1, 52:54], pb[2][0:1, 0:2]),
      (PSK[2],), (("tiny", 52),))
    V(lambda E: E.tensor_copy(tiny[0:1, 54:56], pb[2][0:1, 4:6]),
      (PSK[2],), (("tiny", 54),))
    T(lambda E: E.matmul(pb[3][:, 0:4], onesf[:], tiny[0:1, 52:56],
                         start=True, stop=True),
      (("tiny", 52), ("tiny", 54), "ones1f"), (PSK[3],))
    V(lambda E: E.tensor_copy(mglob[:], pb[3][:, 0:4]),
      (PSK[3],), ("mglob",))

    for k, buf in ((0, cw0), (1, cw1), (2, cw2)):
        DMA("Y", buf[:], convw_.rearrange("(c p) t -> p c t", p=128)[:, :, k],
            (), (("cw", k),))
        V(lambda E, b=buf, kk=k: E.tensor_scalar(b[:], b[:], mglob[:, 0:1],
                                                 None, ALU.mult),
          (("cw", k), "mglob"), (("cw", k),))
    DMA("Y", cbv[:], convb_.rearrange("(c p) -> p c", p=128), (), ("cbv",))

    with nc.sbuf_tensor("wload0", [128, 2, D], F32) as wl0, \
         nc.sbuf_tensor("wload1", [128, 2, D], F32) as wl1, \
         nc.sbuf_tensor("wi32", [128, 2, D], I32) as wi32, \
         nc.sbuf_tensor("wtern0", [128, 2, D], BF16) as wt0, \
         nc.sbuf_tensor("wtern1", [128, 2, D], BF16) as wt1, \
         nc.sbuf_tensor("wtpA", [128, 512], BF16) as wtpA, \
         nc.sbuf_tensor("wtpB", [128, 512], BF16) as wtpB, \
         nc.sbuf_tensor("wtpC", [128, 512], BF16) as wtpC, \
         nc.sbuf_tensor("wtpD", [128, 512], BF16) as wtpD:
        wloads = (wl0, wl1)
        wterns = (wt0, wt1)
        wtps = (wtpA, wtpB, wtpC, wtpD)
        qn = [0]
        stq = ("S", "G")

        def quant_pair(src_ap, swcol, store_fns):
            sl = qn[0] % 2
            qn[0] += 1
            wload = wloads[sl]
            wtern = wterns[sl]
            DMA("Y", wload[:], src_ap, (("wld", sl),),
                (("wld", sl),), tag="wql")
            V(lambda E, c=swcol, b=wload: E.tensor_scalar(
                wi32[:].rearrange("p a c -> p (a c)"),
                b[:].rearrange("p a c -> p (a c)"),
                mglob[:, c:c + 1], None, ALU.mult),
              (("wld", sl), "mglob"), ("wi32",))
            V(lambda E, b=wtern: E.tensor_scalar(
                b[:].rearrange("p a c -> p (a c)"),
                wi32[:].rearrange("p a c -> p (a c)"), -1.0, 1.0,
                ALU.max, ALU.min),
              ("wi32",), (("wtn", sl),))
            for ch in range(2):
                for half in range(2):
                    b = 4 + half
                    wtp = wtps[ch * 2 + half]
                    for k in range(4):
                        dc = half * 4 + k
                        T(lambda E, d=dc, bb=b, kk=k, wt=wtern, cc=ch:
                          E.matmul(
                              pb[bb][:, kk * 128:(kk + 1) * 128],
                              wt[:, cc, d * 128:(d + 1) * 128], identb[:],
                              start=True, stop=True),
                          (("wtn", sl), "idb"), (PSK[b],), accum=True,
                          tag="wqt")
                    S(lambda E, bb=b, wp=wtp: E.activation(
                        wp[:], pb[bb][:], AF.Identity, bias=0.0, scale=1.0),
                      (PSK[b],), (("wtp", ch, half),))
                    store_fns[ch](half, wtp, ("wtp", ch, half),
                                  stq[(ch * 2 + half) % 2])

        def mk_st1(hh):
            def st1(half, wtp, wkey, q):
                DMA(q, w1q_s[hh, half], wtp[:], (wkey,),
                    (("w1qs", hh, half),), tag="w1st")
            return st1

        for hc2 in range(HC // 2):
            quant_pair(w1_[hc2 * 256:(hc2 + 1) * 256, :]
                       .rearrange("(a p) c -> p a c", p=128), 2,
                       (mk_st1(hc2 * 2), mk_st1(hc2 * 2 + 1)))

        def mk_st2(dd, pp):
            def st2(half, wtp, wkey, q):
                r0 = (pp * 8 + half * 4) * 128
                DMA(q, w2q_s[r0:r0 + 512, dd * 128:(dd + 1) * 128]
                    .rearrange("(a p) c -> p a c", p=128),
                    wtp[:].rearrange("p (a c) -> p a c", c=128),
                    (wkey,),
                    tuple(("w2qs", pp, half * 4 + k, dd)
                          for k in range(4)),
                    tag="w2st")
            return st2

        for dmc in range(DC):
            for p2 in range(2):
                quant_pair(w2_[dmc * 128:(dmc + 1) * 128,
                               p2 * 2 * D:(p2 + 1) * 2 * D]
                           .rearrange("p (a c) -> p a c", c=D), 3,
                           (mk_st2(dmc, p2 * 2), mk_st2(dmc, p2 * 2 + 1)))

    sch.barrier()

    ph1 = ExitStack()
    _p = lambda n, s, d=F32: ph1.enter_context(nc.sbuf_tensor(n, s, d))
    qLc = _p("qLc", [128, DC, BLK], FP8)
    qHc = _p("qHc", [128, DC, BLK], FP8)
    qmix = _p("qmix", [128, DC, BLK], FP8)
    kfm = [_p("kfm0", [128, BT, D], FP8), _p("kfm1", [128, BT, D], FP8)]
    xnk2 = [_p("xnk0", [128, BT, D], BF16), _p("xnk1", [128, BT, D], BF16)]
    x0k = _p("x0k", [128, BT, D], F32)
    attnT = _p("attnT", [128, BT, BLK], BF16)
    _sg = _p("sg80", [128, D], FP8)
    sg8 = [_sg, _sg]
    if True:
        for t in range(2 * BT):
            V(lambda E, tt=t: E.memset(resid[:, tt, :], 0.0), (),
              (("resid", t),))

        def emit_dmas(src_ap_fn):
            for t in range(BT):
                DMA("Y" if t % 2 == 0 else "G", x0k[:, t, :], src_ap_fn(t),
                    (("x0k", t),), (("x0k", t),), tag="kld")

        def emit_stats_sums(par, refine=False):
            base = par * BT
            for t in range(BT):
                ln_sums_tile(x0k[:, t, :], (("x0k", t),), t, "st", base,
                             False, square=False)
            ln_finish_mean(BT, "st", base)
            if refine:
                emit_refine(par)

        def emit_squares(par):
            base = par * BT
            for t in range(BT):
                ln_sq_tile(x0k[:, t, :], (("x0k", t),), t, "st", base)

        def emit_refine(par):
            base = par * BT
            for t in range(BT):
                S(lambda E, b=base, tt=t: E.activation(
                    trash[:], x0k[:, tt, :], AF.Identity,
                    bias=st_b[:, b + tt:b + tt + 1], scale=1.0),
                  (("x0k", t), ("st_b", base)), ("trash",), tag="cmp")
                V(lambda E: E.tensor_reduce(
                    tiny[:, 60:61], trash[:], AX, ALU.add),
                  ("trash",), (("tiny", 60),), tag="cmp")
                V(lambda E, tt=t, b=base: E.scalar_tensor_tensor(
                    st_b[:, b + tt:b + tt + 1], tiny[:, 60:61], -INVD,
                    st_b[:, b + tt:b + tt + 1], ALU.mult, ALU.add),
                  (("tiny", 60), ("st_b", base)), (("st_b", base),),
                  tag="cmp")

        def emit_stats_sign(par, refine=False):
            emit_stats_sums(par, refine)
            emit_squares(par)
            ln_finish_rstd(BT, "st", par * BT)

        def emit_sign_tp(dst, dstkey, gscale, par, tiles):
            base = par * BT
            sg = sg8[par]
            sgk = ("sg8",)
            for t in tiles:
                S(lambda E, tt=t, b=base, s2=sg: E.activation(
                    s2[:], x0k[:, tt, :], AF.Sign,
                    bias=st_b[:, b + tt:b + tt + 1], scale=1.0),
                  (("x0k", t), ("st_b", base)), (sgk,), tag="sgn")
                tp8(lambda d, s2=sg: s2[:, d * 128:(d + 1) * 128], (sgk,),
                    (lambda d, tt=t, d2=dst: d2[:, d, tt * 128:(tt + 1) * 128]),
                    (lambda dc, tt=t, dk=dstkey: dk + (dc, tt)), FP8,
                    "sg" + str(dstkey),
                    scale_ap_fn=(lambda d: g_fm[:, d:d + 1]) if gscale
                    else None,
                    scale_key="g_fm" if gscale else None)

        def emit_sign_tpk(par, tiles):
            base = par * BT
            sg = sg8[par]
            sgk = ("sg8",)
            kb2 = kfm[par]
            for t in tiles:
                S(lambda E, tt=t, b=base, s2=sg: E.activation(
                    s2[:], x0k[:, tt, :], AF.Sign,
                    bias=st_b[:, b + tt:b + tt + 1], scale=1.0),
                  (("x0k", t), ("st_b", base)), (sgk,), tag="sgn")
                tp8(lambda d, s2=sg: s2[:, d * 128:(d + 1) * 128], (sgk,),
                    None, None, FP8, "sgk",
                    wide_dst_fn=(lambda h, tt=t, kb=kb2:
                                 kb[:, tt, h * 512:(h + 1) * 512]),
                    wide_key_fn=(lambda h, tt=t, pp=par: ("kf", pp, tt, h)))

        def prep_partA(src_ap_fn, dst, dstkey, gscale, par, refine=False):
            emit_dmas(src_ap_fn)
            emit_stats_sign(par, refine)
            emit_sign_tp(dst, dstkey, gscale, par, range(BT))

        def prep_partK(src_ap_fn, par):
            emit_dmas(src_ap_fn)
            emit_stats_sign(par)
            emit_sign_tpk(par, range(BT))

        def emit_xn_tile(v, t):
            base = (v % 2) * BT
            xb = xnk2[v % 2]
            S(lambda E, tt=t, b=base, x2=xb: E.activation(
                x2[:, tt, :], x0k[:, tt, :], AF.Identity,
                bias=stb2[:, b + tt:b + tt + 1],
                scale=st_r[:, b + tt:b + tt + 1]),
              (("x0k", t), ("st_r", base), ("stb2", base)),
              (("xnk", v % 2, t),), tag="xnk")

        def emit_dma_tile(src_ap_fn, t):
            DMA("Y" if t % 2 == 0 else "G", x0k[:, t, :], src_ap_fn(t),
                (("x0k", t),), (("x0k", t),), tag="kld")

        def emit_sum_tile(par, t):
            ln_sums_tile(x0k[:, t, :], (("x0k", t),), t, "st", par * BT,
                         False, square=False)

        def emit_xn(v):
            for t in range(BT):
                emit_xn_tile(v, t)

        for s, qdst, qkey in ((0, qLc, ("qL",)), (1, qHc, ("qH",))):
            prep_partA(lambda t, s2=s: myblk[s2, t], qdst, qkey, True, s,
                       refine=True)

        qL_reads = tuple(("qL", dc, t) for dc in range(DC) for t in range(BT))
        qH_reads = tuple(("qH", dc, t) for dc in range(DC) for t in range(BT))

        def emit_qmix_piece(v, idx):
            wLap = wbuf[:, 2 * v:2 * v + 1]
            wHap = wbuf[:, 2 * v + 1:2 * v + 2]
            dc = idx // 2
            if idx % 2 == 0:
                V(lambda E, d=dc, w=wLap: E.tensor_scalar(
                    qmix[:, d, :], qLc[:, d, :], w, None, ALU.mult),
                  qL_reads + ("wbuf",), (("qmix", dc),), tag="qmx")
            else:
                V(lambda E, d=dc, w=wHap: E.scalar_tensor_tensor(
                    qmix[:, d, :], qHc[:, d, :], w, qmix[:, d, :],
                    ALU.mult, ALU.add),
                  qH_reads + ("wbuf", ("qmix", dc)), (("qmix", dc),),
                  tag="qmx")

        def emit_qmix(v):
            for idx in range(2 * DC):
                emit_qmix_piece(v, idx)

        def emit_scores(v, inject_fn=None):
            par = v % 2
            diag = v >= 15
            kbuf = kfm[par]
            sb_rot = 0
            nsteps = 12 if diag else 16
            for jc in range(BT):
                for ih in range(2):
                    if diag and jc > ih * 4 + 3:
                        continue
                    if inject_fn is not None:
                        t_inj = sb_rot - (nsteps - BT)
                        if t_inj >= 0:
                            inject_fn(t_inj)
                    b = sb_rot % 4
                    sb_rot += 1
                    for dc in range(DC):
                        T(lambda E, d=dc, j=jc, i=ih, bb=b, kb2=kbuf:
                          E.matmul(
                              pb[bb][:], kb2[:, j, d * 128:(d + 1) * 128],
                              qmix[:, d, i * 512:(i + 1) * 512],
                              start=(d == 0), stop=(d == DC - 1)),
                          (("kf", par, jc, dc // 4), ("qmix", dc)),
                          (PSK[b],), accum=True, tag="sc")
                    S(lambda E, j=jc, i=ih, bb=b: E.activation(
                        attnT[:, j, i * 512:(i + 1) * 512], pb[bb][:],
                        AF.Sigmoid, bias=0.0, scale=0.125),
                      (PSK[b],), (("aT", jc, ih),), tag="sig")
                    if diag:
                        for k in range(4):
                            ic = ih * 4 + k
                            if ic == jc:
                                V(lambda E, j=jc, c=ic: E.tensor_tensor(
                                    attnT[:, j, c * 128:(c + 1) * 128],
                                    attnT[:, j, c * 128:(c + 1) * 128],
                                    maskUT[:], ALU.mult),
                                  (("aT", jc, ih), "mUT"), (("aT", jc, ih),),
                                  tag="msk")
        def emit_av(v, inject_fn=None):
            par = v % 2
            diag = v >= 15
            wLap = wbuf[:, 2 * v:2 * v + 1]
            wHap = wbuf[:, 2 * v + 1:2 * v + 2]
            av_rot = 0
            for ic in range(BT):
                njc = (ic + 1) if diag else BT
                for dmh in range(2):
                    b = 6 + av_rot % 2
                    av_rot += 1
                    for jj in range(njc):
                        T(lambda E, j=jj, c=ic, m=dmh, bb=b, n=njc,
                          x2=xnk2[v % 2]:
                          E.matmul(
                              pb[bb][:], attnT[:, j, c * 128:(c + 1) * 128],
                              x2[:, j, m * 512:(m + 1) * 512],
                              start=(j == 0), stop=(j == n - 1)),
                          (("aT", jj, ic // 4), ("xnk", v % 2, jj)),
                          (PSK[b],), accum=True, tag="av")
                    for slot, wap in ((0, wLap), (1, wHap)):
                        rt = slot * BT + ic
                        V(lambda E, bb=b, r=rt, m=dmh, w=wap:
                          E.scalar_tensor_tensor(
                              resid[:, r, m * 512:(m + 1) * 512], pb[bb][:],
                              w, resid[:, r, m * 512:(m + 1) * 512],
                              ALU.mult, ALU.add),
                          ((PSK[b],) + ("wbuf", ("resid", rt))),
                          (("resid", rt),), tag="avac")
                    if inject_fn is not None:
                        inject_fn(av_rot - 1)

        prep_partK(lambda t: kvis[0, t], 0)
        emit_xn(0)
        emit_qmix(0)
        for t in range(BT):
            emit_dma_tile(lambda tt: kvis[1, tt], t)
            emit_sum_tile(1, t)
        for v in range(NVIS):
            nx = v + 1
            if nx < NVIS:
                ln_finish_mean(BT, "st", (nx % 2) * BT)
                inj_tp = lambda t, p=nx % 2: emit_sign_tpk(p, [t])
                work = []
                for i in range(8):
                    work.append(lambda i=i, n=nx: emit_qmix_piece(n, i))
                work.append(
                    lambda n=nx: ln_finish_rstd(BT, "st", (n % 2) * BT))
                for i in range(8, 16):
                    work.append(lambda i=i, n=nx: emit_qmix_piece(n, i))
                for t in range(BT):
                    def wtile(t=t, n=nx):
                        emit_xn_tile(n, t)
                        if n + 1 < NVIS:
                            emit_dma_tile(
                                lambda tt, v2=n + 1: kvis[v2, tt], t)
                            emit_sum_tile((n + 1) % 2, t)
                    work.append(wtile)
                done = [0]

                def inj_qm(slot, w=work, d=done):
                    want = (len(w) * (slot + 1) + 15) // 16
                    while d[0] < min(want, len(w)):
                        w[d[0]]()
                        d[0] += 1
            else:
                inj_tp = None
                inj_qm = None
            emit_scores(v, inject_fn=inj_tp)
            if nx < NVIS:
                emit_squares(nx % 2)
            emit_av(v, inject_fn=inj_qm)
            if nx < NVIS:
                inj_qm(15)

        def fold_dma(rt):
            sl = rt % 4
            DMA("Y" if rt % 2 == 0 else "G", x0k[:, sl, :],
                myblk[rt // BT, rt % BT],
                (("x0k", sl),), (("x0k", sl),), tag="fold")
        for rt in range(4):
            fold_dma(rt)
        for rt in range(2 * BT):
            sl = rt % 4
            V(lambda E, r=rt: E.tensor_tensor(tmprow[:], resid[:, r, :],
                                              bvs_bc[:], ALU.mult),
              (("resid", rt), "bvs"), ("tmprow",), tag="fold")
            V(lambda E, r=rt, s2=sl: E.tensor_tensor(
                resid[:, r, :], tmprow[:], x0k[:, s2, :], ALU.add),
              ("tmprow", ("x0k", sl)), (("resid", rt),), tag="fold")
            if rt + 4 < 2 * BT:
                fold_dma(rt + 4)

    ph1.close()
    sch.barrier()
    if debug >= 1:
        for t in range(2 * BT):
            DMA("Y", dbg[0, t * 128:(t + 1) * 128, :], resid[:, t, :],
                (("resid", t),), (("dbg", 0, t),), tag="dbg")
        sch.barrier()

    def ffn_sublayer(agi):
        phf = ExitStack()
        _q = lambda n, s, d=F32: phf.enter_context(nc.sbuf_tensor(n, s, d))
        hbuf = _q(f"hbuf{agi}", [128, HC, BLK + 2], BF16)
        r1fm = _q(f"r1fm{agi}", [128, DC, BLK + 2], BF16)
        wsh = _q(f"wsh{agi}", [128, 2048], BF16)
        wshb = _q(f"wshb{agi}", [128, 2048], BF16)
        wshs = (wsh, wshb)
        xq1 = _q(f"xq1{agi}", [128, D], BF16)
        fi32 = _q(f"fi32{agi}", [128, D], I32)
        utmp = _q(f"utmp{agi}", [128, BLK], F32)
        utmps = ((utmp, ("ut",)), (trash, ("trash",)))
        r2s = _q(f"r2s{agi}", [128, 512], BF16)
        r2sb = _q(f"r2sb{agi}", [128, 512], BF16)
        r2ss = (r2s, r2sb)
        amaxp = _q(f"amx{agi}", [128, BLK], BF16)
        aminp = _q(f"amn{agi}", [128, BLK], BF16)
        haloq = _q(f"hq{agi}", [4, D], BF16)
        haloi = _q(f"hi{agi}", [4, D], I32)
        s2bc = _q(f"s2bc{agi}", [128, BLK], BF16)
        if True:

            edges = [(0, 0, 0), (0, BT - 1, 127), (1, 0, 0), (1, BT - 1, 127)]
            for r, (s, t, p) in enumerate(edges):
                DMA("Y", ag_ei[agi][r:r + 1, :],
                    resid[p:p + 1, s * BT + t, :],
                    (("resid", s * BT + t),), (("aei", agi, r),), tag="edg")
            sch.emit("G", lambda E: E.collective_compute(
                "AllGather", ALU.bypass,
                replica_groups=[list(range(NCORES))],
                ins=[ag_ei[agi][:]], outs=[ag_eo[agi][:]]),
                tuple(("aei", agi, r) for r in range(4)), (("aeo", agi),),
                is_dma=True, is_coll=True, tag="AG")
            DMA("Y", agbuf[:], ag_eo[agi], (("aeo", agi), "agbuf"),
                ("agbuf",), tag="agl")
            ln_stats(lambda j: resid[:, j, :], lambda j: ("resid", j), NB,
                     "fst")
            for half in range(2):
                T(lambda E, m=half: E.matmul(
                    pb[6][0:4, 0:512], hsel[:],
                    agbuf[:, m * 512:(m + 1) * 512],
                    start=True, stop=True),
                  ("hsel", "agbuf"), (PSK[6],), tag="hx")
                V(lambda E, m=half: E.tensor_copy(
                    halox[0:4, m * 512:(m + 1) * 512], pb[6][0:4, 0:512]),
                  (PSK[6],), ("halox",))
            hx = halox[0:4, :]
            ti = lambda a, b: tiny[0:4, a:b]
            V(lambda E: E.tensor_reduce(ti(32, 33), hx, AX, ALU.add),
              ("halox",), (("hti", 32),))
            S(lambda E: E.activation(trash[0:4, :], hx, AF.Square, bias=0.0,
                                     scale=1.0, accum_out=ti(33, 34)),
              ("halox",), (("hti", 33), "trash"))
            V(lambda E: E.tensor_scalar(ti(34, 35), ti(32, 33), INVD, None,
                                        ALU.mult), (("hti", 32),),
              (("hti", 34),))
            V(lambda E: E.tensor_tensor(ti(35, 36), ti(34, 35), ti(34, 35),
                                        ALU.mult), (("hti", 34),),
              (("hti", 35),))
            V(lambda E: E.scalar_tensor_tensor(ti(35, 36), ti(33, 34), INVD,
                                               ti(35, 36), ALU.mult,
                                               ALU.subtract),
              (("hti", 33), ("hti", 35)), (("hti", 35),))
            V(lambda E: E.tensor_scalar(ti(35, 36), ti(35, 36), EPS_LN,
                                        None, ALU.add),
              (("hti", 35),), (("hti", 35),))
            S(lambda E: E.activation(ti(35, 36), ti(35, 36), AF.Sqrt,
                                     bias=0.0, scale=1.0),
              (("hti", 35),), (("hti", 35),))
            V(lambda E: E.reciprocal(ti(36, 37), ti(35, 36)),
              (("hti", 35),), (("hti", 36),))
            V(lambda E: E.tensor_reduce(ti(37, 38), hx, AX, ALU.max),
              ("halox",), (("hti", 37),))
            V(lambda E: E.tensor_reduce(ti(38, 39), hx, AX, ALU.min),
              ("halox",), (("hti", 38),))
            V(lambda E: E.tensor_tensor(ti(37, 38), ti(37, 38), ti(34, 35),
                                        ALU.subtract),
              (("hti", 37), ("hti", 34)), (("hti", 37),))
            V(lambda E: E.tensor_tensor(ti(38, 39), ti(34, 35), ti(38, 39),
                                        ALU.subtract),
              (("hti", 38), ("hti", 34)), (("hti", 38),))
            V(lambda E: E.tensor_tensor(ti(37, 38), ti(37, 38), ti(38, 39),
                                        ALU.max),
              (("hti", 37), ("hti", 38)), (("hti", 37),))
            V(lambda E: E.tensor_tensor(ti(37, 38), ti(37, 38), ti(36, 37),
                                        ALU.mult),
              (("hti", 37), ("hti", 36)), (("hti", 37),))
            V(lambda E: E.tensor_scalar(ti(37, 38), ti(37, 38), 1e-5, None,
                                        ALU.max), (("hti", 37),),
              (("hti", 37),))
            V(lambda E: E.reciprocal(ti(39, 40), ti(37, 38)),
              (("hti", 37),), (("hti", 39),))
            V(lambda E: E.tensor_scalar(ti(39, 40), ti(39, 40), 127.0, None,
                                        ALU.mult), (("hti", 39),),
              (("hti", 39),))
            V(lambda E: E.tensor_scalar(ti(41, 42), ti(37, 38), INV127, None,
                                        ALU.mult), (("hti", 37),),
              (("hti", 41),))
            V(lambda E: E.tensor_tensor(ti(42, 43), ti(36, 37), ti(39, 40),
                                        ALU.mult),
              (("hti", 36), ("hti", 39)), (("hti", 42),))
            V(lambda E: E.scalar_tensor_tensor(ti(43, 44), ti(34, 35), -1.0,
                                               ti(42, 43), ALU.mult,
                                               ALU.mult),
              (("hti", 34), ("hti", 42)), (("hti", 43),))
            V(lambda E: E.tensor_scalar(haloi[:], hx, ti(42, 43), ti(43, 44),
                                        ALU.mult, ALU.add),
              ("halox", ("hti", 42), ("hti", 43)), ("haloi",))
            V(lambda E: E.tensor_scalar(haloq[:], haloi[:], ti(41, 42), None,
                                        ALU.mult),
              ("haloi", ("hti", 41)), ("haloq",))
            for dc in range(DC):
                T(lambda E, d=dc: E.matmul(
                    pb[7][:, d * 4:(d + 1) * 4],
                    haloq[0:4, d * 128:(d + 1) * 128], identb[0:4, 0:4],
                    start=(d == 0), stop=(d == DC - 1)),
                  ("haloq", "idb"), (PSK[7],), accum=True, tag="ht")
            V(lambda E: E.tensor_copy(
                halor[:].rearrange("p a b -> p (a b)"), pb[7][:, 0:32]),
              (PSK[7],), ("halor",))

            for s in range(2):
                for t in range(BT):
                    rt = s * BT + t
                    V(lambda E, r=rt: E.tensor_reduce(
                        tiny[:, 64:65], resid[:, r, :], AX, ALU.max),
                      (("resid", rt),), (("fti", 64),))
                    V(lambda E, r=rt: E.tensor_reduce(
                        tiny[:, 65:66], resid[:, r, :], AX, ALU.min),
                      (("resid", rt),), (("fti", 65),))
                    V(lambda E, r=rt: E.tensor_tensor(
                        tiny[:, 64:65], tiny[:, 64:65], st_m[:, r:r + 1],
                        ALU.subtract),
                      (("fti", 64), ("st_m", 0)), (("fti", 64),))
                    V(lambda E, r=rt: E.tensor_tensor(
                        tiny[:, 65:66], st_m[:, r:r + 1], tiny[:, 65:66],
                        ALU.subtract),
                      (("fti", 65), ("st_m", 0)), (("fti", 65),))
                    V(lambda E: E.tensor_tensor(
                        tiny[:, 64:65], tiny[:, 64:65], tiny[:, 65:66],
                        ALU.max),
                      (("fti", 64), ("fti", 65)), (("fti", 64),))
                    V(lambda E, r=rt: E.tensor_tensor(
                        qa[:, r:r + 1], tiny[:, 64:65], st_r[:, r:r + 1],
                        ALU.mult),
                      (("fti", 64), ("st_r", 0)), (("qa", rt),))
                    V(lambda E, r=rt: E.tensor_scalar(
                        qa[:, r:r + 1], qa[:, r:r + 1], 1e-5, None, ALU.max),
                      (("qa", rt),), (("qa", rt),))
                    V(lambda E, r=rt: E.reciprocal(tiny[:, 66:67],
                                                   qa[:, r:r + 1]),
                      (("qa", rt),), (("fti", 66),))
                    V(lambda E, r=rt: E.scalar_tensor_tensor(
                        qs1a[:, r:r + 1], tiny[:, 66:67], 127.0,
                        st_r[:, r:r + 1], ALU.mult, ALU.mult),
                      (("fti", 66), ("st_r", 0)), (("qs1a", rt),))
                    V(lambda E, r=rt: E.scalar_tensor_tensor(
                        qs1b[:, r:r + 1], st_m[:, r:r + 1], -1.0,
                        qs1a[:, r:r + 1], ALU.mult, ALU.mult),
                      (("st_m", 0), ("qs1a", rt)), (("qs1b", rt),))
                    V(lambda E, r=rt: E.tensor_scalar(
                        qsi[:, r:r + 1], qa[:, r:r + 1], INV127, None,
                        ALU.mult), (("qa", rt),), (("qsi", rt),))
                    V(lambda E, r=rt: E.tensor_scalar(
                        fi32[:], resid[:, r, :], qs1a[:, r:r + 1],
                        qs1b[:, r:r + 1], ALU.mult, ALU.add),
                      (("resid", rt), ("qs1a", rt), ("qs1b", rt)), ("fi32",))
                    V(lambda E, r=rt: E.tensor_scalar(
                        xq1[:], fi32[:], qsi[:, r:r + 1], None, ALU.mult),
                      ("fi32", ("qsi", rt)), ("xq1",))
                    tp8(lambda d: xq1[:, d * 128:(d + 1) * 128], ("xq1",),
                        (lambda d, tt=t: r1fm[:, d,
                                              1 + tt * 128:1 + (tt + 1) * 128]),
                        (lambda dc, tt=t: ("r1", dc, tt)), BF16, "r1")
                V(lambda E, ss=s: E.tensor_copy(
                    r1fm[:, :, 0:1].rearrange("p a b -> p (a b)"),
                    halor[:, :, 2 * ss:2 * ss + 1]
                    .rearrange("p a b -> p (a b)")),
                  ("halor",), (("r1", "hl"),), tag="r1h")
                V(lambda E, ss=s: E.tensor_copy(
                    r1fm[:, :, BLK + 1:BLK + 2]
                    .rearrange("p a b -> p (a b)"),
                    halor[:, :, 2 * ss + 1:2 * ss + 2]
                    .rearrange("p a b -> p (a b)")),
                  ("halor",), (("r1", "hr"),), tag="r1h")
                r1_reads = tuple(("r1", dc, t) for dc in range(DC)
                                 for t in range(BT)) + \
                    (("r1", "hl"), ("r1", "hr"))

                rot = 0
                for hcg in range(16):
                    wsl = hcg % 2
                    wsc = wshs[wsl]
                    DMA("G", wsc[:].rearrange("p (g c) -> p g c", c=512),
                        w1q_s[hcg * 2:(hcg + 1) * 2]
                        .rearrange("a h p c -> p (a h) c"),
                        (("wsh", wsl),) + tuple(("w1qs", hcg * 2 + i, hf)
                                                for i in range(2)
                                                for hf in range(2)),
                        (("wsh", wsl),), tag="w1ld")
                    for hcl in range(2):
                        hc = hcg * 2 + hcl
                        banks = [(rot + i) % 6 for i in range(3)]
                        rot += 3
                        for gi in range(3):
                            c0 = gi * 342
                            cw = 342
                            for dc in range(DC):
                                T(lambda E, d=dc, h=hcl, bb=banks[gi], cc=c0,
                                  ww=cw, wc=wsc: E.matmul(
                                      pb[bb][:, 0:ww],
                                      wc[:, (h * 8 + d) * 128:
                                         (h * 8 + d + 1) * 128],
                                      r1fm[:, d, cc:cc + ww],
                                      start=(d == 0), stop=(d == DC - 1)),
                                  (("wsh", wsl),) + r1_reads,
                                  ((PSK[banks[gi]], "f1"),), accum=True,
                                  tag="f1")
                        for gi in range(3):
                            c0 = gi * 342
                            V(lambda E, h=hc, bb=banks[gi], cc=c0:
                              E.tensor_copy(hbuf[:, h, cc:cc + 342],
                                            pb[bb][:, 0:342]),
                              ((PSK[banks[gi]], "f1"),), (("hb", hc),),
                              tag="f1ev")
                        ut, utk = utmps[hc % 2]
                        V(lambda E, h=hc, u=ut: E.tensor_scalar(
                            u[:, 0:BLK], hbuf[:, h, 0:BLK], cw0[:, h:h + 1],
                            None, ALU.mult), (("hb", hc), ("cw", 0)),
                          utk, tag="cv")
                        V(lambda E, h=hc, u=ut: E.scalar_tensor_tensor(
                            u[:, 0:BLK], hbuf[:, h, 1:BLK + 1],
                            cw1[:, h:h + 1], u[:, 0:BLK], ALU.mult, ALU.add),
                          (("hb", hc), ("cw", 1)) + utk, utk, tag="cv")
                        V(lambda E, h=hc, u=ut: E.scalar_tensor_tensor(
                            u[:, 0:BLK], hbuf[:, h, 2:BLK + 2],
                            cw2[:, h:h + 1], u[:, 0:BLK], ALU.mult, ALU.add),
                          (("hb", hc), ("cw", 2)) + utk, utk, tag="cv")
                        S(lambda E, h=hc, u=ut: E.activation(
                            hbuf[:, h, 0:BLK], u[:, 0:BLK], AF.Silu,
                            bias=cbv[:, h:h + 1], scale=1.0),
                          utk + ("cbv",), (("hb", hc),), tag="silu")
                        if hc == 1:
                            V(lambda E: E.tensor_tensor(
                                amaxp[:], hbuf[:, 0, 0:BLK],
                                hbuf[:, 1, 0:BLK], ALU.max),
                              (("hb", 0), ("hb", 1)), ("amx",), tag="am")
                            V(lambda E: E.tensor_tensor(
                                aminp[:], hbuf[:, 0, 0:BLK],
                                hbuf[:, 1, 0:BLK], ALU.min),
                              (("hb", 0), ("hb", 1)), ("amn",), tag="am")
                        elif hc >= 2:
                            V(lambda E, h=hc: E.tensor_tensor(
                                amaxp[:], amaxp[:], hbuf[:, h, 0:BLK],
                                ALU.max),
                              ("amx", ("hb", hc)), ("amx",), tag="am")
                            V(lambda E, h=hc: E.tensor_tensor(
                                aminp[:], aminp[:], hbuf[:, h, 0:BLK],
                                ALU.min),
                              ("amn", ("hb", hc)), ("amn",), tag="am")

                V(lambda E: E.tensor_scalar(aminp[:], aminp[:], -1.0, None,
                                            ALU.mult), ("amn",), ("amn",),
                  tag="am")
                V(lambda E: E.tensor_tensor(amaxp[:], amaxp[:], aminp[:],
                                            ALU.max), ("amx", "amn"),
                  ("amx",), tag="am")
                for t in range(BT):
                    bb = 6 + (t % 2)
                    T(lambda E, tt=t, b2=bb: E.matmul(
                        pb[b2][:, 0:128], amaxp[:, tt * 128:(tt + 1) * 128],
                        identb[:], start=True, stop=True),
                      ("amx", "idb"), (PSK[bb],), tag="amt")
                    V(lambda E, tt=t, b2=bb: E.tensor_reduce(
                        s2t[:, tt:tt + 1], pb[b2][:, 0:128], AX, ALU.max),
                      (PSK[bb],), (("s2", t),), tag="amr")
                s2_reads = tuple(("s2", t) for t in range(BT))
                V(lambda E: E.tensor_scalar(s2t[:], s2t[:], 1e-5, None,
                                            ALU.max), s2_reads, ("s2c",))
                V(lambda E: E.reciprocal(tiny[:, 48:48 + BT], s2t[:]),
                  ("s2c",), (("fti", 48),))
                V(lambda E: E.tensor_scalar(s2t[:], tiny[:, 48:48 + BT],
                                            127.0, None, ALU.mult),
                  (("fti", 48),), ("s2v",))
                V(lambda E: E.tensor_copy(s2b16[:], s2t[:]), ("s2v",),
                  ("s2b",))
                V(lambda E: E.tensor_copy(tiny[:, 56:56 + BT], s2b16[:]),
                  ("s2b",), (("fti", 56),))
                V(lambda E: E.reciprocal(s2i[:], tiny[:, 56:56 + BT]),
                  (("fti", 56),), ("s2iv",))
                V(lambda E: E.tensor_scalar(s2i[:], s2i[:], mglob[:, 1:2],
                                            None, ALU.mult),
                  ("s2iv", "mglob"), ("s2iv",))
                for grp in range(2):
                    for k in range(4):
                        T(lambda E, tt=grp * 4 + k, kk=k: E.matmul(
                            pb[6][0:1, kk * 128:(kk + 1) * 128],
                            s2b16[:, tt:tt + 1], identb[:],
                            start=(kk == 0), stop=(kk == 3)),
                          ("s2b", "idb"), (PSK[6],), accum=True, tag="s2r")
                    V(lambda E, g=grp: E.tensor_copy(
                        tmprow[0:1, g * 512:(g + 1) * 512],
                        pb[6][0:1, 0:512]),
                      (PSK[6],), ("s2row",))
                for half in range(2):
                    T(lambda E, m=half: E.matmul(
                        pb[7][:, 0:512], onesf[:],
                        tmprow[0:1, m * 512:(m + 1) * 512], start=True,
                        stop=True),
                      ("s2row", "ones1f"), (PSK[7],), tag="s2bm")
                    V(lambda E, m=half: E.tensor_copy(
                        s2bc[:, m * 512:(m + 1) * 512], pb[7][:, 0:512]),
                      (PSK[7],), (("s2bc", half),), tag="s2be")

                if dbg2 is not None and agi == 1 and s == 0:
                    dump(s2t[:, 0:BT], ("s2v",))
                    dump(s2i[:, 0:BT], ("s2iv",))
                    i = dump_n[0]
                    dump_n[0] += 1
                    DMA("G", dbg2[i, :, 0:BLK], s2bc[:],
                        (("s2bc", 0), ("s2bc", 1), ("dbg2", i)),
                        (("dbg2", i),), tag="dumps2bc")
                    i = dump_n[0]
                    dump_n[0] += 1
                    DMA("G", dbg2[i, :, 0:BLK], amaxp[:, 0:BLK],
                        ("amx", ("dbg2", i)), (("dbg2", i),), tag="dumpam")
                for th in range(2):
                    for hcg in range(16):
                        wsl = hcg % 2
                        wsc = wshs[wsl]
                        DMA("G", wsc[:].rearrange("p (g c) -> p g c", c=D),
                            w2q_s[hcg * 256:(hcg + 1) * 256, :]
                            .rearrange("(a p) c -> p a c", p=128),
                            (("wsh", wsl),) + tuple(("w2qs", pp, k2, dd)
                                                    for pp in range(4)
                                                    for k2 in range(8)
                                                    for dd in range(DC)
                                                    if (pp * 8 + k2) // 2
                                                    == hcg),
                            (("wsh", wsl),), tag="w2ld")
                        for hcl in range(2):
                            hc = hcg * 2 + hcl
                            rsl = hc % 2
                            rsc = r2ss[rsl]
                            V(lambda E, h=hc, tt=th: E.tensor_tensor(
                                fi32[:, 0:512],
                                hbuf[:, h, tt * 512:(tt + 1) * 512],
                                s2bc[:, tt * 512:(tt + 1) * 512], ALU.mult),
                              (("hb", hc), ("s2bc", th)), ("f32q",),
                              tag="q2")
                            V(lambda E, rc=rsc: E.tensor_copy(
                                rc[:], fi32[:, 0:512]),
                              ("f32q",), (("r2s", rsl),), tag="q2c")
                            for tt in range(4):
                                for dmh in range(2):
                                    b = tt * 2 + dmh
                                    T(lambda E, h=hcl, t4=tt, m=dmh, bb=b,
                                      fr=(hc == 0), la=(hc == HC - 1),
                                      rc=rsc, wc=wsc:
                                      E.matmul(
                                          pb[bb][:],
                                          rc[:, t4 * 128:(t4 + 1) * 128],
                                          wc[:, h * 1024 + m * 512:
                                             h * 1024 + (m + 1) * 512],
                                          start=fr, stop=la),
                                      (("r2s", rsl), ("wsh", wsl)), (PSK[b],),
                                      accum=True, tag="f2")
                    for tt in range(4):
                        rt = s * BT + th * 4 + tt
                        for dmh in range(2):
                            b = tt * 2 + dmh
                            V(lambda E, bb=b, r=rt, m=dmh,
                              sc=th * 4 + tt: E.scalar_tensor_tensor(
                                  resid[:, r, m * 512:(m + 1) * 512],
                                  pb[bb][:], s2i[:, sc:sc + 1],
                                  resid[:, r, m * 512:(m + 1) * 512],
                                  ALU.mult, ALU.add),
                              (PSK[b], "s2iv", ("resid", rt)),
                              (("resid", rt),), tag="f2ev")
                        if agi == 1:
                            DMA(("Y", "G", "S")[rt % 3],
                                out_y[rt * 128:(rt + 1) * 128, :],
                                resid[:, rt, :],
                                (("resid", rt),), (("out", rt),), tag="out")
        phf.close()
        sch.barrier()

    ffn_sublayer(0)
    if debug >= 2:
        for t in range(2 * BT):
            DMA("Y", dbg[1, t * 128:(t + 1) * 128, :], resid[:, t, :],
                (("resid", t),), (("dbg", 1, t),), tag="dbg")
        sch.barrier()

    ph2 = ExitStack()
    _r = lambda n, s, d=F32: ph2.enter_context(nc.sbuf_tensor(n, s, d))
    phikT = _r("phikT", [128, BT, D], BF16)
    vtm = _r("vtm", [128, BT, D], BF16)
    phiqF = _r("phiqF", [128, DC, BLK], BF16)
    pkF = _r("pkF", [128, DC, 128], BF16)
    xnfm = _r("xnfm", [128, DC, 512], BF16)
    xnt = _r("xnt", [128, D], BF16)
    amk = _r("amk", [128, 2048], BF16)
    otm = _r("otm", [128, 4, D], BF16)
    ofm = _r("ofm", [128, DC, 128], BF16)
    kvst = _r("kvst", [128, 8 * 65], BF16)
    kvpref = _r("kvpref", [128, 2, 8 * 65], F32)
    stbuf = _r("stbuf", [128, 8 * 65], F32)
    wstr = _r("wstr", [128, D], BF16)
    wob = _r("wob", [128, DC, D], BF16)
    bqfm = _r("bqfm", [128, DC], F32)
    bkbc = _r("bkbc", [128, D], F32)
    bvbc = _r("bvbc", [128, D], F32)
    bobc = _r("bobc", [128, D], F32)
    putm = _r("putm", [128, 512], F32)
    phm = _r("phm", [128, 512], BF16)
    phe = _r("phe", [128, 512], BF16)
    r16 = _r("r16", [128, 16], F32)
    if True:

        DMA("Y", bqfm[:], bq_.rearrange("(c p) -> p c", p=128), (), ("bqf",))
        DMA("Y", bkbc[:], bk_[None, :].partition_broadcast(128), (), ("bkb",))
        DMA("Y", bvbc[:], bvb_[None, :].partition_broadcast(128), (), ("bvb",))
        DMA("Y", bobc[:], bo_[None, :].partition_broadcast(128), (), ("bob",))
        for hdc in range(DC):
            DMA("G", wob[:, hdc, :], wo_[hdc * 128:(hdc + 1) * 128, :],
                (("wob", hdc),), (("wob", hdc),), tag="wold")
        ln_stats(lambda j: resid[:, j, :], lambda j: ("resid", j), NB, "bst")


    def state_bank_open(bb, tag):
        T(lambda E, b2=bb: E.matmul(pb[b2][:, 0:260], zrow[0:1, 0:128],
                                    zrow[0:1, 0:260], start=True, stop=False),
          ("zrow",), (PSK[bb],), accum=True, tag=tag + "o")

    def state_bank_close(bb, tag):
        T(lambda E, b2=bb: E.matmul(pb[b2][:, 0:260], zrow[0:1, 0:128],
                                    zrow[0:1, 0:260], start=False, stop=True,
                                    skip_group_check=True),
          ("zrow",), (PSK[bb],), accum=True, tag=tag + "c")

        def xn_chunk(s, ach):
            for k in range(4):
                t = ach * 4 + k
                rt = s * BT + t
                V(lambda E, r=rt: E.tensor_scalar(
                    xnt[:], resid[:, r, :], st_r[:, r:r + 1],
                    stb2[:, r:r + 1], ALU.mult, ALU.add),
                  (("resid", rt), ("st_r", 0), ("stb2", 0)), ("xnt",), tag="bxn")
                tp8(lambda d: xnt[:, d * 128:(d + 1) * 128], ("xnt",),
                    (lambda d, kk=k: xnfm[:, d, kk * 128:(kk + 1) * 128]),
                    (lambda dc, kk=k: ("xnf", dc, kk)), BF16, "bx")

        xn_reads = tuple(("xnf", dc, k) for dc in range(DC) for k in range(4))

        def phi_evac(bb, bkey, dst_ap, dkey, bias_ap, bias_key):
            V(lambda E, a=bias_ap: E.tensor_tensor(putm[:], pb[bb][:], a,
                                                   ALU.add),
              (bkey, bias_key), ("putm",), tag="phb")
            V(lambda E: E.tensor_scalar(phm[:], putm[:], 0.0, None,
                                        ALU.min), ("putm",), ("trm",),
              tag="phm")
            S(lambda E: E.activation(phe[:], phm[:],
                                     AF.Exp, bias=0.0, scale=1.0),
              ("trm",), ("tre",), tag="phe")
            V(lambda E, d=dst_ap: E.scalar_tensor_tensor(
                d, putm[:], 0.0, phe[:], ALU.max, ALU.add),
              ("putm", "tre"), (dkey,), tag="phx")

        for s in range(2):
            for ach in range(2):
                xn_chunk(s, ach)
                for wi, (wap, dstbuf, isphi, bias_ap, bias_key, wkey) in \
                        enumerate(((wk_, phikT, True, bkbc, "bkb", "wK"),
                                   (wv_, vtm, False, bvbc, "bvb", "wV"))):
                    for dmc in range(DC):
                        DMA("G", wstr[:], wap[dmc * 128:(dmc + 1) * 128, :],
                            ((wkey,),) if False else ("wstr",), ("wstr",),
                            tag="bwld")
                        for k in range(4):
                            for hdh in range(2):
                                b = k * 2 + hdh
                                T(lambda E, d=dmc, k2=k, m=hdh, bb=b:
                                  E.matmul(
                                      pb[bb][:],
                                      xnfm[:, d, k2 * 128:(k2 + 1) * 128],
                                      wstr[:, m * 512:(m + 1) * 512],
                                      start=(d == 0), stop=(d == DC - 1)),
                                  (("xnf", dmc, k), "wstr"),
                                  (PSK[b],), accum=True,
                                  tag="bpj")
                    for k in range(4):
                        t = ach * 4 + k
                        for hdh in range(2):
                            b = k * 2 + hdh
                            if isphi:
                                phi_evac(
                                    b, PSK[b],
                                    dstbuf[:, t, hdh * 512:(hdh + 1) * 512],
                                    ("pk", t, hdh),
                                    bias_ap[:, hdh * 512:(hdh + 1) * 512],
                                    bias_key)
                            else:
                                V(lambda E, bb=b, t2=t, m=hdh:
                                  E.tensor_tensor(
                                      vtm[:, t2, m * 512:(m + 1) * 512],
                                      pb[bb][:],
                                      bvbc[:, m * 512:(m + 1) * 512],
                                      ALU.add),
                                  (PSK[b], "bvb"),
                                  (("vt", t, hdh),), tag="bpv")
            state_bank_open(6, "s1")
            state_bank_open(7, "s1")
            for n in range(BT):
                for h in range(NH):
                    bb = 6 + h // 8
                    p0 = (h % 2) * 64
                    c0 = ((h % 8) // 2) * 65
                    hk = (h * 64) // 512
                    T(lambda E, hh=h, nn=n, b2=bb, pp=p0, cc=c0: E.matmul(
                        pb[b2][pp:pp + 64, cc:cc + 64],
                        phikT[:, nn, hh * 64:(hh + 1) * 64],
                        vtm[:, nn, hh * 64:(hh + 1) * 64],
                        start=False, stop=False, skip_group_check=True),
                      (("pk", n, hk), ("vt", n, hk)),
                      (PSK[bb],), accum=True, tag="s1")
                    T(lambda E, hh=h, nn=n, b2=bb, pp=p0, cc=c0: E.matmul(
                        pb[b2][pp:pp + 64, cc + 64:cc + 65],
                        phikT[:, nn, hh * 64:(hh + 1) * 64], onesb[:],
                        start=False, stop=False, skip_group_check=True),
                      (("pk", n, hk), "onesb"),
                      (PSK[bb],), accum=True, tag="s1k")
            for half in range(2):
                bb = 6 + half
                V(lambda E, b2=bb, hh=half: E.tensor_copy(
                    stbuf[:, hh * 260:(hh + 1) * 260], pb[b2][:, 0:260]),
                  (PSK[bb],), (("stb", s, half),), tag="s1ev")
            state_bank_close(6, "s1")
            state_bank_close(7, "s1")
            DMA("Y", ag_si[s], stbuf[:], (("stb", s, 0), ("stb", s, 1)),
                (("agsi", s),), tag="stpub")
            for t in range(BT):
                DMA("Y", spill[s, 0, t], phikT[:, t, :],
                    (("pk", t, 0), ("pk", t, 1)), (("spl", s, 0, t),),
                    tag="spl")
                DMA("Y", spill[s, 1, t], vtm[:, t, :],
                    (("vt", t, 0), ("vt", t, 1)), (("spl", s, 1, t),),
                    tag="spl")

        sch.emit("G", lambda E: E.collective_compute(
            "AllGather", ALU.bypass, replica_groups=[list(range(NCORES))],
            ins=[ag_si[:]], outs=[ag_so[:]]),
            (("agsi", 0), ("agsi", 1)), ("agso",), is_dma=True, is_coll=True,
            tag="AGs")

        for s in range(2):
            V(lambda E, ss=s: E.memset(kvpref[:, ss, :], 0.0), (),
              (("kvp", s),))
        for b in range(NB):
            DMA("Y", stbuf[:], ag_so[b], ("agso", ("stb", "p")),
                (("stb", "p"),), tag="pfl")
            for s in range(2):
                V(lambda E, bb=b, ss=s: E.scalar_tensor_tensor(
                    kvpref[:, ss, :], stbuf[:],
                    stwb[:, 2 * bb + ss:2 * bb + ss + 1], kvpref[:, ss, :],
                    ALU.mult, ALU.add),
                  (("stb", "p"), "stwb", ("kvp", s)), (("kvp", s),),
                  tag="pfw")

        for s in range(2):
            for t in range(BT):
                DMA("Y", phikT[:, t, :], spill[s, 0, t],
                    (("spl", s, 0, t), ("pk", t, 0), ("pk", t, 1)),
                    (("pk", t, 0), ("pk", t, 1)), tag="rld")
                DMA("Y", vtm[:, t, :], spill[s, 1, t],
                    (("spl", s, 1, t), ("vt", t, 0), ("vt", t, 1)),
                    (("vt", t, 0), ("vt", t, 1)), tag="rld")
            for ach in range(2):
                xn_chunk(s, ach)
                for dmc in range(DC):
                    DMA("G", wstr[:], wq_[dmc * 128:(dmc + 1) * 128, :],
                        ("wstr",), ("wstr",), tag="bwq")
                    for hdc in range(DC):
                        T(lambda E, d=dmc, hh=hdc: E.matmul(
                            pb[hh][:], wstr[:, hh * 128:(hh + 1) * 128],
                            xnfm[:, d, :], start=(d == 0),
                            stop=(d == DC - 1)),
                          (("wstr"), ) + xn_reads, (PSK[hdc],),
                          accum=True, tag="bqj")
                for hdc in range(DC):
                    V(lambda E, hh=hdc: E.tensor_scalar(
                        putm[:], pb[hh][:], bqfm[:, hh:hh + 1], None,
                        ALU.add),
                      (PSK[hdc], "bqf"), ("putm",), tag="bqb")
                    V(lambda E: E.tensor_scalar(phm[:], putm[:],
                                                0.0, None, ALU.min),
                      ("putm",), ("trm",), tag="bqm")
                    S(lambda E: E.activation(phe[:],
                                             phm[:], AF.Exp,
                                             bias=0.0, scale=1.0),
                      ("trm",), ("tre",), tag="bqe")
                    V(lambda E, hh=hdc, aa=ach: E.scalar_tensor_tensor(
                        phiqF[:, hh, aa * 512:(aa + 1) * 512], putm[:], 0.0,
                        phe[:], ALU.max, ALU.add),
                      ("putm", "tre"), (("pq", hdc, ach),), tag="bqx")
            V(lambda E, ss=s: E.tensor_copy(kvst[:], kvpref[:, ss, :]),
              (("kvp", s),), ("kvst",))
            state_bank_open(6, "s2")
            state_bank_open(7, "s2")
            for n in range(BT):
                tp8(lambda d: phikT[:, n, d * 128:(d + 1) * 128],
                    (("pk", n, 0), ("pk", n, 1)),
                    (lambda d: pkF[:, d, :]),
                    (lambda dc: ("pkf", dc)), BF16, "pk2")
                for h in range(NH):
                    bb = h // 4
                    cc = (h % 4) * 128
                    p0 = (h % 2) * 64
                    dq = h // 2
                    T(lambda E, b2=bb, c2=cc, pp=p0, d=dq, nn=n: E.matmul(
                        pb[b2][:, c2:c2 + 128], pkF[pp:pp + 64, d, :],
                        phiqF[pp:pp + 64, d, nn * 128:(nn + 1) * 128],
                        start=True, stop=True),
                      (("pkf", dq), ("pq", dq, n // 4)),
                      (PSK[bb],), accum=True, tag="A")
                for bb in range(4):
                    V(lambda E, b2=bb: E.tensor_tensor(
                        amk[:, b2 * 512:(b2 + 1) * 512], pb[b2][:],
                        maskW[:], ALU.mult),
                      (PSK[bb], ("mW", 0), ("mW", 1), ("mW", 2),
                       ("mW", 3)),
                      (("amk", bb),), tag="Am")
                NB_OF = [(0, 0, 7), (1, 7, 7), (2, 14, 2)]
                for bb, h0, cnt in NB_OF:
                    for i in range(cnt):
                        h = h0 + i
                        cc = i * 65
                        p0 = (h % 2) * 64
                        dq = h // 2
                        hk = (h * 64) // 512
                        T(lambda E, b2=bb, c2=cc, hh=h, nn=n, i2z=i: E.matmul(
                            pb[b2][:, c2:c2 + 64],
                            amk[:, ((hh // 4) * 512 + (hh % 4) * 128):
                                ((hh // 4) * 512 + (hh % 4) * 128 + 128)],
                            vtm[:, nn, hh * 64:(hh + 1) * 64],
                            start=(i2z == 0), stop=False),
                          (("amk", h // 4), ("vt", n, hk)),
                          (PSK[bb],), accum=True, tag="nmv")
                        T(lambda E, b2=bb, c2=cc, hh=h, nn=n: E.matmul(
                            pb[b2][:, c2 + 64:c2 + 65],
                            amk[:, ((hh // 4) * 512 + (hh % 4) * 128):
                                ((hh // 4) * 512 + (hh % 4) * 128 + 128)],
                            onesb[:], start=False, stop=False),
                          (("amk", h // 4), "onesb"),
                          (PSK[bb],), accum=True, tag="nmd")
                        T(lambda E, b2=bb, c2=cc, hh=h, pp=p0, d=dq, nn=n:
                          E.matmul(
                              pb[b2][:, c2:c2 + 65],
                              phiqF[pp:pp + 64, d,
                                    nn * 128:(nn + 1) * 128],
                              kvst[pp:pp + 64, (hh // 2) * 65:
                                   (hh // 2) * 65 + 65],
                              start=False, stop=(hh == h0 + cnt - 1)),
                          (("pq", dq, n // 4), "kvst"),
                          (PSK[bb],), accum=True, tag="nmp")
                for bb, h0, cnt in NB_OF:
                    V(lambda E, b2=bb, hh=h0, c2=cnt: E.tensor_scalar(
                        r16[:, hh:hh + c2],
                        pb[b2][:, 0:c2 * 65].rearrange(
                            "p (a b) -> p a b", b=65)[:, :, 64], EPS_LA,
                        None, ALU.add),
                      (PSK[bb],), (("r16", bb),), tag="den")
                V(lambda E: E.reciprocal(r16[:], r16[:]),
                  (("r16", 0), ("r16", 1), ("r16", 2)), ("r16r",), tag="rcp")
                for bb, h0, cnt in NB_OF:
                    for i in range(cnt):
                        h = h0 + i
                        V(lambda E, b2=bb, i2=i, hh=h, nn=n: E.tensor_scalar(
                            otm[:, nn % 4, hh * 64:(hh + 1) * 64],
                            pb[b2][:, i2 * 65:i2 * 65 + 64],
                            r16[:, hh:hh + 1], None, ALU.mult),
                          (PSK[bb], "r16r"),
                          (("otm", n % 4, h),), tag="oev")
                for h in range(NH):
                    bb = 6 + h // 8
                    p0 = (h % 2) * 64
                    c0 = ((h % 8) // 2) * 65
                    hk = (h * 64) // 512
                    T(lambda E, hh=h, nn=n, b2=bb, pp=p0, cc=c0: E.matmul(
                        pb[b2][pp:pp + 64, cc:cc + 64],
                        phikT[:, nn, hh * 64:(hh + 1) * 64],
                        vtm[:, nn, hh * 64:(hh + 1) * 64],
                        start=False, stop=False, skip_group_check=True),
                      (("pk", n, hk), ("vt", n, hk)),
                      (PSK[bb],), accum=True, tag="s2")
                    T(lambda E, hh=h, nn=n, b2=bb, pp=p0, cc=c0: E.matmul(
                        pb[b2][pp:pp + 64, cc + 64:cc + 65],
                        phikT[:, nn, hh * 64:(hh + 1) * 64], onesb[:],
                        start=False, stop=False, skip_group_check=True),
                      (("pk", n, hk), "onesb"),
                      (PSK[bb],), accum=True, tag="s2k")
                for half in range(2):
                    bb = 6 + half
                    V(lambda E, b2=bb, hh=half, ss=s: E.tensor_tensor(
                        kvst[:, hh * 260:(hh + 1) * 260], pb[b2][:, 0:260],
                        kvpref[:, ss, hh * 260:(hh + 1) * 260], ALU.add),
                      (PSK[bb], ("kvp", s)), ("kvst",), tag="kvu")
                if n == BT - 1:
                    state_bank_close(6, "s2")
                    state_bank_close(7, "s2")
                if n % 4 == 3:
                    ach = n // 4
                    for k in range(4):
                        tp8(lambda d, kk=k: otm[:, kk, d * 128:(d + 1) * 128],
                            tuple(("otm", k, h) for h in range(NH)),
                            (lambda d: ofm[:, d, :]),
                            (lambda dc: ("ofm", dc)), BF16, "otp")
                        for dmh in range(2):
                            b = 4 + dmh
                            for hdc in range(DC):
                                T(lambda E, d=hdc, m=dmh, bb=b: E.matmul(
                                    pb[bb][:], ofm[:, d, :],
                                    wob[:, d, m * 512:(m + 1) * 512],
                                    start=(d == 0), stop=(d == DC - 1)),
                                  (("ofm", hdc), ("wob", hdc)),
                                  (PSK[b],), accum=True, tag="woj")
                            rt = s * BT + ach * 4 + k
                            V(lambda E, bb=b, m=dmh: E.tensor_tensor(
                                putm[:], pb[bb][:],
                                bobc[:, m * 512:(m + 1) * 512], ALU.add),
                              (PSK[b], "bob"), ("putm",), tag="wob")
                            V(lambda E, r=rt, m=dmh: E.tensor_tensor(
                                resid[:, r, m * 512:(m + 1) * 512],
                                resid[:, r, m * 512:(m + 1) * 512],
                                putm[:], ALU.add),
                              ("putm", ("resid", rt)), (("resid", rt),),
                              tag="wor")
    ph2.close()
    sch.barrier()
    if debug >= 3:
        for t in range(2 * BT):
            DMA("Y", dbg[2, t * 128:(t + 1) * 128, :], resid[:, t, :],
                (("resid", t),), (("dbg", 2, t),), tag="dbg")
        sch.barrier()

    ffn_sublayer(1)

    sch.emit("Y", lambda E: E.nop(),
             tuple(("out", t) for t in range(2 * BT)), (), tag="fin")

    with es, nc.allow_non_contiguous_dma(
            reason="tiny one-time vector loads (bias/conv rearranges)"), \
            nc.Block() as block:
        sch.replay(block, sems)
    return nc


def _stage_inputs(inputs):
    x = np.ascontiguousarray(
        np.asarray(inputs["L0_attn_a_x"], dtype=np.float32).reshape(T_TOT, D))
    xt = x.reshape(NB, BT, 128, D)

    common = dict(
        w1=np.ascontiguousarray(np.asarray(inputs["L0_ffn_w1"], np.float32)),
        w2=np.ascontiguousarray(np.asarray(inputs["L0_ffn_w2"], np.float32)),
        conv_w=np.ascontiguousarray(
            np.asarray(inputs["L0_ffn_conv_w"], np.float32).reshape(HID, 3)),
        conv_b=np.ascontiguousarray(
            np.asarray(inputs["L0_ffn_conv_b"], np.float32)),
        wq=np.ascontiguousarray(np.asarray(inputs["L0_attn_b_Wq"], np.float32)),
        wk=np.ascontiguousarray(np.asarray(inputs["L0_attn_b_Wk"], np.float32)),
        wv=np.ascontiguousarray(np.asarray(inputs["L0_attn_b_Wv"], np.float32)),
        wo=np.ascontiguousarray(np.asarray(inputs["L0_attn_b_Wo"], np.float32)),
        bq=np.ascontiguousarray(np.asarray(inputs["L0_attn_b_bq"], np.float32)),
        bk=np.ascontiguousarray(np.asarray(inputs["L0_attn_b_bk"], np.float32)),
        bvb=np.ascontiguousarray(np.asarray(inputs["L0_attn_b_bv"], np.float32)),
        bo=np.ascontiguousarray(np.asarray(inputs["L0_attn_b_bo"], np.float32)),
        bv_q=np.ascontiguousarray(np.asarray(inputs["L0_attn_a_bv_q"], np.float32)),
        bv_k=np.ascontiguousarray(np.asarray(inputs["L0_attn_a_bv_k"], np.float32)),
        bv_v=np.ascontiguousarray(np.asarray(inputs["L0_attn_a_bv_v"], np.float32)),
    )

    in_maps = []
    for c in range(NCORES):
        bl, bh = c, 15 - c
        visits = [(0, kb) for kb in range(bl)] + \
                 [(1, kb) for kb in range(bh)] + [(0, bl), (1, bh)]
        assert len(visits) == NVIS
        kv = np.stack([xt[kb] for (_, kb) in visits])
        ws = np.zeros(NVIS * 2, np.float32)
        for v, (slot, _) in enumerate(visits):
            ws[2 * v + slot] = 1.0
        mb = np.stack([xt[bl], xt[bh]])
        hs = np.zeros((4 * NCORES, 4), np.float32)
        halo_toks = [bl * BLK - 1, (bl + 1) * BLK, bh * BLK - 1,
                     (bh + 1) * BLK]
        for j, tk in enumerate(halo_toks):
            if tk < 0 or tk >= T_TOT:
                continue
            b = tk // BLK
            r = b if b <= 7 else 15 - b
            so = 0 if b == r else 1
            first = (tk % BLK == 0)
            row = 4 * r + 2 * so + (0 if first else 1)
            hs[row, j] = 1.0
        sw = np.zeros(NB * 2, np.float32)
        for bidx in range(NB):
            blkid = (bidx // 2) if bidx % 2 == 0 else 15 - (bidx // 2)
            if blkid < bl:
                sw[2 * bidx + 0] = 1.0
            if blkid < bh:
                sw[2 * bidx + 1] = 1.0
        in_maps.append(dict(common, kvis=np.ascontiguousarray(kv),
                            myblk=np.ascontiguousarray(mb), wsel=ws,
                            halosel=hs, stw=sw))
    return in_maps


_NC_CACHE = {}


def kernel(**inputs):
    debug = int(inputs.pop("_debug", 0))
    if debug not in _NC_CACHE:
        _NC_CACHE[debug] = build_program(debug)
    nc = _NC_CACHE[debug]
    in_maps = _stage_inputs(inputs)
    res = run_bass_kernel_spmd(nc, in_maps, core_ids=list(range(NCORES)))
    out = np.zeros((T_TOT, D), np.float32)
    for c in range(NCORES):
        y = np.asarray(res.results[c]["y"]).reshape(2 * BLK, D)
        out[c * BLK:(c + 1) * BLK] = y[0:BLK]
        out[(15 - c) * BLK:(16 - c) * BLK] = y[BLK:2 * BLK]
    if debug:
        nd = min(debug, 3)
        dbgs = [np.asarray(res.results[c]["dbg"]).reshape(nd, 2 * BLK, D)
                for c in range(NCORES)]
        full = np.zeros((nd, T_TOT, D), np.float32)
        for c in range(NCORES):
            full[:, c * BLK:(c + 1) * BLK] = dbgs[c][:, 0:BLK]
            full[:, (15 - c) * BLK:(16 - c) * BLK] = dbgs[c][:, BLK:2 * BLK]
        kernel._dbg = full
        if debug >= 4:
            kernel._dbg2 = [np.asarray(res.results[c]["dbg2"])
                            .reshape(16, 128, 8192) for c in range(NCORES)]
    return out.reshape(1, T_TOT, D)
